# version 01
# baseline (speedup 1.0000x reference)
import numpy as np
import ml_dtypes
NO_PSL = False
PH4 = 9
DETECT_RACES = True
from contextlib import ExitStack

import concourse.bass as bass
import concourse.mybir as mybir
from concourse.bass_utils import run_bass_kernel_spmd

F32 = mybir.dt.float32
BF16 = mybir.dt.bfloat16
AF = mybir.ActivationFunctionType
ALU = mybir.AluOpType
AX = mybir.AxisListType.X

NCORES = 8
T = 1024
HALO = 4
TH = T + HALO
H = 4096
KH = H // 128

BF = ml_dtypes.bfloat16


def tile_w_omajor(w):
    IN, OUT = w.shape
    t = w.reshape(IN // 128, 128, OUT // 128, 128).transpose(2, 0, 1, 3)
    return np.ascontiguousarray(t.reshape(OUT // 128, IN, 128))


def col_layout(v):
    return np.ascontiguousarray(np.asarray(v, np.float32).reshape(-1, 128).T)


class Sem:
    __slots__ = ("h", "n")

    def __init__(self, h):
        self.h = h
        self.n = 0


class KB:
    ENGS = ("tensor", "vector", "scalar", "gpsimd", "sync")

    def __init__(self, nc):
        self.nc = nc
        self.es = ExitStack()
        self.ops = {k: [] for k in self.ENGS}
        self._names = {}
        self._bars = {}
        self._open = {}
        self._seq = 0

    def name(self, p):
        i = self._names.get(p, 0)
        self._names[p] = i + 1
        return f"{p}_{i}"

    def sem(self, p="s"):
        return Sem(self.es.enter_context(self.nc.semaphore(self.name(p))))

    def sb(self, shape, dtype, p="t"):
        return self.es.enter_context(self.nc.sbuf_tensor(self.name(p), shape, dtype))

    def sbt(self, shape, dtype, p="t"):
        g = self.nc.sbuf_tensor(self.name(p), shape, dtype, side="right")
        h = g.__enter__()
        self._seq += 1
        self._open[id(h)] = (self._seq, g)
        return h

    def free(self, *hs):
        items = sorted((self._open.pop(id(h)) for h in hs), reverse=True,
                       key=lambda x: x[0])
        for _, g in items:
            g.__exit__(None, None, None)

    def psum(self, shape, dtype=F32, p="ps"):
        return self.es.enter_context(self.nc.psum_tensor(self.name(p), shape, dtype))

    def _push(self, eng, fn):
        self.ops[eng].append(fn)

    def wait(self, eng, sem, val=None):
        v = sem.n if val is None else val
        self._push(eng, lambda e, h=sem.h, v=v: e.wait_ge(h, v))

    def o(self, eng, meth, *a, sem=None, inc=1, bar=False, **kw):
        bsem = bval = None
        if bar and sem is None:
            if eng not in self._bars:
                self._bars[eng] = self.sem(f"bar_{eng}")
            sem, inc = self._bars[eng], 1
        if sem is not None:
            sem.n += inc
            if bar:
                bsem, bval = sem, sem.n
        def f(e, meth=meth, a=a, kw=kw, sem=sem, inc=inc, bsem=bsem, bval=bval):
            r = getattr(e, meth)(*a, **kw)
            if sem is not None:
                r.then_inc(sem.h, inc)
            if bsem is not None:
                e.wait_ge(bsem.h, bval)
        self._push(eng, f)

    def dma(self, eng, out, in_, sem=None, **kw):
        if sem is not None:
            sem.n += 16
        def f(e, out=out, in_=in_, kw=kw, sem=sem):
            r = e.dma_start(out=out, in_=in_, **kw)
            if sem is not None:
                r.then_inc(sem.h, 16)
        self._push(eng, f)

    def cc(self, kind, op, groups, ins, outs, sem):
        sem.n += 1
        def f(e, kind=kind, op=op, groups=groups, ins=ins, outs=outs, sem=sem):
            e.collective_compute(kind, op, replica_groups=groups,
                                 ins=ins, outs=outs).then_inc(sem.h, 1)
        self._push("gpsimd", f)

    def finalize(self):
        ops = self.ops
        with self.nc.Block() as blk:
            @blk.tensor
            def _(e):
                for f in ops["tensor"]:
                    f(e)

            @blk.vector
            def _(e):
                for f in ops["vector"]:
                    f(e)

            @blk.scalar
            def _(e):
                for f in ops["scalar"]:
                    f(e)

            @blk.gpsimd
            def _(e):
                for f in ops["gpsimd"]:
                    f(e)

            @blk.sync
            def _(e):
                for f in ops["sync"]:
                    f(e)
        for _, g in sorted(self._open.values(), reverse=True,
                           key=lambda x: x[0]):
            g.__exit__(None, None, None)
        self._open.clear()
        self.es.close()
        return self.nc


class Common:

    def __init__(self, kb, groups=(1, 1, 1, 1, 1, 1, 1, 1)):
        self.kb = kb
        self.sW = kb.sem("sW")
        self.sX = kb.sem("sX")
        self.sT = kb.sem("sT")
        self.sV = kb.sem("sV")
        self.sA = kb.sem("sA")
        self.sS = kb.sem("sS")
        self.sCC = kb.sem("sCC")
        self.sG = kb.sem("sG")
        self.sM = kb.sem("sM")
        assert sum(groups) == 8
        self.tg = [kb.psum([128, 512 * gsz], F32, f"ps{i}")
                   for i, gsz in enumerate(groups)]
        self.pb = []
        for t, gsz in zip(self.tg, groups):
            for i in range(gsz):
                self.pb.append(t[:, i * 512:(i + 1) * 512])
        self.ones_col_bf = kb.sb([128, 1], BF16, "ones_col_bf")
        self.one_row = kb.sb([1, 128], F32, "one_row")
        self.cst = kb.sb([128, 4], F32, "cst")
        kb.o("gpsimd", "memset", self.ones_col_bf[:], 1.0)
        kb.o("gpsimd", "memset", self.one_row[:], 1.0)
        kb.o("gpsimd", "memset", self.cst[:, 0:1], 1.0)
        kb.o("gpsimd", "memset", self.cst[:, 1:2], 1e-5)
        kb.o("gpsimd", "memset", self.cst[:, 2:3], -1.0)
        kb.o("gpsimd", "memset", self.cst[:, 3:4], 0.0, sem=self.sM)

    def pub(self, eng, sem, n=1):
        meth = "mul" if eng == "scalar" else "tensor_scalar_mul"
        for i in range(n):
            self.kb.o(eng, meth, self.cst[:, 3:4], self.cst[:, 3:4], 1.0,
                      sem=(sem if i == n - 1 else None), bar=True)

    def c1(self):
        return self.cst[:, 0:1]

    def ceps(self):
        return self.cst[:, 1:2]


def emit_ln(kb, cm, xb, n_tok, mb_sb, rb_sb, banks=(0, 1, 2, 3),
            keep_rows=False):
    nk = len(xb)
    slabs = [(s, min(512, n_tok - s)) for s in range(0, n_tok, 512)]
    mrow = kb.sb([1, n_tok], F32, "ln_mrow") if keep_rows else         kb.sbt([1, n_tok], F32, "ln_mrow")
    vrow = kb.sb([1, n_tok], F32, "ln_vrow") if keep_rows else         kb.sbt([1, n_tok], F32, "ln_vrow")
    sq = [kb.sbt([128, 512], BF16, "ln_sq") for _ in range(2)]
    m2 = kb.sbt([1, 512], F32, "ln_m2")
    rrow = kb.sbt([1, n_tok], F32, "ln_rrow")
    ps_a = cm.pb[banks[0]]
    ps_b = cm.pb[banks[1]]
    ps_bc = cm.pb[banks[2]]
    ps_bc2 = cm.pb[banks[3]]
    tvals = {}
    prev_stats_v = 0
    for si, (s0, sl) in enumerate(slabs):
        for k in range(nk):
            bq = k % 2
            pk = (si, k - 2) if k >= 2 else ((si - 1, nk - 2 + k) if si > 0 else None)
            if pk is not None:
                kb.wait("scalar", cm.sT, tvals[pk])
            kb.o("scalar", "activation", sq[bq][:, 0:sl], xb[k][:, s0:s0 + sl],
                 AF.Square, sem=cm.sA)
            kb.wait("tensor", cm.sA)
            if k == 0 and si > 0:
                kb.wait("tensor", cm.sV, prev_stats_v)
            kb.o("tensor", "matmul", ps_a[0:1, 0:sl], cm.ones_col_bf[:],
                 xb[k][:, s0:s0 + sl], start=(k == 0), stop=(k == nk - 1))
            kb.o("tensor", "matmul", ps_b[0:1, 0:sl], cm.ones_col_bf[:],
                 sq[bq][:, 0:sl], start=(k == 0), stop=(k == nk - 1), sem=cm.sT)
            tvals[(si, k)] = cm.sT.n
        kb.wait("vector", cm.sT)
        kb.o("vector", "tensor_scalar", mrow[0:1, s0:s0 + sl], ps_a[0:1, 0:sl],
             1.0 / H, None, ALU.mult, sem=cm.sV)
        kb.wait("scalar", cm.sV)
        kb.o("scalar", "activation", m2[0:1, 0:sl], mrow[0:1, s0:s0 + sl],
             AF.Square, sem=cm.sA)
        kb.wait("vector", cm.sA)
        kb.o("vector", "scalar_tensor_tensor", vrow[0:1, s0:s0 + sl],
             ps_b[0:1, 0:sl], 1.0 / H, m2[0:1, 0:sl], ALU.mult, ALU.subtract,
             sem=cm.sV)
        prev_stats_v = cm.sV.n
        kb.wait("scalar", cm.sV)
        kb.o("scalar", "activation", rrow[0:1, s0:s0 + sl],
             vrow[0:1, s0:s0 + sl], AF.Sqrt, bias=cm.ceps()[0:1, :], sem=cm.sA)
        kb.wait("vector", cm.sA)
        kb.o("vector", "reciprocal", rrow[0:1, s0:s0 + sl],
             rrow[0:1, s0:s0 + sl], sem=cm.sV)
        kb.wait("tensor", cm.sV)
        kb.o("tensor", "matmul", ps_bc[:, 0:sl], cm.one_row[:],
             mrow[0:1, s0:s0 + sl], start=True, stop=True)
        kb.o("tensor", "matmul", ps_bc2[:, 0:sl], cm.one_row[:],
             rrow[0:1, s0:s0 + sl], start=True, stop=True, sem=cm.sT)
        kb.wait("scalar", cm.sT)
        kb.o("scalar", "copy", mb_sb[:, s0:s0 + sl], ps_bc[:, 0:sl])
        kb.o("scalar", "copy", rb_sb[:, s0:s0 + sl], ps_bc2[:, 0:sl], sem=cm.sA)
        kb.wait("vector", cm.sA)
        for k in range(nk):
            kb.o("vector", "tensor_sub", xb[k][:, s0:s0 + sl],
                 xb[k][:, s0:s0 + sl], mb_sb[:, s0:s0 + sl])
            kb.o("vector", "tensor_mul", xb[k][:, s0:s0 + sl],
                 xb[k][:, s0:s0 + sl], rb_sb[:, s0:s0 + sl],
                 sem=(cm.sV if k == nk - 1 else None))
        if si == len(slabs) - 1:
            pass
    if keep_rows:
        kb.free(*sq, m2, rrow)
    else:
        kb.free(*sq, m2, rrow, mrow, vrow)
    return mrow, vrow


def emit_dense(kb, cm, sEv, wdram, n_k, n_o, rhs_fn, n_tok, evac_fn,
               lora=None, o_order=None, dbanks=(0, 1), nbuf=2, wb=None):
    slabs = [(s, min(512, n_tok - s)) for s in range(0, n_tok, 512)]
    ns = len(slabs)
    if wb is None:
        wb = [kb.sb([128, n_k * 128], BF16, "dw") for _ in range(nbuf)]
    pss = [cm.pb[dbanks[0]], cm.pb[dbanks[1]]]
    o_order = list(range(n_o)) if o_order is None else o_order
    base = sEv.n
    j = 0
    for i, o in enumerate(o_list := o_order):
        b = i % nbuf
        if i == 0:
            kb.wait("gpsimd", cm.sW)
        if i >= nbuf:
            kb.wait("gpsimd", sEv, base + (i - nbuf + 1) * ns)
        kb.dma("gpsimd", wb[b][:].rearrange("p (k c) -> p k c", c=128),
               wdram.ap()[o].rearrange("(k p) c -> p k c", p=128), sem=cm.sW)
        wv = cm.sW.n
        for si, (s0, sl) in enumerate(slabs):
            ip = j % 2
            kb.wait("tensor", cm.sW, wv)
            if j >= 1:
                kb.wait("tensor", sEv, base + j)
            for k in range(n_k):
                kb.o("tensor", "matmul", pss[ip][:, 0:sl],
                     wb[b][:, k * 128:(k + 1) * 128], rhs_fn(k, s0, sl),
                     start=(k == 0), stop=(k == n_k - 1 and lora is None),
                     sem=(cm.sT if (k == n_k - 1 and lora is None) else None))
            if lora is not None:
                kb.o("tensor", "matmul", pss[ip][:, 0:sl], lora[0](o),
                     lora[1](s0, sl), start=False, stop=True, sem=cm.sT)
            evac_fn(j, o, s0, sl, pss[ip][:, 0:sl])
            j += 1
    return j


def build_l0ffn(debug=False):
    nc = bass.Bass("TRN2", target_bir_lowering=False)
    xin = nc.dram_tensor("xin", [KH, 128, T], F32, kind="ExternalInput")
    wbase = nc.dram_tensor("wbase", [KH, KH * 128, 128], F32, kind="ExternalInput")
    bias = nc.dram_tensor("bias", [128, KH], F32, kind="ExternalInput")
    wr = nc.dram_tensor("wr", [KH * 128, 8], F32, kind="ExternalInput")
    wa = nc.dram_tensor("wa", [KH * 128, 128], F32, kind="ExternalInput")
    wbm = nc.dram_tensor("wbm", [32, 128, 128], F32, kind="ExternalInput")
    e16 = nc.dram_tensor("e16", [8, 128], F32, kind="ExternalInput")
    idf = nc.dram_tensor("idf", [128, 128], F32, kind="ExternalInput")
    xout = nc.dram_tensor("xout", [KH, 128, T], F32, kind="ExternalOutput")
    dbg = {}
    if debug:
        dbg["dxln"] = nc.dram_tensor("dxln", [KH, 128, T], F32, kind="ExternalOutput")
        dbg["dlg"] = nc.dram_tensor("dlg", [8, T], F32, kind="ExternalOutput")
        dbg["dwfm"] = nc.dram_tensor("dwfm", [8, T], F32, kind="ExternalOutput")
        dbg["dhw"] = nc.dram_tensor("dhw", [128, T], F32, kind="ExternalOutput")
        dbg["dwt"] = nc.dram_tensor("dwt", [128, 64], F32, kind="ExternalOutput")
        dbg["dlgt"] = nc.dram_tensor("dlgt", [128, 64], F32, kind="ExternalOutput")
        dbg["dcol"] = nc.dram_tensor("dcol", [128, 6], F32, kind="ExternalOutput")
        dbg["dstep"] = nc.dram_tensor("dstep", [128, 32], F32, kind="ExternalOutput")

    kb = KB(nc)
    cm = Common(kb)
    emit_l0ffn(kb, cm, dict(xin=xin, wbase=wbase, bias=bias, wr=wr, wa=wa,
                            wbm=wbm, e16=e16, idf=idf, xout=xout), dbg=dbg)
    kb.wait("sync", cm.sS)
    kb.finalize()
    return nc


def emit_l0ffn(kb, cm, io, dbg=None):
    xin, wbase, bias = io["xin"], io["wbase"], io["bias"]
    wr, wa, wbm, e16, idf, xout = (io["wr"], io["wa"], io["wbm"], io["e16"],
                                   io["idf"], io["xout"])

    xb = [kb.sb([128, T], BF16, "f_xb") for _ in range(KH)]
    mb = kb.sb([128, T], BF16, "f_mb")
    rb = kb.sb([128, T], BF16, "f_rb")
    wr_sb = kb.sb([128, KH * 8], BF16, "f_wr")
    wa_sb = kb.sb([128, KH * 128], BF16, "f_wa")
    wbm_sb = kb.sb([128, 32 * 128], BF16, "f_wbm")
    e16_sb = kb.sb([8, 128], BF16, "f_e16")
    idf_sb = kb.sb([128, 128], F32, "f_idf")
    bias_sb = kb.sb([128, KH], F32, "f_bias")
    lg = kb.sb([8, T], F32, "f_lg")
    hA = kb.sb([128, T], F32, "f_hA")
    hAw = kb.sb([128, T], BF16, "f_hAw")
    lgt = kb.sb([128, 8], F32, "f_lgt")
    wtok = kb.sb([128, 8], F32, "f_wtok")
    wfm = kb.sb([8, T], BF16, "f_wfm")
    scr = kb.sb([128, 8], F32, "f_scr")
    col = kb.sb([128, 6], F32, "f_col")
    xres = [kb.sb([128, 512], F32, "f_xres") for _ in range(4)]
    dbg_wt = kb.sb([128, 64], F32, "f_dbgwt")
    dbg_st = kb.sb([128, 32], F32, "f_dbgst")
    dbg_lgt = kb.sb([128, 64], F32, "f_dbglgt")
    xo = [kb.sb([128, 512], F32, "f_xo") for _ in range(2)]

    ps8 = [cm.pb[4], cm.pb[5]]
    psA = [cm.pb[6], cm.pb[7]]
    ps_t = cm.pb[4]
    ps_t2 = cm.pb[5]
    psW = cm.pb[6]

    for k in range(KH):
        kb.dma("gpsimd", xb[k][:], xin.ap()[k], sem=cm.sW)
    kb.dma("gpsimd", wr_sb[:].rearrange("p (k e) -> p k e", e=8),
           wr.ap().rearrange("(k p) e -> p k e", p=128), sem=cm.sW)
    kb.dma("gpsimd", wa_sb[:].rearrange("p (k c) -> p k c", c=128),
           wa.ap().rearrange("(k p) c -> p k c", p=128), sem=cm.sW)
    kb.dma("gpsimd", wbm_sb[:].rearrange("p (o c) -> p o c", c=128),
           wbm.ap().rearrange("o p c -> p o c"), sem=cm.sW)
    kb.dma("gpsimd", e16_sb[:], e16.ap(), sem=cm.sW)
    kb.dma("sync", idf_sb[:], idf.ap(), sem=cm.sX)
    kb.dma("sync", bias_sb[:], bias.ap(), sem=cm.sX)
    ldw, ldx = cm.sW.n, cm.sX.n
    for eng in ("tensor", "vector", "scalar"):
        kb.wait(eng, cm.sW, ldw)
        kb.wait(eng, cm.sX, ldx)
        kb.wait(eng, cm.sM)

    emit_ln(kb, cm, xb, T, mb, rb)

    if dbg:
        kb.wait("gpsimd", cm.sV)
        for k in range(KH):
            kb.dma("gpsimd", dbg["dxln"].ap()[k], xb[k][:], sem=cm.sS)

    kb.wait("tensor", cm.sV)
    for si in range(2):
        s0 = si * 512
        for k in range(KH):
            kb.o("tensor", "matmul", ps8[si][0:8, :],
                 wr_sb[:, k * 8:(k + 1) * 8], xb[k][:, s0:s0 + 512],
                 start=(k == 0), stop=(k == KH - 1),
                 sem=(cm.sT if k == KH - 1 else None))
        for k in range(KH):
            kb.o("tensor", "matmul", psA[si][:],
                 wa_sb[:, k * 128:(k + 1) * 128], xb[k][:, s0:s0 + 512],
                 start=(k == 0), stop=(k == KH - 1),
                 sem=(cm.sT if k == KH - 1 else None))
    kb.wait("scalar", cm.sT)
    for si in range(2):
        s0 = si * 512
        kb.o("scalar", "copy", lg[0:8, s0:s0 + 512], ps8[si][0:8, :])
        kb.o("scalar", "copy", hA[:, s0:s0 + 512], psA[si][:],
             sem=(cm.sA if si == 1 else None))

    for c in range(8):
        cs = slice(c * 128, (c + 1) * 128)
        kb.wait("tensor", cm.sA)
        kb.o("tensor", "transpose", ps_t[:, 0:8], lg[0:8, cs], idf_sb[0:8, 0:8],
             sem=cm.sT)
        kb.wait("scalar", cm.sT)
        kb.o("scalar", "copy", lgt[:], ps_t[:, 0:8], sem=cm.sA)
        kb.wait("vector", cm.sA)
        kb.o("vector", "reduce_max", col[:, 0:1], lgt[:], axis=AX, bar=True)
        kb.o("vector", "tensor_scalar", scr[:], lgt[:], col[:, 0:1], None,
             ALU.is_equal, bar=True)
        kb.o("vector", "scalar_tensor_tensor", scr[:], scr[:], -1e30, lgt[:],
             ALU.mult, ALU.add, bar=True)
        kb.o("vector", "reduce_max", col[:, 1:2], scr[:], axis=AX, bar=True)
        kb.o("vector", "tensor_sub", col[:, 2:3], col[:, 1:2], col[:, 0:1],
             bar=True)
        kb.o("vector", "tensor_scalar", col[:, 3:4], col[:, 0:1], -1.0, None,
             ALU.mult, sem=cm.sV, bar=True)
        kb.wait("scalar", cm.sV)
        kb.o("scalar", "activation", col[:, 5:6], col[:, 2:3], AF.Sigmoid,
             scale=-1.0, bar=True)
        kb.o("scalar", "activation", scr[:], lgt[:], AF.Exp,
             bias=col[:, 3:4], bar=True)
        cm.pub("scalar", cm.sA, n=2)
        kb.wait("vector", cm.sA)
        kb.o("vector", "tensor_scalar", wtok[:], lgt[:], col[:, 1:2], None,
             ALU.is_ge, bar=True)
        kb.o("vector", "tensor_mul", wtok[:], wtok[:], scr[:], bar=True)
        kb.o("vector", "tensor_scalar", wtok[:], wtok[:], col[:, 5:6], None,
             ALU.mult, bar=True)
        cm.pub("vector", cm.sV)
        if dbg:
            kb.o("vector", "tensor_copy", dbg_wt[:, c * 8:(c + 1) * 8], wtok[:])
            kb.o("vector", "tensor_copy", dbg_lgt[:, c * 8:(c + 1) * 8], lgt[:],
                 sem=cm.sV)
        kb.wait("tensor", cm.sV)
        kb.o("tensor", "transpose", ps_t2[0:8, 0:128], wtok[:], idf_sb[:], sem=cm.sT)
        kb.wait("scalar", cm.sT)
        kb.o("scalar", "copy", wfm[0:8, cs], ps_t2[0:8, 0:128], sem=cm.sA)

    if dbg:
        kb.wait("gpsimd", cm.sA)
        kb.dma("gpsimd", dbg["dlg"].ap(), lg[0:8, :], sem=cm.sS)

    for si in range(2):
        s0 = si * 512
        kb.wait("tensor", cm.sA)
        if si == 1:
            kb.wait("tensor", cm.sV)
        kb.o("tensor", "matmul", psW[:], e16_sb[:], wfm[0:8, s0:s0 + 512],
             start=True, stop=True, sem=cm.sT)
        kb.wait("vector", cm.sT)
        kb.o("vector", "scalar_tensor_tensor", hAw[:, s0:s0 + 512],
             hA[:, s0:s0 + 512], 2.0, psW[:], ALU.mult, ALU.mult, sem=cm.sV)

    if dbg:
        kb.wait("gpsimd", cm.sV)
        kb.dma("gpsimd", dbg["dwt"].ap(), dbg_wt[:], sem=cm.sS)
        kb.dma("gpsimd", dbg["dcol"].ap(), col[:], sem=cm.sS)
        kb.dma("gpsimd", dbg["dstep"].ap(), dbg_st[:], sem=cm.sS)
        kb.dma("gpsimd", dbg["dlgt"].ap(), dbg_lgt[:], sem=cm.sS)
        kb.dma("gpsimd", dbg["dwfm"].ap(), wfm[0:8, :], sem=cm.sS)
        kb.dma("gpsimd", dbg["dhw"].ap(), hAw[:], sem=cm.sS)

    sEv = kb.sem("f_sEv")
    for jj in range(3):
        o0, s0 = divmod(jj, 2)
        kb.dma("sync", xres[jj % 4][:], xin.ap()[o0][:, s0 * 512:(s0 + 1) * 512],
               sem=cm.sX)

    def evac(j, o, s0, sl, ps):
        jf = j + 3
        if jf < 64:
            of, sf = divmod(jf, 2)
            if j > 0:
                kb.wait("sync", sEv, j)
            kb.dma("sync", xres[jf % 4][:],
                   xin.ap()[of][:, sf * 512:(sf + 1) * 512], sem=cm.sX)
        kb.wait("vector", cm.sT)
        kb.wait("vector", cm.sX, ldx + 16 * (j + 1))
        if j >= 2:
            kb.wait("vector", cm.sS, 16 * (j - 1))
        kb.o("vector", "scalar_tensor_tensor", xo[j % 2][:, 0:sl], ps,
             bias_sb[:, o:o + 1], xres[j % 4][:, 0:sl], ALU.add, ALU.add,
             sem=sEv)
        kb.wait("sync", sEv, sEv.n)
        kb.dma("sync", xout.ap()[o][:, s0:s0 + sl], xo[j % 2][:, 0:sl],
               sem=cm.sS)

    kb.wait("tensor", cm.sV)
    emit_dense(kb, cm, sEv, wbase, KH, KH,
               lambda k, s0, sl: xb[k][:, s0:s0 + sl], T, evac,
               lora=(lambda o: wbm_sb[:, o * 128:(o + 1) * 128],
                     lambda s0, sl: hAw[:, s0:s0 + sl]), dbanks=(0, 1))


def host_prep_l0ffn(inputs, x_fm):
    wbase_t = tile_w_omajor(np.ascontiguousarray(inputs["L0_ffn_Wb"].T))
    wr_t = np.ascontiguousarray(inputs["L0_ffn_Wr"].T)
    wa_t = np.ascontiguousarray(inputs["L0_ffn_A"].reshape(128, H).T)
    wbm_t = np.ascontiguousarray(
        inputs["L0_ffn_Bm"].transpose(0, 2, 1).reshape(128, H)
        .reshape(128, KH, 128).transpose(1, 0, 2))
    bias_t = col_layout(inputs["L0_ffn_bias"])
    e16_t = np.zeros((8, 128), np.float32)
    for e in range(8):
        e16_t[e, e * 16:(e + 1) * 16] = 1.0
    idf_t = np.eye(128, dtype=np.float32)
    maps = []
    for c in range(NCORES):
        maps.append({
            "xin": np.ascontiguousarray(x_fm[c]),
            "wbase": wbase_t, "bias": bias_t, "wr": wr_t, "wa": wa_t,
            "wbm": wbm_t, "e16": e16_t, "idf": idf_t,
        })
    return maps


def x_to_fm(x):
    out = np.empty((NCORES, KH, 128, T), np.float32)
    for c in range(NCORES):
        b, t0 = divmod(c, 4)
        sl = x[b, t0 * T:(t0 + 1) * T, :]
        out[c] = np.ascontiguousarray(sl.T).reshape(KH, 128, T)
    return out


def fm_to_x(fm):
    x = np.empty((2, 4 * T, H), np.float32)
    for c in range(NCORES):
        b, t0 = divmod(c, 4)
        x[b, t0 * T:(t0 + 1) * T, :] = fm[c].reshape(H, T).T
    return x


def kernel(**inputs):
    cores = list(range(8))
    maps0 = host_prep_l0attn(inputs, inputs["L0_attn_hidden_states"])
    res0 = run_bass_kernel_spmd(build_l0attn(), maps0, core_ids=cores)
    x1_fm = np.stack([res0.results[c]["xout"] for c in range(8)])
    maps1 = host_prep_l0ffn(inputs, x1_fm)
    res1 = run_bass_kernel_spmd(build_l0ffn(), maps1, core_ids=cores)
    x2_fm = np.stack([res1.results[c]["xout"] for c in range(8)])
    maps2 = host_prep_l1attn(inputs, x2_fm)
    mapsA = [{k: m[k] for k in ("xin", "wqkv", "bqkv", "cosd", "sind")}
             for m in maps2]
    resA = run_bass_kernel_spmd(build_l1attn_a(), mapsA, core_ids=cores)
    qkv = [resA.results[c]["qkvout"] for c in range(8)]
    kvin = [np.concatenate([qkv[r][c * 768:(c + 1) * 768] for r in range(8)],
                           0) for c in range(8)]
    mapsB = [{"kvin": kvin[c], "masks": maps2[c]["masks"],
              "idf": maps2[c]["idf"]} for c in range(8)]
    resB = run_bass_kernel_spmd(build_l1attn_b(), mapsB, core_ids=cores)
    ctx = [resB.results[c]["ctxout"] for c in range(8)]
    ctxin = [np.concatenate([ctx[r][c * 512:(c + 1) * 512] for r in range(8)],
                            0) for c in range(8)]
    mapsC = [{"xin": maps2[c]["xin"], "ctxin": ctxin[c],
              "wdo": maps2[c]["wdo"], "bd": maps2[c]["bd"]}
             for c in range(8)]
    resC = run_bass_kernel_spmd(build_l1attn_c(), mapsC, core_ids=cores)
    x3_fm = np.stack([resC.results[c]["xout"] for c in range(8)])
    maps3 = host_prep_l1ffn(inputs, x3_fm)
    res3 = run_bass_kernel_spmd(build_l1ffn(), maps3, core_ids=cores)
    x4_fm = np.stack([res3.results[c]["xout"] for c in range(8)])
    return fm_to_x(x4_fm)


def build_l1ffn(debug=False, upto=99):
    nc = bass.Bass("TRN2", target_bir_lowering=False)
    xin = nc.dram_tensor("xin", [KH, 128, T], F32, kind="ExternalInput")
    wq = nc.dram_tensor("wq", [KH * 128, 64], F32, kind="ExternalInput")
    wrp = nc.dram_tensor("wrp", [64, 48], F32, kind="ExternalInput")
    akT = nc.dram_tensor("akT", [64, 1024], F32, kind="ExternalInput")
    av = nc.dram_tensor("av", [8, 128, H], F32, kind="ExternalInput")
    wvd = nc.dram_tensor("wvd", [KH * 128, 256], F32, kind="ExternalInput")
    wvu = nc.dram_tensor("wvu", [2 * 128, H], F32, kind="ExternalInput")
    wu1o = nc.dram_tensor("wu1o", [8, KH * 128, 128], F32, kind="ExternalInput")
    wu2 = nc.dram_tensor("wu2", [128, 8], F32, kind="ExternalInput")
    bu1c = nc.dram_tensor("bu1c", [128, 8], F32, kind="ExternalInput")
    sc3 = nc.dram_tensor("sc3", [1, 4], F32, kind="ExternalInput")
    lr = nc.dram_tensor("lr", [1, 1024], F32, kind="ExternalInput")
    idf = nc.dram_tensor("idf", [128, 128], F32, kind="ExternalInput")
    xout = nc.dram_tensor("xout", [KH, 128, T], F32, kind="ExternalOutput")
    dbg = {}
    if debug:
        dbg["dgate"] = nc.dram_tensor("dgate", [1, T], F32, kind="ExternalOutput")
        dbg["dm8"] = nc.dram_tensor("dm8", [128, 8], F32, kind="ExternalOutput")
        dbg["daux"] = nc.dram_tensor("daux", [128, KH * 128], F32, kind="ExternalOutput")

    kb = KB(nc)
    cm = Common(kb)
    emit_l1ffn(kb, cm, dict(xin=xin, wq=wq, wrp=wrp, akT=akT, av=av, wvd=wvd,
                            wvu=wvu, wu1o=wu1o, wu2=wu2, bu1c=bu1c, sc3=sc3,
                            lr=lr, idf=idf, xout=xout), dbg=dbg, upto=upto)
    kb.wait("sync", cm.sS)
    kb.finalize()
    return nc


def emit_l1ffn(kb, cm, io, dbg=None, upto=99):
    xin, xout = io["xin"], io["xout"]
    RD = np.sqrt(48.0)

    xb = [kb.sb([128, T], BF16, "g_xb") for _ in range(KH)]
    mb = kb.sb([128, T], BF16, "g_mb")
    rb = kb.sb([128, T], BF16, "g_rb")
    idb_sb = kb.sb([128, 128], BF16, "g_idb")
    wq_sb = kb.sb([128, KH * 64], BF16, "g_wq")
    wrp_sb = kb.sb([64, 48], BF16, "g_wrp")
    akT_sb = kb.sb([64, 1024], BF16, "g_akT")
    avbuf = [kb.sb([128, H], BF16, "g_avb") for _ in range(2)]
    wvd_sb = kb.sb([128, KH * 256], BF16, "g_wvd")
    wvu_sb = kb.sb([128, 2 * H], BF16, "g_wvu")
    wu2_sb = kb.sb([128, 8], BF16, "g_wu2")
    bu1_sb = kb.sb([128, 8], F32, "g_bu1")
    sc3_sb = kb.sb([1, 4], F32, "g_sc3")
    lr_sb = kb.sb([1, 1024], F32, "g_lr")
    q_sb = kb.sb([64, T], BF16, "g_q")
    rq_sb = kb.sb([48, T], BF16, "g_rq")
    rk_sb = kb.sb([48, 1024], BF16, "g_rk")
    u1tmp = kb.sb([128, 512], BF16, "g_u1t")
    g0 = kb.sb([128, 512], F32, "g_g0")
    g1 = kb.sb([128, 512], F32, "g_g1")
    g2 = kb.sb([128, 512], F32, "g_g2")
    rowz = kb.sb([1, 16], F32, "g_rowz")
    urow = kb.sb([1, T], F32, "g_urow")
    u2row = kb.sb([1, T], F32, "g_u2row")
    gb_sb = kb.sb([128, T], F32, "g_gb")
    s_orig = kb.sb([128, 1024], F32, "g_sorig")
    eqt = kb.sb([128, 1024], F32, "g_eqt")
    s_work = kb.sb([128, 1024], F32, "g_swork")
    sa_sb = kb.sb([128, 1024], F32, "g_sa")
    p_sb = kb.sb([128, 1024], BF16, "g_p")
    pT_sb = kb.sb([128, 1024], BF16, "g_pT")
    aux_tc = kb.sb([128, KH * 128], BF16, "g_aux")
    vd_sb = kb.sb([128, 256], BF16, "g_vd")
    col = kb.sb([128, 4], F32, "g_col")
    xres = [kb.sb([128, 128], F32, "g_xres") for _ in range(4)]
    xo = [kb.sb([128, 128], F32, "g_xo") for _ in range(2)]

    ar_in = kb.nc.dram_tensor(kb.name("g_arin"), [1, 16], F32, kind="Internal")
    ar_out = kb.nc.dram_tensor(kb.name("g_arout"), [1, 16], F32, kind="Internal")

    for k in range(KH):
        kb.dma("gpsimd", xb[k][:], xin.ap()[k], sem=cm.sW)
    kb.dma("gpsimd", idb_sb[:], io["idf"].ap(), sem=cm.sW)
    kb.dma("gpsimd", wq_sb[:].rearrange("p (k c) -> p k c", c=64),
           io["wq"].ap().rearrange("(k p) c -> p k c", p=128), sem=cm.sW)
    kb.dma("gpsimd", wrp_sb[:], io["wrp"].ap(), sem=cm.sW)
    kb.dma("gpsimd", akT_sb[:], io["akT"].ap(), sem=cm.sW)
    kb.dma("gpsimd", wvd_sb[:].rearrange("p (k c) -> p k c", c=256),
           io["wvd"].ap().rearrange("(k p) c -> p k c", p=128), sem=cm.sW)
    kb.dma("gpsimd", wvu_sb[:].rearrange("p (k c) -> p k c", c=H),
           io["wvu"].ap().rearrange("(k p) c -> p k c", p=128), sem=cm.sW)
    kb.dma("gpsimd", wu2_sb[:], io["wu2"].ap(), sem=cm.sW)
    kb.dma("sync", bu1_sb[:], io["bu1c"].ap(), sem=cm.sX)
    kb.dma("sync", sc3_sb[:], io["sc3"].ap(), sem=cm.sX)
    kb.dma("sync", lr_sb[:], io["lr"].ap(), sem=cm.sX)
    ldw, ldx = cm.sW.n, cm.sX.n
    for eng in ("tensor", "vector", "scalar", "gpsimd"):
        kb.wait(eng, cm.sW, ldw)
        kb.wait(eng, cm.sX, ldx)
        kb.wait(eng, cm.sM)

    mrow, vrow = emit_ln(kb, cm, xb, T, mb, rb)

    if upto < 2:
        return
    kb.o("vector", "memset", rowz[:], 0.0)
    kb.o("vector", "tensor_scalar", urow[0:1, :], vrow[0:1, :], 1e-5, None,
         ALU.add, bar=True)
    kb.o("vector", "reciprocal", urow[0:1, :], urow[0:1, :], bar=True)
    kb.o("vector", "tensor_mul", urow[0:1, :], urow[0:1, :], vrow[0:1, :],
         bar=True)
    kb.wait("scalar", cm.sV, cm.sV.n)
    cm.pub("vector", cm.sV)
    kb.wait("scalar", cm.sV)
    kb.o("scalar", "activation", u2row[0:1, :], urow[0:1, :], AF.Ln,
         bias=cm.c1()[0:1, :], bar=True)
    cm.pub("scalar", cm.sA)
    kb.wait("vector", cm.sA)
    kb.o("vector", "reduce_sum", rowz[0:1, 0:1], u2row[0:1, :], axis=AX,
         bar=True)
    cm.pub("vector", cm.sV)
    kb.wait("gpsimd", cm.sV)
    kb.dma("gpsimd", ar_in.ap(), rowz[:], sem=cm.sG)
    kb.wait("gpsimd", cm.sG)
    kb.cc("AllReduce", ALU.add, [list(range(8))], [ar_in.ap().opt()],
          [ar_out.ap().opt()], cm.sCC)
    kb.wait("gpsimd", cm.sCC)
    kb.dma("gpsimd", rowz[:], ar_out.ap(), sem=cm.sG)

    if upto < 3:
        return
    kb.wait("tensor", cm.sV)
    psQ = [cm.pb[0], cm.pb[1]]
    for si in range(2):
        s0 = si * 512
        for k in range(KH):
            kb.o("tensor", "matmul", psQ[si][0:64, :],
                 wq_sb[:, k * 64:(k + 1) * 64], xb[k][:, s0:s0 + 512],
                 start=(k == 0), stop=(k == KH - 1),
                 sem=(cm.sT if k == KH - 1 else None))
    kb.wait("vector", cm.sT)
    for si in range(2):
        kb.o("vector", "tensor_scalar", q_sb[0:64, si * 512:(si + 1) * 512],
             psQ[si][0:64, :], 0.125, None, ALU.mult,
             sem=(cm.sA if si == 1 else None))
    kb.wait("tensor", cm.sA)
    psR = [cm.pb[2], cm.pb[3]]
    for si in range(2):
        s0 = si * 512
        kb.o("tensor", "matmul", psR[si][0:48, :], wrp_sb[:],
             q_sb[0:64, s0:s0 + 512], start=True, stop=True,
             sem=(cm.sT if si == 1 else None))
    kb.wait("vector", cm.sT)
    for si in range(2):
        s0 = si * 512
        kb.o("vector", "tensor_scalar", rq_sb[0:48, s0:s0 + 512],
             psR[si][0:48, :], 8.0 / RD, None, ALU.mult,
             sem=(cm.sV if si == 1 else None))
    kb.wait("tensor", cm.sV)
    psK = [cm.pb[4], cm.pb[5]]
    for si in range(2):
        s0 = si * 512
        kb.o("tensor", "matmul", psK[si][0:48, :], wrp_sb[:],
             akT_sb[:, s0:s0 + 512], start=True, stop=True,
             sem=(cm.sT if si == 1 else None))
    kb.wait("scalar", cm.sT)
    for si in range(2):
        s0 = si * 512
        kb.o("scalar", "copy", rk_sb[0:48, s0:s0 + 512], psK[si][0:48, :],
             sem=(cm.sA if si == 1 else None))

    if upto < 4:
        return
    sEv1 = kb.sem("g_sEv1")

    def evac_u1(j, o, s0, sl, ps):
        if PH4 < 2:
            kb.wait("vector", cm.sT)
            kb.o("vector", "tensor_scalar", g0[:, 0:sl], ps, 1.0, None,
                 ALU.mult, sem=sEv1)
            return
        kb.wait("vector", cm.sT)
        kb.o("vector", "tensor_scalar", g0[:, 0:sl], ps, bu1_sb[:, o:o + 1],
             None, ALU.add, sem=cm.sV)
        kb.wait("scalar", cm.sV)
        kb.o("scalar", "activation", g1[:, 0:sl], g0[:, 0:sl], AF.Square,
             sem=cm.sA)
        kb.wait("vector", cm.sA)
        kb.o("vector", "tensor_mul", g1[:, 0:sl], g1[:, 0:sl], g0[:, 0:sl])
        kb.o("vector", "scalar_tensor_tensor", g1[:, 0:sl], g1[:, 0:sl],
             0.044715, g0[:, 0:sl], ALU.mult, ALU.add, sem=cm.sV)
        kb.wait("scalar", cm.sV)
        kb.o("scalar", "activation", g2[:, 0:sl], g1[:, 0:sl], AF.Tanh,
             scale=0.7978845608028654, sem=cm.sA)
        kb.wait("vector", cm.sA)
        kb.o("vector", "tensor_scalar", g2[:, 0:sl], g2[:, 0:sl], 1.0, 0.5,
             ALU.add, ALU.mult)
        kb.o("vector", "tensor_tensor", u1tmp[:, 0:sl], g2[:, 0:sl],
             g0[:, 0:sl], ALU.mult, bar=True)
        cm.pub("vector", cm.sV)
        kb.wait("tensor", cm.sV)
        if NO_PSL:
            kb.o("tensor", "matmul", psL[j % 2][0:1, 0:sl], wu2_sb[:, o:o + 1],
                 u1tmp[:, 0:sl], start=True, stop=True, sem=sEv1)
        else:
            kb.o("tensor", "matmul", psL[j % 2][0:1, 0:sl], wu2_sb[:, o:o + 1],
                 u1tmp[:, 0:sl], start=(o == 0), stop=(o == 7), sem=sEv1)

    psL = [cm.pb[0], cm.pb[1]]
    kb.wait("tensor", cm.sA)
    emit_dense(kb, cm, sEv1, io["wu1o"], KH, 8,
               lambda k, s0, sl: xb[k][:, s0:s0 + sl], T, evac_u1,
               dbanks=(6, 7), nbuf=1)

    if PH4 < 3:
        return
    kb.wait("vector", sEv1)
    for si in range(2):
        s0 = si * 512
        kb.o("vector", "tensor_scalar", urow[0:1, s0:s0 + 512],
             psL[si][0:1, :], sc3_sb[0:1, 0:1], None, ALU.add,
             sem=(cm.sV if si == 1 else None), bar=True)
    if PH4 < 4:
        return
    kb.wait("vector", cm.sG)
    kb.o("vector", "tensor_scalar", rowz[0:1, 0:1], rowz[0:1, 0:1],
         1.0 / 8192.0, 1e-6, ALU.mult, ALU.add, bar=True)
    kb.o("vector", "reciprocal", rowz[0:1, 1:2], rowz[0:1, 0:1], bar=True)
    cm.pub("vector", cm.sV)
    kb.wait("scalar", cm.sV)
    kb.o("scalar", "activation", urow[0:1, :], urow[0:1, :], AF.Sigmoid,
         bar=True)
    cm.pub("scalar", cm.sA)
    kb.wait("vector", cm.sA)
    kb.o("vector", "tensor_scalar", u2row[0:1, :], u2row[0:1, :],
         rowz[0:1, 1:2], 0.5, ALU.mult, ALU.mult, bar=True)
    kb.o("vector", "tensor_scalar", urow[0:1, :], urow[0:1, :], 2.5, None,
         ALU.mult, bar=True)
    kb.o("vector", "tensor_add", urow[0:1, :], urow[0:1, :], u2row[0:1, :],
         bar=True)
    kb.o("vector", "tensor_scalar", urow[0:1, :], urow[0:1, :], 0.0, 5.0,
         ALU.max, ALU.min, bar=True)
    kb.o("vector", "tensor_scalar", urow[0:1, :], urow[0:1, :], -0.5,
         1.0 / 1.5, ALU.add, ALU.mult, bar=True)
    kb.o("vector", "tensor_scalar", urow[0:1, :], urow[0:1, :], 0.0, 1.0,
         ALU.max, ALU.min, bar=True)
    cm.pub("vector", cm.sV)
    kb.wait("scalar", cm.sV)
    kb.o("scalar", "activation", urow[0:1, :], urow[0:1, :], AF.Sigmoid,
         bias=sc3_sb[0:1, 2:3], scale=sc3_sb[0:1, 1:2], bar=True)
    cm.pub("scalar", cm.sA)
    kb.wait("vector", cm.sA)
    kb.o("vector", "tensor_scalar", u2row[0:1, :], urow[0:1, :], 0.05, None,
         ALU.is_ge, bar=True)
    kb.o("vector", "tensor_mul", urow[0:1, :], urow[0:1, :], u2row[0:1, :],
         bar=True)
    cm.pub("vector", cm.sV)
    if dbg:
        kb.wait("gpsimd", cm.sV)
        kb.dma("gpsimd", dbg["dgate"].ap(), urow[0:1, :], sem=cm.sS)
    kb.wait("tensor", cm.sV)
    for si in range(2):
        s0 = si * 512
        if si == 1:
            kb.wait("tensor", cm.sA)
        kb.o("tensor", "matmul", cm.pb[2][:], cm.one_row[:],
             urow[0:1, s0:s0 + 512], start=True, stop=True, sem=cm.sT)
        kb.wait("scalar", cm.sT)
        kb.o("scalar", "copy", gb_sb[:, s0:s0 + 512], cm.pb[2][:], sem=cm.sA)

    if upto < 5:
        return
    sEv2 = kb.sem("g_sEv2")
    avals = {}
    avw = {}
    avmm = {}
    for jj in range(3):
        o0 = jj
        kb.dma("sync", xres[jj % 4][:], xin.ap()[o0][:, 0:128], sem=cm.sX)
    for tc in range(8):
        ts0 = tc * 128
        kb.wait("tensor", cm.sA)
        kb.wait("tensor", cm.sV)
        for si in range(2):
            s0 = si * 512
            kb.o("tensor", "matmul", cm.pb[si][:], rq_sb[0:48, ts0:ts0 + 128],
                 rk_sb[0:48, s0:s0 + 512], start=True, stop=False)
            kb.o("tensor", "matmul", cm.pb[si][:], cm.one_row[:],
                 lr_sb[0:1, s0:s0 + 512], start=False, stop=True,
                 sem=cm.sT)
            kb.o("tensor", "matmul", cm.pb[2 + si][:],
                 q_sb[0:64, ts0:ts0 + 128], akT_sb[:, s0:s0 + 512],
                 start=True, stop=False)
            kb.o("tensor", "matmul", cm.pb[2 + si][:], cm.one_row[:],
                 lr_sb[0:1, s0:s0 + 512], start=False, stop=True, sem=cm.sT)
        kb.wait("scalar", cm.sT)
        for si in range(2):
            s0 = si * 512
            kb.o("scalar", "copy", s_orig[:, s0:s0 + 512], cm.pb[si][:])
            kb.o("scalar", "copy", s_work[:, s0:s0 + 512], cm.pb[si][:],
                 sem=(cm.sA if si == 1 else None))
        kb.wait("vector", cm.sA)
        for si in range(2):
            s0 = si * 512
            kb.o("vector", "tensor_scalar", sa_sb[:, s0:s0 + 512],
                 cm.pb[2 + si][:], 1.0, None, ALU.mult,
                 sem=(cm.sV if si == 1 else None))
        for it in range(8):
            kb.o("vector", "reduce_max", col[:, 0:1], s_work[:], axis=AX,
                 bar=True)
            if it < 7:
                kb.o("vector", "tensor_scalar", eqt[:], s_work[:],
                     col[:, 0:1], None, ALU.is_equal, bar=True)
                kb.o("vector", "scalar_tensor_tensor", s_work[:], eqt[:],
                     -1e30, s_work[:], ALU.mult, ALU.add, bar=True)
        kb.o("vector", "tensor_scalar", s_work[:], s_orig[:], col[:, 0:1],
             None, ALU.is_ge, bar=True)
        kb.o("vector", "tensor_scalar", sa_sb[:], sa_sb[:], 1e4, None,
             ALU.add)
        kb.o("vector", "tensor_mul", sa_sb[:], sa_sb[:], s_work[:], bar=True)
        kb.o("vector", "tensor_scalar", sa_sb[:], sa_sb[:], -1e4, None,
             ALU.add, bar=True)
        kb.o("vector", "reduce_max", col[:, 1:2], sa_sb[:], axis=AX, bar=True)
        kb.o("vector", "tensor_scalar", col[:, 2:3], col[:, 1:2], -1.0, None,
             ALU.mult, bar=True)
        cm.pub("vector", cm.sV)
        kb.wait("scalar", cm.sV)
        kb.o("scalar", "activation", p_sb[:], sa_sb[:], AF.Exp,
             bias=col[:, 2:3], accum_out=col[:, 3:4], bar=True)
        cm.pub("scalar", cm.sA)
        kb.wait("vector", cm.sA)
        kb.o("vector", "reciprocal", col[:, 3:4], col[:, 3:4], bar=True)
        kb.o("vector", "tensor_scalar", p_sb[:], p_sb[:], col[:, 3:4], None,
             ALU.mult, bar=True)
        cm.pub("vector", cm.sV)
        kb.wait("tensor", cm.sV)
        for sblk in range(8):
            kb.o("tensor", "transpose",
                 cm.pb[4][:, 64 * (sblk % 2):64 * (sblk % 2) + 64].bitcast(BF16),
                 p_sb[:, sblk * 128:(sblk + 1) * 128], idb_sb[:], sem=cm.sT)
            kb.wait("scalar", cm.sT)
            kb.o("scalar", "copy", pT_sb[:, sblk * 128:(sblk + 1) * 128],
                 cm.pb[4][:, 64 * (sblk % 2):64 * (sblk % 2) + 64].bitcast(BF16),
                 sem=cm.sA)
            if sblk < 7:
                kb.wait("tensor", cm.sA)
        if tc == 0:
            for pre in range(2):
                kb.dma("gpsimd", avbuf[pre][:].rearrange(
                           "p (k c) -> p k c", c=512),
                       io["av"].ap()[:, :, pre * 512:(pre + 1) * 512]
                       .rearrange("k p c -> p k c"), sem=cm.sW)
        kb.wait("tensor", cm.sA)
        kb.wait("tensor", sEv2)
        for og in range(8):
            pbk = cm.pb[5 + og % 2]
            if og >= 2:
                kb.wait("tensor", cm.sA, avals[og - 2])
            kb.wait("tensor", cm.sW, avw[(tc, og)] if (tc, og) in avw
                    else cm.sW.n)
            for oo in range(4):
                o = og * 4 + oo
                for k in range(8):
                    kb.o("tensor", "matmul", pbk[:, oo * 128:oo * 128 + 128],
                         avbuf[og % 2][:, k * 512 + oo * 128:k * 512 + (oo + 1) * 128],
                         pT_sb[:, k * 128:(k + 1) * 128],
                         start=(k == 0 and oo == 0), stop=(k == 7 and oo == 3),
                         sem=(cm.sT if (k == 7 and oo == 3) else None))
            avmm[(tc, og)] = cm.sT.n
            ng, ntc = og + 2, tc
            if ng >= 8:
                ng, ntc = ng - 8, tc + 1
            if ntc < 8:
                kb.wait("gpsimd", cm.sT, avmm[(tc, og)])
                kb.dma("gpsimd", avbuf[og % 2][:].rearrange(
                           "p (k c) -> p k c", c=512),
                       io["av"].ap()[:, :, ng * 512:(ng + 1) * 512]
                       .rearrange("k p c -> p k c"), sem=cm.sW)
                avw[(ntc, ng)] = cm.sW.n
            kb.wait("scalar", cm.sT, avmm[(tc, og)])
            kb.o("scalar", "copy", aux_tc[:, og * 512:(og + 1) * 512], pbk[:],
                 sem=cm.sA)
            avals[og] = cm.sA.n
        kb.wait("tensor", cm.sA)
        for o2 in range(2):
            for k in range(KH):
                kb.o("tensor", "matmul",
                     cm.pb[7][:, o2 * 128:o2 * 128 + 128],
                     wvd_sb[:, k * 256 + o2 * 128:k * 256 + o2 * 128 + 128],
                     aux_tc[:, k * 128:(k + 1) * 128],
                     start=(k == 0 and o2 == 0), stop=(k == KH - 1 and o2 == 1),
                     sem=(cm.sT if (k == KH - 1 and o2 == 1) else None))
        kb.wait("scalar", cm.sT)
        kb.o("scalar", "copy", vd_sb[:], cm.pb[7][:, 0:256], sem=cm.sA)
        kb.wait("tensor", cm.sA)
        for o in range(KH):
            j = tc * KH + o
            pbk = cm.pb[5 + j % 2]
            if j >= 2:
                kb.wait("tensor", sEv2, j - 1)
            for k2 in range(2):
                kb.o("tensor", "matmul", pbk[:, 0:128],
                     wvu_sb[:, k2 * H + o * 128:k2 * H + (o + 1) * 128],
                     vd_sb[:, k2 * 128:(k2 + 1) * 128],
                     start=(k2 == 0), stop=(k2 == 1),
                     sem=(cm.sT if k2 == 1 else None))
            jf = j + 3
            if jf < 256:
                tf, of = divmod(jf, KH)
                if j > 0:
                    kb.wait("sync", sEv2, j)
                kb.dma("sync", xres[jf % 4][:],
                       xin.ap()[of][:, tf * 128:(tf + 1) * 128], sem=cm.sX)
            kb.wait("vector", cm.sT)
            kb.wait("vector", cm.sX, ldx + 16 * (j + 1))
            if j >= 2:
                kb.wait("vector", cm.sS, 16 * (j - 1))
            kb.o("vector", "tensor_tensor", xo[j % 2][:], pbk[:, 0:128],
                 gb_sb[:, ts0:ts0 + 128], ALU.mult)
            kb.o("vector", "tensor_tensor", xo[j % 2][:], xo[j % 2][:],
                 xb[o][:, ts0:ts0 + 128], ALU.add)
            kb.o("vector", "tensor_tensor", xo[j % 2][:], xo[j % 2][:],
                 xres[j % 4][:], ALU.add, sem=sEv2)
            kb.wait("sync", sEv2, sEv2.n)
            kb.dma("sync", xout.ap()[o][:, ts0:ts0 + 128], xo[j % 2][:],
                   sem=cm.sS)
        if dbg and tc == 0:
            kb.wait("gpsimd", cm.sA)
            kb.dma("gpsimd", dbg["daux"].ap(), aux_tc[:], sem=cm.sS)


def host_prep_l1ffn(inputs, x_fm):
    wq_t = np.ascontiguousarray(inputs["L1_ffn_Wq"].T)
    wrp_t = np.ascontiguousarray(inputs["L1_ffn_Wr"].T)
    akT_t = np.ascontiguousarray(inputs["L1_ffn_aux_keys"].T)
    av_t = np.ascontiguousarray(
        inputs["L1_ffn_aux_values"].reshape(8, 128, H))
    wvd_t = np.ascontiguousarray(inputs["L1_ffn_Wvd"].T)
    wvu_t = np.ascontiguousarray(inputs["L1_ffn_Wvu"].T)
    wu1o_t = tile_w_omajor(np.ascontiguousarray(inputs["L1_ffn_Wu1"].T))
    wu2_t = np.ascontiguousarray(inputs["L1_ffn_Wu2"].reshape(8, 128).T)
    bu1c_t = col_layout(inputs["L1_ffn_bu1"])
    sc3_t = np.array([[inputs["L1_ffn_bu2"].reshape(-1)[0],
                       float(inputs["L1_ffn_gate_w1"]),
                       float(inputs["L1_ffn_gate_bias"]), 0.0]], np.float32)
    lr_t = np.ascontiguousarray(inputs["L1_ffn_log_rel"].reshape(1, 1024))
    idf_t = np.eye(128, dtype=np.float32)
    maps = []
    for c in range(NCORES):
        maps.append({
            "xin": np.ascontiguousarray(x_fm[c]), "wq": wq_t, "wrp": wrp_t,
            "akT": akT_t, "av": av_t, "wvd": wvd_t, "wvu": wvu_t,
            "wu1o": wu1o_t, "wu2": wu2_t, "bu1c": bu1c_t, "sc3": sc3_t,
            "lr": lr_t, "idf": idf_t,
        })
    return maps


def build_l1attn_a():
    nc = bass.Bass("TRN2", target_bir_lowering=False)
    xin = nc.dram_tensor("xin", [KH, 128, T], F32, kind="ExternalInput")
    wqkv = nc.dram_tensor("wqkv", [48, KH * 128, 128], F32, kind="ExternalInput")
    bqkv = nc.dram_tensor("bqkv", [128, 48], F32, kind="ExternalInput")
    cosd = nc.dram_tensor("cosd", [128, T], F32, kind="ExternalInput")
    sind = nc.dram_tensor("sind", [128, T], F32, kind="ExternalInput")
    qkvout = nc.dram_tensor("qkvout", [6144, 1024], BF16, kind="ExternalOutput")
    kb = KB(nc)
    cm = Common(kb, groups=(4, 1, 1, 1, 1))
    emit_l1attn_qkv(kb, cm, dict(xin=xin, wqkv=wqkv, bqkv=bqkv, cosd=cosd,
                                 sind=sind), qkvout)
    kb.wait("sync", cm.sS)
    kb.finalize()
    return nc


def build_l1attn_b():
    nc = bass.Bass("TRN2", target_bir_lowering=False)
    kvin = nc.dram_tensor("kvin", [6144, 1024], BF16, kind="ExternalInput")
    masks = nc.dram_tensor("masks", [6, 128, 512], F32, kind="ExternalInput")
    idf = nc.dram_tensor("idf", [128, 128], F32, kind="ExternalInput")
    ctxout = nc.dram_tensor("ctxout", [4096, 1024], BF16, kind="ExternalOutput")
    kb = KB(nc)
    cm = Common(kb, groups=(4, 1, 1, 1, 1))
    emit_l1attn_att(kb, cm, dict(masks=masks, idf=idf), kvin, ctxout)
    kb.wait("sync", cm.sS)
    kb.finalize()
    return nc


def build_l1attn_c():
    nc = bass.Bass("TRN2", target_bir_lowering=False)
    xin = nc.dram_tensor("xin", [KH, 128, T], F32, kind="ExternalInput")
    ctxin = nc.dram_tensor("ctxin", [4096, 1024], BF16, kind="ExternalInput")
    wdo = nc.dram_tensor("wdo", [KH, KH * 128, 128], F32, kind="ExternalInput")
    bd = nc.dram_tensor("bd", [128, KH], F32, kind="ExternalInput")
    xout = nc.dram_tensor("xout", [KH, 128, T], F32, kind="ExternalOutput")
    kb = KB(nc)
    cm = Common(kb, groups=(4, 1, 1, 1, 1))
    emit_l1attn_dense(kb, cm, dict(xin=xin, wdo=wdo, bd=bd, xout=xout), ctxin)
    kb.wait("sync", cm.sS)
    kb.finalize()
    return nc


def emit_l1attn_qkv(kb, cm, io, a2a_in):
    xin = io["xin"]

    xb = [kb.sbt([128, T], BF16, "a_xb") for _ in range(KH)]
    mb = kb.sbt([128, T], BF16, "a_mb")
    rb = kb.sbt([128, T], BF16, "a_rb")
    cos_sb = kb.sbt([128, T], F32, "a_cos")
    sin_sb = kb.sbt([128, T], F32, "a_sin")
    g0 = kb.sbt([128, 512], F32, "a_g0")
    rtmp = kb.sbt([128, 512], F32, "a_rt")
    qbuf = [kb.sbt([128, 512], BF16, "a_qb") for _ in range(2)]
    bq_sb = kb.sbt([128, 48], F32, "a_bq")

    for k in range(KH):
        kb.dma("gpsimd", xb[k][:], xin.ap()[k], sem=cm.sW)
    kb.dma("sync", cos_sb[:], io["cosd"].ap(), sem=cm.sX)
    kb.dma("sync", sin_sb[:], io["sind"].ap(), sem=cm.sX)
    kb.dma("sync", bq_sb[:], io["bqkv"].ap(), sem=cm.sX)
    ldw, ldx = cm.sW.n, cm.sX.n
    for eng in ("tensor", "vector", "scalar"):
        kb.wait(eng, cm.sW, ldw)
        kb.wait(eng, cm.sX, ldx)
        kb.wait(eng, cm.sM)
    emit_ln(kb, cm, xb, T, mb, rb, banks=(0, 4, 1, 5))

    sEvq = kb.sem("a_sEvq")

    def evac_qkv(j, o, s0, sl, ps):
        rope = (o % 6) < 5
        qb = qbuf[j % 2]
        kb.wait("vector", cm.sT)
        if j >= 2:
            kb.wait("vector", cm.sS, 16 * (j - 1))
        kb.o("vector", "tensor_scalar", g0[:, 0:sl], ps, bq_sb[:, o:o + 1],
             None, ALU.add, sem=cm.sV)
        if rope:
            kb.wait("scalar", cm.sV)
            kb.o("scalar", "copy", rtmp[0:64, 0:sl], g0[64:128, 0:sl])
            kb.o("scalar", "copy", rtmp[64:128, 0:sl], g0[0:64, 0:sl],
                 sem=cm.sA)
            kb.wait("vector", cm.sA)
            kb.o("vector", "tensor_mul", rtmp[:, 0:sl], rtmp[:, 0:sl],
                 sin_sb[:, s0:s0 + sl])
            kb.o("vector", "tensor_mul", g0[:, 0:sl], g0[:, 0:sl],
                 cos_sb[:, s0:s0 + sl])
            kb.o("vector", "tensor_add", qb[:, 0:sl], g0[:, 0:sl],
                 rtmp[:, 0:sl], sem=sEvq)
        else:
            kb.o("vector", "tensor_scalar", qb[:, 0:sl], g0[:, 0:sl], 1.0,
                 None, ALU.mult, sem=sEvq)
        kb.wait("sync", sEvq, sEvq.n)
        kb.dma("sync", a2a_in.ap()[o * 128:(o + 1) * 128, s0:s0 + sl],
               qb[:, 0:sl], sem=cm.sS)

    kb.wait("tensor", cm.sV)
    emit_dense(kb, cm, sEvq, io["wqkv"], KH, 48,
               lambda k, s0, sl: xb[k][:, s0:s0 + sl], T, evac_qkv,
               dbanks=(4, 5))
    kb.free(*xb, mb, rb, cos_sb, sin_sb, g0, rtmp, *qbuf, bq_sb)


def emit_l1attn_att(kb, cm, io, a2a_out, a2b_in):
    Kb_sb = kb.sbt([128, 4096], BF16, "a_K")
    Vt_sb = kb.sbt([128, 4096], BF16, "a_V")
    qI = kb.sbt([128, 8192], BF16, "a_qI")
    ctx_sb = kb.sbt([128, 512], BF16, "a_ctx")
    mks = kb.sbt([128, 6 * 512], F32, "a_mks")
    idf_sb = kb.sbt([128, 128], F32, "a_idf")
    idb_sb = kb.sbt([128, 128], BF16, "a_idb")
    mrow_sb = kb.sbt([1, 512], F32, "a_mrow")
    mbc = kb.sbt([128, 512], F32, "a_mbc")
    rbc = kb.sbt([128, 512], F32, "a_rbc")
    srow = kb.sbt([1, 512], F32, "a_srow")
    ps_big = cm.pb

    kb.dma("sync", idf_sb[:], io["idf"].ap(), sem=cm.sX)
    kb.dma("gpsimd", idb_sb[:], io["idf"].ap(), sem=cm.sW)
    kb.dma("sync", mks[:].rearrange("p (m c) -> p m c", c=512),
           io["masks"].ap().rearrange("m p c -> p m c"), sem=cm.sX)

    kb._mks = mks

    def mk(i):
        return mks[:, i * 512:(i + 1) * 512]

    for eng in ("tensor", "vector", "scalar", "gpsimd", "sync"):
        kb.wait(eng, cm.sX, cm.sX.n)
        kb.wait(eng, cm.sW, cm.sW.n)
        kb.wait(eng, cm.sM)

    emit_l1attn_body(kb, cm, io, a2a_out, a2b_in,
                     Kb_sb, Vt_sb, qI, ctx_sb, mk, idf_sb, idb_sb,
                     mrow_sb, mbc, rbc, srow)



def emit_l1attn_body(kb, cm, io, a2a_out, a2b_in, Kb_sb, Vt_sb, qI,
                     ctx_sb, mk, idf_sb, idb_sb, mrow_sb, mbc, rbc, srow):
    SM = 1.0 / 128.0
    PS = cm.tg[0]
    t_sb = kb.sbt([128, 512], F32, "a_t")
    pT_sb = kb.sbt([128, 512], BF16, "a_pT")
    m4 = kb.sbt([128, 4], F32, "a_m4")
    for b in range(2):
        kb.wait("gpsimd", cm.sT, cm.sT.n)
        kb.dma("gpsimd", Kb_sb[:].rearrange("p (r t) -> p r t", t=1024),
               a2a_out.ap().rearrange("(r q p) t -> q p r t", q=6, p=128)
               [4][:, 4 * b:4 * b + 4], sem=cm.sW)
        kb.dma("gpsimd", Vt_sb[:].rearrange("p (r t) -> p r t", t=1024),
               a2a_out.ap().rearrange("(r q p) t -> q p r t", q=6, p=128)
               [5][:, 4 * b:4 * b + 4], sem=cm.sW)
        wv = cm.sW.n
        for eng in ("tensor", "vector", "scalar"):
            kb.wait(eng, cm.sW, wv)
        for vt in range(32):
            if vt >= 1:
                kb.wait("tensor", cm.sA)
            kb.o("tensor", "transpose",
                 cm.pb[7][:, 64 * (vt % 2):64 * (vt % 2) + 64].bitcast(BF16),
                 Vt_sb[:, vt * 128:(vt + 1) * 128], idb_sb[:], sem=cm.sT)
            kb.wait("scalar", cm.sT)
            kb.o("scalar", "copy", Vt_sb[:, vt * 128:(vt + 1) * 128],
                 cm.pb[7][:, 64 * (vt % 2):64 * (vt % 2) + 64].bitcast(BF16),
                 sem=cm.sA)
        for qh in range(2):
            kb.wait("gpsimd", cm.sT, cm.sT.n)
            for h in range(4):
                for rr in range(2):
                    kb.dma("gpsimd",
                           qI[:].rearrange("p (r blk hh q) -> p hh r blk q",
                                           r=2, blk=8, hh=4)[:, h, rr],
                           a2a_out.ap()
                           .rearrange("(r q p) t -> q p r t", q=6, p=128)
                           [h][:, 4 * b + 2 * qh + rr]
                           .rearrange("p (blk q) -> p blk q", q=128),
                           sem=cm.sW)
            wv2 = cm.sW.n
            kb.wait("tensor", cm.sW, wv2)
            for P in range(16):
                Q = qh * 16 + P
                stri = list(range(3, Q - 8, 4))
                wnd = list(range(max(0, Q - 8), Q + 1))
                tiles = stri + wnd
                nt = len(tiles)

                def tmask(kp):
                    if kp in stri:
                        return 5
                    if Q >= 8 and kp == Q - 8:
                        return 5 if Q % 4 == 3 else 4
                    if kp == Q:
                        return 3
                    return None

                for h in range(4):
                    kb.wait("tensor", cm.sV)
                    kb.wait("tensor", cm.sA)
                    lq = qI[:, P * 512 + h * 128:P * 512 + (h + 1) * 128]
                    for ti, kp in enumerate(tiles):
                        kb.o("tensor", "matmul", PS[:, ti * 128:(ti + 1) * 128],
                             lq, Kb_sb[:, kp * 128:(kp + 1) * 128],
                             start=True, stop=True,
                             sem=(cm.sT if ti == nt - 1 else None))
                    kb.wait("vector", cm.sT)
                    for ti, kp in enumerate(tiles):
                        mi = tmask(kp)
                        reg = PS[:, ti * 128:(ti + 1) * 128]
                        if mi is not None:
                            kb.o("vector", "scalar_tensor_tensor", reg, reg,
                                 SM, mkT(kb, mi), ALU.mult, ALU.add)
                        else:
                            kb.o("vector", "tensor_scalar", reg, reg, SM,
                                 None, ALU.mult)
                    kb.o("vector", "reduce_max", m4[:, h:h + 1],
                         PS[:, 0:nt * 128], axis=AX, bar=True)
                    cm.pub("vector", cm.sV)
                kb.wait("tensor", cm.sV)
                for h in range(4):
                    kb.o("tensor", "transpose",
                         cm.pb[7][0:1, h * 128:(h + 1) * 128],
                         m4[:, h:h + 1], idf_sb[:],
                         sem=(cm.sT if h == 3 else None))
                kb.wait("vector", cm.sT)
                kb.o("vector", "tensor_scalar", mrow_sb[0:1, :],
                     cm.pb[7][0:1, 0:512], -1.0, None, ALU.mult, bar=True)
                cm.pub("vector", cm.sV)
                kb.wait("tensor", cm.sV)
                kb.o("tensor", "matmul", cm.pb[7][:], cm.one_row[:],
                     mrow_sb[0:1, :], start=True, stop=True, sem=cm.sT)
                kb.wait("scalar", cm.sT)
                kb.o("scalar", "copy", mbc[:], cm.pb[7][:], sem=cm.sA)
                kb.wait("vector", cm.sA)
                for ti, kp in enumerate(tiles):
                    if ti > 0:
                        kb.wait("tensor", cm.sA, aexp)
                    else:
                        kb.wait("tensor", cm.sV)
                    kb.o("tensor", "matmul", cm.pb[4][:],
                         Kb_sb[:, kp * 128:(kp + 1) * 128],
                         qI[:, P * 512:(P + 1) * 512],
                         start=True, stop=True, sem=cm.sT)
                    kb.wait("vector", cm.sT)
                    mi = tmask(kp)
                    if mi is not None:
                        kb.o("vector", "scalar_tensor_tensor", t_sb[:],
                             cm.pb[4][:], SM, mk(mi), ALU.mult, ALU.add)
                        kb.o("vector", "tensor_sub", t_sb[:], t_sb[:], mbc[:],
                             sem=cm.sV)
                    else:
                        kb.o("vector", "scalar_tensor_tensor", t_sb[:],
                             cm.pb[4][:], SM, mbc[:], ALU.mult, ALU.subtract,
                             sem=cm.sV)
                    kb.wait("scalar", cm.sV)
                    kb.o("scalar", "activation", pT_sb[:], t_sb[:], AF.Exp,
                         sem=cm.sA)
                    aexp = cm.sA.n
                    kb.wait("tensor", cm.sA, aexp)
                    kb.o("tensor", "matmul", cm.pb[5][:],
                         Vt_sb[:, kp * 128:(kp + 1) * 128], pT_sb[:],
                         start=(ti == 0), stop=(ti == nt - 1))
                    kb.o("tensor", "matmul", cm.pb[6][0:1, 0:512],
                         cm.ones_col_bf[:], pT_sb[:],
                         start=(ti == 0), stop=(ti == nt - 1),
                         sem=(cm.sT if ti == nt - 1 else None))
                kb.wait("vector", cm.sT)
                kb.o("vector", "reciprocal", srow[0:1, :],
                     cm.pb[6][0:1, 0:512], bar=True)
                cm.pub("vector", cm.sV)
                kb.wait("tensor", cm.sV)
                kb.o("tensor", "matmul", cm.pb[7][:], cm.one_row[:],
                     srow[0:1, :], start=True, stop=True, sem=cm.sT)
                kb.wait("scalar", cm.sT)
                kb.o("scalar", "copy", rbc[:], cm.pb[7][:], sem=cm.sA)
                kb.wait("vector", cm.sA)
                if P > 0 or qh > 0 or b > 0:
                    kb.wait("vector", cm.sS, cm.sS.n)
                kb.o("vector", "tensor_tensor", ctx_sb[:], cm.pb[5][:],
                     rbc[:], ALU.mult, sem=cm.sV)
                kb.wait("sync", cm.sV)
                gt0 = qh * 2048 + P * 128
                r = 4 * b + gt0 // 1024
                tl = gt0 % 1024
                kb.dma("sync",
                       a2b_in.ap()[r * 512:(r + 1) * 512, tl:tl + 128]
                       .rearrange("(hh d) q -> d hh q", d=128),
                       ctx_sb[:].rearrange("d (hh q) -> d hh q", q=128),
                       sem=cm.sS)
    kb.free(Kb_sb, Vt_sb, qI, ctx_sb, t_sb, pT_sb, m4, kb._mks,
            idf_sb, idb_sb, mrow_sb, mbc, rbc, srow)


def emit_l1attn_dense(kb, cm, io, a2b_out):
    ctx_all = [kb.sbt([128, T], BF16, "a_ctxall") for _ in range(KH)]
    bd_sb = kb.sbt([128, KH], F32, "a_bd")
    xres = [kb.sbt([128, 512], F32, "a_xres") for _ in range(4)]
    xo = [kb.sbt([128, 512], F32, "a_xo") for _ in range(2)]
    for eng in ("tensor", "vector", "scalar", "gpsimd"):
        kb.wait(eng, cm.sM)
    for k in range(KH):
        kb.dma("gpsimd", ctx_all[k][:], a2b_out.ap()[k * 128:(k + 1) * 128, :],
               sem=cm.sW)
    kb.dma("sync", bd_sb[:], io["bd"].ap(), sem=cm.sX)
    ldw2, ldx2 = cm.sW.n, cm.sX.n
    for eng in ("tensor", "vector"):
        kb.wait(eng, cm.sW, ldw2)
        kb.wait(eng, cm.sX, ldx2)
    sEvd = kb.sem("a_sEvd")
    sbase = cm.sS.n
    xbase = cm.sX.n
    for jj in range(3):
        of, sf = divmod(jj, 2)
        kb.dma("sync", xres[jj % 4][:],
               io["xin"].ap()[of][:, sf * 512:(sf + 1) * 512], sem=cm.sX)

    def evac_d(j, o, s0, sl, ps):
        jf = j + 3
        if jf < 64:
            of, sf = divmod(jf, 2)
            if j > 0:
                kb.wait("sync", sEvd, j)
            kb.dma("sync", xres[jf % 4][:],
                   io["xin"].ap()[of][:, sf * 512:(sf + 1) * 512], sem=cm.sX)
        kb.wait("vector", cm.sT)
        kb.wait("vector", cm.sX, xbase + 16 * (j + 1))
        if j >= 2:
            kb.wait("vector", cm.sS, sbase + 16 * (j - 1))
        kb.o("vector", "scalar_tensor_tensor", xo[j % 2][:, 0:sl], ps,
             bd_sb[:, o:o + 1], xres[j % 4][:, 0:sl], ALU.add, ALU.add,
             sem=sEvd)
        kb.wait("sync", sEvd, sEvd.n)
        kb.dma("sync", io["xout"].ap()[o][:, s0:s0 + sl], xo[j % 2][:, 0:sl],
               sem=cm.sS)

    emit_dense(kb, cm, sEvd, io["wdo"], KH, KH,
               lambda k, s0, sl: ctx_all[k][:, s0:s0 + sl], T, evac_d,
               dbanks=(4, 5))


def mkT(kb, mi):
    return kb._mks[:, (mi - 3) * 512:(mi - 3) * 512 + 128]


def _bs_masks():
    NEG = -1e30
    k = np.arange(128)[:, None]
    q = np.arange(128)[None, :]
    vis_diag = ((k < 64) & (q < 64) & (k <= q)) | ((k < 64) & (q >= 64)) | \
               ((k >= 64) & (q >= 64) & (k - 64 <= q - 64))
    vis_edge = (k >= 64) & (q < 64)
    vis_stride = np.broadcast_to(k >= 64, (128, 128))
    out = np.zeros((6, 128, 512), np.float32)
    for i, vis in enumerate([vis_diag, vis_edge, vis_stride]):
        m = np.where(vis, 0.0, NEG).astype(np.float32)
        out[i, :, 0:128] = m.T
        out[3 + i] = np.tile(m, (1, 4))
    return out


def host_prep_l1attn(inputs, x_fm):
    wqkv_t = tile_w_omajor(np.ascontiguousarray(inputs["L1_attn_w_qkv"].T))
    bqkv_t = col_layout(inputs["L1_attn_b_qkv"])
    wdo_t = tile_w_omajor(np.ascontiguousarray(inputs["L1_attn_w_dense"].T))
    bd_t = col_layout(inputs["L1_attn_b_dense"])
    inv_freq = 1.0 / (1e6 ** (np.arange(0, 128, 2, np.float32) / 128.0))
    masks_t = _bs_masks()
    idf_t = np.eye(128, dtype=np.float32)
    maps = []
    for c in range(NCORES):
        t0 = (c % 4) * T
        pos = np.arange(t0, t0 + T, dtype=np.float32)
        fr = np.outer(inv_freq, pos)
        cos_t = np.concatenate([np.cos(fr), np.cos(fr)], 0).astype(np.float32)
        sin_t = np.concatenate([-np.sin(fr), np.sin(fr)], 0).astype(np.float32)
        maps.append({
            "xin": np.ascontiguousarray(x_fm[c]), "wqkv": wqkv_t,
            "bqkv": bqkv_t, "wdo": wdo_t, "bd": bd_t, "cosd": cos_t,
            "sind": sin_t, "masks": masks_t, "idf": idf_t,
        })
    return maps


def build_l0attn(debug=False, upto=99):
    nc = bass.Bass("TRN2", target_bir_lowering=False,
                   detect_race_conditions=DETECT_RACES)
    xT = nc.dram_tensor("xT", [KH, 128, TH], F32, kind="ExternalInput")
    w1t = nc.dram_tensor("w1t", [145, KH * 128, 128], F32, kind="ExternalInput")
    convw = nc.dram_tensor("convw", [128, 80 * 4], F32, kind="ExternalInput")
    convb = nc.dram_tensor("convb", [128, 80], F32, kind="ExternalInput")
    dtb = nc.dram_tensor("dtb", [128, 1], F32, kind="ExternalInput")
    alog = nc.dram_tensor("alog", [128, 1], F32, kind="ExternalInput")
    DD = nc.dram_tensor("DD", [1, 128], F32, kind="ExternalInput")
    normw = nc.dram_tensor("normw", [128, 64], F32, kind="ExternalInput")
    opw = nc.dram_tensor("opw", [KH, 64 * 128, 128], F32, kind="ExternalInput")
    upper = nc.dram_tensor("upper", [128, 128], F32, kind="ExternalInput")
    maskui = nc.dram_tensor("maskui", [128, 128], F32, kind="ExternalInput")
    idf = nc.dram_tensor("idf", [128, 128], F32, kind="ExternalInput")
    msel = nc.dram_tensor("msel", [128, 16], F32, kind="ExternalInput")
    selbc = nc.dram_tensor("selbc", [128, 4], F32, kind="ExternalInput")
    xout = nc.dram_tensor("xout", [KH, 128, T], F32, kind="ExternalOutput")
    kb = KB(nc)
    cm = Common(kb, groups=(2, 2, 1, 1, 2))
    emit_l0attn(kb, cm, dict(xT=xT, w1t=w1t, convw=convw, convb=convb, dtb=dtb,
                             alog=alog, DD=DD, normw=normw, opw=opw,
                             upper=upper, maskui=maskui, idf=idf, msel=msel,
                             selbc=selbc, xout=xout), upto=upto)
    kb.wait("sync", cm.sS)
    kb.finalize()
    return nc


def emit_l0attn(kb, cm, io, upto=99):
    nc = kb.nc
    ag_in = nc.dram_tensor(kb.name("m_agin"), [128, 8192], F32, kind="Internal")
    ag_out = nc.dram_tensor(kb.name("m_agout"), [512, 8192], F32, kind="Internal")
    aga_in = nc.dram_tensor(kb.name("m_againa"), [128, 64], F32, kind="Internal")
    aga_out = nc.dram_tensor(kb.name("m_agouta"), [512, 64], F32, kind="Internal")
    y3loc = nc.dram_tensor(kb.name("m_y3"), [64, 128, 1024], BF16, kind="Internal")
    sgt = nc.dram_tensor(kb.name("m_sgt"), [64, 128, 1024], BF16, kind="Internal")
    yfd = nc.dram_tensor(kb.name("m_yf"), [64, 128, 1024], BF16, kind="Internal")

    convw_sb = kb.sbt([128, 320], F32, "m_cw")
    convb_sb = kb.sbt([128, 80], F32, "m_cb")
    dtb_sb = kb.sbt([128, 1], F32, "m_dtb")
    alog_sb = kb.sbt([128, 1], F32, "m_alog")
    dd_row = kb.sbt([1, 128], F32, "m_ddrow")
    normw_sb = kb.sbt([128, 64], F32, "m_nw")
    upper_sb = kb.sbt([128, 128], BF16, "m_upb")
    maskui_bf = kb.sbt([128, 128], BF16, "m_muib")
    idf_sb = kb.sbt([128, 128], F32, "m_idf")
    idb_sb = kb.sbt([128, 128], BF16, "m_idb")
    msel_sb = kb.sbt([128, 16], F32, "m_msel")
    selbc_sb = kb.sbt([128, 4], F32, "m_selbc")
    ddbc = kb.sbt([128, 128], F32, "m_ddbc")
    dt_tok = kb.sbt([128, T], F32, "m_dttok")
    a_tok = kb.sbt([128, T], F32, "m_atok")
    sd_tok = kb.sbt([128, T], F32, "m_sdtok")
    dtds2 = kb.sbt([128, T], F32, "m_dtds2")
    sdp_tok = kb.sbt([128, T], F32, "m_sdp")
    dtot_row = kb.sbt([1, 1024], F32, "m_dtot")
    dtotbc = kb.sbt([128, 1024], F32, "m_dtotbc")
    pc_row = kb.sbt([1, 1024], F32, "m_pc")
    aczero = kb.sbt([128, 64], F32, "m_acz")
    C_all = kb.sbt([128, 8 * 1024], BF16, "m_call")
    S_f = kb.sbt([128, 1024], F32, "m_sf")
    S_bf = kb.sbt([128, 1024], BF16, "m_sbf")
    wdense = [kb.sbt([128, KH * 128], BF16, "m_dw") for _ in range(2)]
    xb = [kb.sbt([128, TH], BF16, "m_xb") for _ in range(KH)]
    mb = kb.sbt([128, TH], BF16, "m_mb")
    rb = kb.sbt([128, TH], BF16, "m_rb")

    for k in range(KH):
        kb.dma("gpsimd", xb[k][:], io["xT"].ap()[k], sem=cm.sW)
    kb.dma("gpsimd", upper_sb[:], io["upper"].ap(), sem=cm.sW)
    kb.dma("gpsimd", idb_sb[:], io["idf"].ap(), sem=cm.sW)
    kb.dma("gpsimd", maskui_bf[:], io["maskui"].ap(), sem=cm.sW)
    for nm, t in (("convw", convw_sb), ("convb", convb_sb), ("dtb", dtb_sb),
                  ("alog", alog_sb), ("DD", dd_row), ("normw", normw_sb),
                  ("idf", idf_sb), ("msel", msel_sb),
                  ("selbc", selbc_sb)):
        kb.dma("sync", t[:], io[nm].ap(), sem=cm.sX)
    ldw, ldx = cm.sW.n, cm.sX.n
    for eng in ("tensor", "vector", "scalar"):
        kb.wait(eng, cm.sW, ldw)
        kb.wait(eng, cm.sX, ldx)
        kb.wait(eng, cm.sM)

    emit_ln(kb, cm, xb, TH, mb, rb, banks=(0, 2, 4, 5))
    kb.wait("vector", cm.sV)
    kb.free(rb, mb)

    if upto < 1.2:
        return
    kb.wait("tensor", cm.sV)
    kb.o("tensor", "matmul", cm.pb[4][:, 0:128], cm.one_row[:], dd_row[0:1, :],
         start=True, stop=True, sem=cm.sT)
    kb.wait("scalar", cm.sT)
    kb.o("scalar", "copy", ddbc[:], cm.pb[4][:, 0:128], sem=cm.sA)

    dtraw = kb.sbt([128, TH], F32, "m_dtraw")
    clr = kb.sbt([1, 1024], F32, "m_clr")
    prow = [kb.sbt([1, 1024], F32, "m_pr") for _ in range(2)]
    dt_fm = kb.sbt([128, T], F32, "m_dtfm")
    a_fm = kb.sbt([128, T], F32, "m_afm")
    cum_fm = kb.sbt([128, T], F32, "m_cumfm")
    zer = kb.sbt([128, 128], F32, "m_zer")
    scr2 = kb.sbt([128, 128], F32, "m_scr2")
    kb.o("gpsimd", "memset", zer[:], 0.0, sem=cm.sM)
    slabs = [(s, min(512, TH - s)) for s in range(0, TH, 512)]
    if upto < 1.3:
        return
    kb.wait("gpsimd", cm.sW)
    kb.dma("gpsimd", wdense[0][:].rearrange("p (k c) -> p k c", c=128),
           io["w1t"].ap()[0].rearrange("(k p) c -> p k c", p=128), sem=cm.sW)
    wv = cm.sW.n
    kb.wait("tensor", cm.sW, wv)
    kb.wait("tensor", cm.sA)
    for si, (s0, sl) in enumerate(slabs):
        ip = 4 + si % 2
        if si >= 2:
            kb.wait("tensor", cm.sA)
        for k in range(KH):
            kb.o("tensor", "matmul", cm.pb[ip][:, 0:sl],
                 wdense[0][:, k * 128:(k + 1) * 128], xb[k][:, s0:s0 + sl],
                 start=(k == 0), stop=(k == KH - 1),
                 sem=(cm.sT if k == KH - 1 else None))
        kb.wait("scalar", cm.sT)
        kb.o("scalar", "copy", dtraw[:, s0:s0 + sl], cm.pb[ip][:, 0:sl],
             sem=cm.sA)
    if upto < 1.35:
        return
    kb.wait("vector", cm.sA)
    kb.o("vector", "tensor_scalar", dtraw[:, 0:T], dtraw[:, HALO:TH],
         dtb_sb[:, 0:1], None, ALU.add, bar=True)
    kb.o("vector", "tensor_scalar", dt_fm[:], dtraw[:, 0:T], 0.0, None,
         ALU.max, bar=True)
    cm.pub("vector", cm.sV)
    kb.wait("scalar", cm.sV)
    kb.o("scalar", "activation", a_fm[:], dtraw[:, 0:T], AF.Abs, bar=True)
    kb.o("scalar", "activation", a_fm[:], a_fm[:], AF.Exp, scale=-1.0,
         bar=True)
    kb.o("scalar", "activation", a_fm[:], a_fm[:], AF.Ln,
         bias=cm.c1(), bar=True)
    kb.o("scalar", "activation", scr2[:, 0:1], alog_sb[:], AF.Exp, sem=cm.sA)
    kb.wait("vector", cm.sA)
    kb.o("vector", "tensor_add", dt_fm[:], dt_fm[:], a_fm[:], bar=True)
    kb.o("vector", "tensor_scalar", scr2[:, 1:2], scr2[:, 0:1], -1.0, None,
         ALU.mult, bar=True)
    kb.o("vector", "tensor_scalar", a_fm[:], dt_fm[:], scr2[:, 1:2], None,
         ALU.mult, bar=True)
    cm.pub("vector", cm.sV)
    for c in range(8):
        kb.o("vector", "tensor_tensor_scan", cum_fm[:, c * 128:(c + 1) * 128],
             a_fm[:, c * 128:(c + 1) * 128], zer[:], 0.0, ALU.add, ALU.add,
             sem=(cm.sV if c == 7 else None))
    kb.wait("tensor", cm.sV)
    tts = [(a_fm, a_tok), (dt_fm, dt_tok), (cum_fm, None)]
    cum_tok = kb.sbt([128, T], F32, "m_cumtok")
    tts[2] = (cum_fm, cum_tok)
    for src, dst in tts:
        for c in range(8):
            kb.wait("tensor", cm.sA)
            kb.o("tensor", "transpose", cm.pb[4][:, 0:128],
                 src[:, c * 128:(c + 1) * 128], idf_sb[:], sem=cm.sT)
            kb.wait("scalar", cm.sT)
            kb.o("scalar", "copy", dst[:, c * 128:(c + 1) * 128],
                 cm.pb[4][:, 0:128], sem=cm.sA)
    if upto < 1.4:
        return
    kb.wait("gpsimd", cm.sA)
    kb.dma("gpsimd", clr[:], cum_tok[127:128, :], sem=cm.sG)
    kb.wait("scalar", cm.sG)
    kb.o("scalar", "activation", dtot_row[0:1, :], clr[0:1, :], AF.Exp,
         sem=cm.sA)
    for si in range(2):
        kb.wait("tensor", cm.sA)
        kb.o("tensor", "matmul", cm.pb[6 + si][:], cm.one_row[:],
             dtot_row[0:1, si * 512:(si + 1) * 512], start=True, stop=True,
             sem=cm.sT)
        kb.wait("scalar", cm.sT)
        kb.o("scalar", "copy", dtotbc[:, si * 512:(si + 1) * 512],
             cm.pb[6 + si][:], sem=cm.sA)
    kb.wait("vector", cm.sG)
    kb.o("vector", "tensor_scalar", prow[0][0:1, :], clr[0:1, :], 1.0, None,
         ALU.mult, bar=True)
    src_i = 0
    for sh in (128, 256, 512):
        dst_i = 1 - src_i
        kb.o("vector", "tensor_scalar", prow[dst_i][0:1, 0:sh],
             prow[src_i][0:1, 0:sh], 1.0, None, ALU.mult)
        kb.o("vector", "tensor_add", prow[dst_i][0:1, sh:1024],
             prow[src_i][0:1, sh:1024], prow[src_i][0:1, 0:1024 - sh],
             bar=True)
        src_i = dst_i
    kb.o("vector", "tensor_sub", prow[1 - src_i][0:1, :], prow[src_i][0:1, :],
         clr[0:1, :], bar=True)
    cm.pub("vector", cm.sV)
    kb.wait("scalar", cm.sV)
    kb.o("scalar", "activation", pc_row[0:1, :], prow[1 - src_i][0:1, :],
         AF.Exp)
    if upto < 1.45:
        return
    kb.o("scalar", "activation", sd_tok[:], cum_tok[:], AF.Exp, sem=cm.sA)
    for c in range(8):
        kb.wait("vector", cm.sA)
        kb.o("vector", "tensor_scalar", scr2[:], cum_fm[:, c * 128:(c + 1) * 128],
             cum_fm[:, c * 128 + 127:c * 128 + 128], None, ALU.subtract,
             sem=cm.sV)
        kb.wait("scalar", cm.sV)
        kb.o("scalar", "activation", scr2[:], scr2[:], AF.Exp, scale=-1.0,
             sem=cm.sA)
        kb.wait("tensor", cm.sA)
        if c > 0:
            kb.wait("tensor", cm.sV)
        kb.o("tensor", "transpose", cm.pb[4][:, 128:256], scr2[:], idf_sb[:],
             sem=cm.sT)
        kb.wait("vector", cm.sT)
        kb.o("vector", "tensor_copy", dtds2[:, c * 128:(c + 1) * 128],
             cm.pb[4][:, 128:256], sem=cm.sV)
    kb.wait("vector", cm.sA)
    kb.o("vector", "tensor_mul", dtds2[:], dtds2[:], dt_tok[:], sem=cm.sV)
    if upto < 1.47:
        return
    kb.o("vector", "memset", aczero[:], 0.0, bar=True)
    for c in range(8):
        kb.o("vector", "tensor_tensor", aczero[:, 0:1], aczero[:, 0:1],
             cum_fm[:, c * 128 + 127:c * 128 + 128], ALU.add, bar=True)
    cm.pub("vector", cm.sV)
    kb.wait("gpsimd", cm.sV)
    kb.dma("gpsimd", aga_in.ap(), aczero[:], sem=cm.sG)
    kb.wait("vector", cm.sA)
    kb.free(cum_tok, scr2, zer, cum_fm, a_fm, dt_fm, *prow, clr, dtraw)
    if upto < 1.5:
        return

    kb.o("vector", "memset", S_f[:], 0.0)
    kb.o("vector", "memset", S_bf[:], 0.0, sem=cm.sV)
    sgf = [kb.sbt([128, TH], BF16, "m_sgf") for _ in range(8)]
    convin = kb.sbt([128, TH], F32, "m_cvin")
    acc = kb.sbt([128, T], F32, "m_acc")
    xc = [kb.sbt([128, T], BF16, "m_xc") for _ in range(8)]
    B_fm = kb.sbt([128, T], BF16, "m_bfm")
    xtok = kb.sbt([128, 1024], BF16, "m_xtok")
    btok = kb.sbt([128, 128], BF16, "m_btok")
    gtm = kb.sbt([128, 128], BF16, "m_gtm")
    rhsl = kb.sbt([128, 128], BF16, "m_rhsl")
    expd = kb.sbt([128, 128], BF16, "m_expd")
    gmlt = kb.sbt([128, 128], BF16, "m_gmlt")
    y1 = kb.sbt([128, 1024], F32, "m_y1")
    xw = kb.sbt([128, 1024], BF16, "m_xw")
    sgt_sb = kb.sbt([128, 1024], BF16, "m_sgts")
    sgm = kb.sbt([128, T], BF16, "m_sgm")

    sEvm = kb.sem("m_sEvm")
    state = {"cvk": None}

    for g in range(8 if upto >= 1.55 else 1):
        o_order = [1 + g * 18 + i for i in range(18)]

        def evac_m(j, o, s0, sl, ps, g=g):
            i = (o - 1) % 18
            si = j % 3
            kb.wait("scalar", sEvm)
            if upto < 1.65:
                kb.wait("scalar", cm.sT)
                kb.o("scalar", "copy", convin[:, s0:s0 + sl], ps, sem=sEvm)
                return
            if i < 8:
                kb.wait("scalar", cm.sT)
                kb.o("scalar", "activation", sgm[:, 0:sl], ps, AF.Sigmoid,
                     sem=cm.sA)
                kb.wait("vector", cm.sA)
                kb.o("vector", "tensor_tensor", sgf[i][:, s0:s0 + sl], ps,
                     sgm[:, 0:sl], ALU.mult, sem=sEvm)
            else:
                kb.wait("scalar", cm.sT)
                if si == 0 and i > 8:
                    kb.wait("scalar", cm.sV)
                kb.o("scalar", "copy", convin[:, s0:s0 + sl], ps,
                     sem=(sEvm if si < 2 else None))
                if si < 2:
                    pass
                else:
                    cidx = g * 10 + (i - 8)
                    kb.o("scalar", "copy", cm.cst[0:1, 3:4], cm.cst[0:1, 3:4],
                         sem=cm.sA)
                    kb.wait("vector", cm.sA)
                    kb.o("vector", "tensor_scalar", acc[:],
                         convin[:, 1:T + 1], convw_sb[:, cidx * 4:cidx * 4 + 1],
                         None, ALU.mult)
                    for kk in range(1, 4):
                        kb.o("vector", "scalar_tensor_tensor", acc[:],
                             convin[:, 1 + kk:T + 1 + kk],
                             convw_sb[:, cidx * 4 + kk:cidx * 4 + kk + 1],
                             acc[:], ALU.mult, ALU.add,
                             bar=(kk == 3))
                    cm.pub("vector", cm.sV)
                    kb.wait("scalar", cm.sV)
                    kb.o("scalar", "activation", sgm[:, 0:T], acc[:],
                         AF.Sigmoid, bias=convb_sb[:, cidx:cidx + 1],
                         bar=True)
                    cm.pub("scalar", cm.sA)
                    kb.wait("vector", cm.sA)
                    dst = (xc[i - 8] if i < 16 else
                           (B_fm if i == 16 else C_all))
                    dap = dst[:, 0:T] if i < 17 else dst[:, g * 1024:(g + 1) * 1024]
                    kb.o("vector", "scalar_tensor_tensor", dap, acc[:],
                         convb_sb[:, cidx:cidx + 1], sgm[:, 0:T], ALU.add,
                         ALU.mult, sem=sEvm)

        kb.wait("tensor", cm.sV)
        kb.wait("gpsimd", cm.sA)
        emit_dense(kb, cm, sEvm, io["w1t"], KH, 145,
                   lambda k, s0, sl: xb[k][:, s0:s0 + sl],
                   (1024 if upto < 1.63 else TH), evac_m,
                   o_order=o_order, dbanks=(4, 5), wb=wdense)
        for eng in ("tensor", "vector", "scalar", "gpsimd"):
            kb.wait(eng, sEvm)
            kb.wait(eng, cm.sV)
        if upto < 1.8:
            continue
        for c in range(8):
            kb.wait("scalar", cm.sG, cm.sG.n)
            for k in range(8):
                kb.wait("tensor", cm.sA)
                kb.o("tensor", "transpose",
                     cm.pb[4][:, 64 * (k % 2):64 * (k % 2) + 64].bitcast(BF16),
                     sgf[k][:, HALO + c * 128:HALO + (c + 1) * 128], idb_sb[:],
                     sem=cm.sT)
                kb.wait("scalar", cm.sT)
                kb.o("scalar", "copy", sgt_sb[:, k * 128:(k + 1) * 128],
                     cm.pb[4][:, 64 * (k % 2):64 * (k % 2) + 64].bitcast(BF16),
                     sem=cm.sA)
            kb.wait("gpsimd", cm.sA)
            kb.dma("gpsimd", sgt.ap()[g * 8 + c], sgt_sb[:], sem=cm.sG)

        if upto < 1.9:
            continue
        for c in range(8):
            kb.wait("tensor", cm.sV)
            for k in range(8):
                kb.wait("tensor", cm.sA)
                kb.o("tensor", "transpose",
                     cm.pb[4][:, 64 * (k % 2):64 * (k % 2) + 64].bitcast(BF16),
                     xc[k][:, c * 128:(c + 1) * 128], idb_sb[:], sem=cm.sT)
                kb.wait("scalar", cm.sT)
                kb.o("scalar", "copy", xtok[:, k * 128:(k + 1) * 128],
                     cm.pb[4][:, 64 * (k % 2):64 * (k % 2) + 64].bitcast(BF16),
                     sem=cm.sA)
            kb.wait("tensor", cm.sA)
            kb.o("tensor", "transpose", cm.pb[4][:, 0:64].bitcast(BF16),
                 B_fm[:, c * 128:(c + 1) * 128], idb_sb[:], sem=cm.sT)
            kb.wait("scalar", cm.sT)
            kb.o("scalar", "copy", btok[:], cm.pb[4][:, 0:64].bitcast(BF16),
                 sem=cm.sA)
            if upto < 1.92:
                continue
            kb.wait("tensor", cm.sA)
            kb.o("tensor", "matmul", cm.pb[5][:, 0:128],
                 B_fm[:, c * 128:(c + 1) * 128],
                 C_all[:, g * 1024 + c * 128:g * 1024 + (c + 1) * 128],
                 start=True, stop=True, sem=cm.sT)
            kb.wait("vector", cm.sT)
            kb.o("vector", "tensor_tensor", gtm[:], cm.pb[5][:, 0:128],
                 maskui_bf[:], ALU.mult, bar=True)
            cm.pub("vector", cm.sV)
            for h in range(16):
                hg = g * 16 + h
                kb.o("vector", "tensor_scalar", rhsl[:], maskui_bf[:],
                     a_tok[:, c * 128 + hg:c * 128 + hg + 1], None, ALU.mult,
                     bar=True)
                cm.pub("vector", cm.sV)
                kb.wait("tensor", cm.sV)
                if h > 0:
                    kb.wait("tensor", cm.sA)
                kb.o("tensor", "matmul", cm.pb[5][:, 128:256], upper_sb[:],
                     rhsl[:], start=True, stop=True, sem=cm.sT)
                kb.wait("scalar", cm.sT)
                kb.o("scalar", "activation", expd[:], cm.pb[5][:, 128:256],
                     AF.Exp, bar=True)
                cm.pub("scalar", cm.sA)
                kb.wait("vector", cm.sA)
                kb.o("vector", "scalar_tensor_tensor", gmlt[:], expd[:],
                     dt_tok[:, c * 128 + hg:c * 128 + hg + 1], gtm[:],
                     ALU.mult, ALU.mult, bar=True)
                cm.pub("vector", cm.sV)
                if upto < 1.94:
                    continue
                kb.wait("tensor", cm.sV)
                kb.o("tensor", "matmul",
                     cm.tg[0][:, h * 64:(h + 1) * 64],
                     gmlt[:], xtok[:, h * 64:(h + 1) * 64],
                     start=(h == 0 or h == 8), stop=(h == 7 or h == 15),
                     sem=(cm.sT if h in (7, 15) else None))
                kb.o("tensor", "matmul",
                     cm.tg[1][:, h * 64:(h + 1) * 64],
                     C_all[:, g * 1024 + c * 128:g * 1024 + (c + 1) * 128],
                     S_bf[:, h * 64:(h + 1) * 64],
                     start=(h == 0 or h == 8), stop=(h == 7 or h == 15),
                     sem=(cm.sT if h in (7, 15) else None))
            if upto < 1.96:
                continue
            kb.wait("vector", cm.sT)
            kb.wait("vector", cm.sG, cm.sG.n)
            for h in range(16):
                hg = g * 16 + h
                kb.o("vector", "tensor_scalar", y1[:, h * 64:(h + 1) * 64],
                     cm.tg[1][:, h * 64:(h + 1) * 64],
                     sd_tok[:, c * 128 + hg:c * 128 + hg + 1], None, ALU.mult)
            kb.o("vector", "tensor_tensor", y1[:], y1[:], cm.tg[0][:, 0:1024],
                 ALU.add, bar=True)
            for h in range(16):
                hg = g * 16 + h
                kb.o("vector", "scalar_tensor_tensor", y1[:, h * 64:(h + 1) * 64],
                     xtok[:, h * 64:(h + 1) * 64], ddbc[:, hg:hg + 1],
                     y1[:, h * 64:(h + 1) * 64], ALU.mult, ALU.add,
                     bar=(h == 15))
            cm.pub("vector", cm.sV)
            kb.wait("gpsimd", cm.sV)
            kb.dma("gpsimd", y3loc.ap()[g * 8 + c], y1[:], sem=cm.sG)
            if upto < 1.98:
                continue
            for h in range(16):
                hg = g * 16 + h
                kb.o("vector", "tensor_scalar", xw[:, h * 64:(h + 1) * 64],
                     xtok[:, h * 64:(h + 1) * 64],
                     dtds2[:, c * 128 + hg:c * 128 + hg + 1], None, ALU.mult,
                     bar=(h == 15))
            cm.pub("vector", cm.sV)
            kb.wait("tensor", cm.sV)
            for h in range(16):
                kb.o("tensor", "matmul",
                     cm.tg[4][:, h * 64:(h + 1) * 64],
                     btok[:], xw[:, h * 64:(h + 1) * 64],
                     start=(h == 0 or h == 8), stop=(h == 7 or h == 15),
                     sem=(cm.sT if h in (7, 15) else None))
            if upto < 1.99:
                continue
            kb.wait("vector", cm.sT)
            for h in range(16):
                hg = g * 16 + h
                kb.o("vector", "tensor_scalar", S_f[:, h * 64:(h + 1) * 64],
                     S_f[:, h * 64:(h + 1) * 64],
                     dtotbc[:, c * 128 + hg:c * 128 + hg + 1], None, ALU.mult,
                     bar=(h == 15))
            kb.o("vector", "tensor_tensor", S_f[:], S_f[:],
                 cm.tg[4][:, 0:1024], ALU.add, bar=True)
            kb.o("vector", "tensor_copy", S_bf[:], S_f[:], bar=True)
            cm.pub("vector", cm.sV)
        if upto < 1.995:
            continue
        kb.wait("gpsimd", cm.sV)
        kb.dma("gpsimd", ag_in.ap()[:, g * 1024:(g + 1) * 1024], S_f[:],
               sem=cm.sG)
        if upto < 1.997:
            continue
        kb.wait("vector", cm.sG)
        kb.o("vector", "memset", S_f[:], 0.0)
        kb.o("vector", "memset", S_bf[:], 0.0, sem=cm.sV)
    kb.free(*sgf, convin, acc, *xc, B_fm, xtok, btok, gtm, rhsl, expd, gmlt,
            xw, sgt_sb, sgm, y1)
    kb.free(*xb, *wdense)
    if upto < 3:
        return

    kb.wait("gpsimd", cm.sG)
    kb.wait("gpsimd", cm.sV)
    kb.cc("AllGather", ALU.bypass, [[0, 1, 2, 3], [4, 5, 6, 7]],
          [ag_in.ap().opt()], [ag_out.ap().opt()], cm.sCC)
    kb.cc("AllGather", ALU.bypass, [[0, 1, 2, 3], [4, 5, 6, 7]],
          [aga_in.ap().opt()], [aga_out.ap().opt()], cm.sCC)

    y1 = kb.sbt([128, 1024], F32, "m_y1b")
    acr = kb.sbt([128, 4], F32, "m_acr")
    wl = kb.sbt([128, 4], F32, "m_wl")
    wsr = kb.sbt([1, 512], F32, "m_wsr")
    wsrb = kb.sbt([128, 512], F32, "m_wsrb")
    srg = kb.sbt([128, 1024], F32, "m_srg")
    S_in = kb.sbt([128, 1024], F32, "m_sin")
    ylb = kb.sbt([128, 1024], F32, "m_ylb")
    sgtb = kb.sbt([128, 1024], F32, "m_sgtb")
    ssq = kb.sbt([128, 2], F32, "m_ssq")
    sqscr = kb.sbt([128, 1024], BF16, "m_sqscr")
    ygn = kb.sbt([128, 1024], BF16, "m_ygn")
    yf_sb = kb.sbt([128, 1024], BF16, "m_yfsb")
    for eng in ("tensor", "vector", "scalar", "gpsimd", "sync"):
        kb.wait(eng, cm.sCC)
    for r in range(4):
        kb.dma("sync", acr[:, r:r + 1], aga_out.ap()[r * 128:(r + 1) * 128,
                                                      0:1], sem=cm.sX,
               allow_slow_non_contiguous=True)
    kb.wait("vector", cm.sX, cm.sX.n)
    kb.o("vector", "memset", wl[:], 0.0, bar=True)
    for r in range(4):
        for m in range(4):
            kb.o("vector", "scalar_tensor_tensor", wl[:, r:r + 1],
                 acr[:, m:m + 1], msel_sb[:, 4 * r + m:4 * r + m + 1],
                 wl[:, r:r + 1], ALU.mult, ALU.add, bar=True)
    cm.pub("vector", cm.sV)
    kb.wait("scalar", cm.sV)
    kb.o("scalar", "activation", wl[:], wl[:], AF.Exp, bar=True)
    cm.pub("scalar", cm.sA)
    kb.wait("vector", cm.sA)
    kb.o("vector", "tensor_mul", wl[:], wl[:], selbc_sb[:], bar=True)
    cm.pub("vector", cm.sV)
    kb.wait("tensor", cm.sV)
    for r in range(4):
        kb.o("tensor", "transpose", cm.tg[4][0:1, r * 128:r * 128 + 128],
             wl[:, r:r + 1], idf_sb[:], sem=(cm.sT if r == 3 else None))
    kb.wait("scalar", cm.sT)
    kb.o("scalar", "copy", wsr[0:1, :], cm.tg[4][0:1, 0:512], sem=cm.sA)
    cm.pub("scalar", cm.sA)
    kb.wait("tensor", cm.sA)
    kb.o("tensor", "matmul", cm.pb[5][:], cm.one_row[:], wsr[0:1, :],
         start=True, stop=True, sem=cm.sT)
    kb.wait("scalar", cm.sT)
    kb.o("scalar", "copy", wsrb[:], cm.pb[5][:], sem=cm.sA)
    for si in range(2):
        kb.wait("tensor", cm.sA)
        kb.wait("tensor", cm.sV)
        kb.o("tensor", "matmul", cm.pb[4 + si][:], cm.one_row[:],
             pc_row[0:1, si * 512:(si + 1) * 512], start=True, stop=True,
             sem=cm.sT)
        kb.wait("vector", cm.sT)
        kb.o("vector", "tensor_tensor", sdp_tok[:, si * 512:(si + 1) * 512],
             sd_tok[:, si * 512:(si + 1) * 512],
             cm.pb[4 + si][:], ALU.mult, sem=cm.sV, bar=True)
    cm.pub("vector", cm.sV)

    sldy = {}
    for g in range(8):
        for r in range(4):
            kb.wait("sync", cm.sV, cm.sV.n)
            kb.dma("sync", srg[:], ag_out.ap()[r * 128:(r + 1) * 128,
                                               g * 1024:(g + 1) * 1024],
                   sem=cm.sX)
            kb.wait("vector", cm.sX, cm.sX.n)
            for h in range(16):
                hg = g * 16 + h
                col = wsrb[:, r * 128 + hg:r * 128 + hg + 1]
                if r == 0:
                    kb.o("vector", "tensor_scalar",
                         S_in[:, h * 64:(h + 1) * 64],
                         srg[:, h * 64:(h + 1) * 64], col, None, ALU.mult,
                         bar=(h == 15))
                else:
                    kb.o("vector", "scalar_tensor_tensor",
                         S_in[:, h * 64:(h + 1) * 64],
                         srg[:, h * 64:(h + 1) * 64], col,
                         S_in[:, h * 64:(h + 1) * 64], ALU.mult, ALU.add,
                         bar=(h == 15))
            cm.pub("vector", cm.sV)
        kb.o("vector", "tensor_copy", S_bf[:], S_in[:], bar=True)
        cm.pub("vector", cm.sV)
        for c in range(8):
            gc = g * 8 + c
            kb.wait("gpsimd", cm.sV, cm.sV.n)
            kb.dma("gpsimd", ylb[:], y3loc.ap()[gc], sem=cm.sW)
            kb.dma("gpsimd", sgtb[:], sgt.ap()[gc], sem=cm.sW)
            wv3 = cm.sW.n
            kb.wait("tensor", cm.sV)
            for h in range(16):
                kb.o("tensor", "matmul", cm.tg[0][:, h * 64:(h + 1) * 64],
                     C_all[:, g * 1024 + c * 128:g * 1024 + (c + 1) * 128],
                     S_bf[:, h * 64:(h + 1) * 64],
                     start=(h == 0 or h == 8), stop=(h == 7 or h == 15),
                     sem=(cm.sT if h in (7, 15) else None))
            kb.wait("vector", cm.sT)
            kb.wait("vector", cm.sW, wv3)
            for h in range(16):
                hg = g * 16 + h
                kb.o("vector", "tensor_scalar", y1[:, h * 64:(h + 1) * 64],
                     cm.tg[0][:, h * 64:(h + 1) * 64],
                     sdp_tok[:, c * 128 + hg:c * 128 + hg + 1], None,
                     ALU.mult)
            kb.o("vector", "tensor_add", y1[:], y1[:], ylb[:], bar=True)
            kb.o("vector", "tensor_mul", y1[:], y1[:], sgtb[:], bar=True)
            cm.pub("vector", cm.sV)
            kb.wait("scalar", cm.sV)
            kb.o("scalar", "activation", sqscr[:], y1[:], AF.Square,
                 accum_out=ssq[:, 0:1], bar=True)
            cm.pub("scalar", cm.sA)
            kb.wait("vector", cm.sA)
            kb.o("vector", "tensor_scalar", ssq[:, 0:1], ssq[:, 0:1],
                 1.0 / 1024.0, 1e-5, ALU.mult, ALU.add, bar=True)
            cm.pub("vector", cm.sV)
            kb.wait("scalar", cm.sV)
            kb.o("scalar", "activation", ssq[:, 1:2], ssq[:, 0:1], AF.Sqrt,
                 bar=True)
            cm.pub("scalar", cm.sA)
            kb.wait("vector", cm.sA)
            kb.o("vector", "reciprocal", ssq[:, 0:1], ssq[:, 1:2], bar=True)
            kb.o("vector", "tensor_scalar", ygn[:], y1[:], ssq[:, 0:1], None,
                 ALU.mult, bar=True)
            cm.pub("vector", cm.sV)
            kb.wait("tensor", cm.sV)
            kb.wait("vector", cm.sG, cm.sG.n)
            for k in range(8):
                kb.wait("tensor", cm.sV)
                kb.o("tensor", "transpose",
                     cm.pb[4][:, 64 * (k % 2):64 * (k % 2) + 64].bitcast(BF16),
                     ygn[:, k * 128:(k + 1) * 128], idb_sb[:], sem=cm.sT)
                kb.wait("vector", cm.sT)
                kb.o("vector", "tensor_scalar", yf_sb[:, k * 128:(k + 1) * 128],
                     cm.pb[4][:, 64 * (k % 2):64 * (k % 2) + 64].bitcast(BF16),
                     normw_sb[:, g * 8 + k:g * 8 + k + 1], None, ALU.mult,
                     sem=cm.sV)
            kb.wait("gpsimd", cm.sV)
            kb.dma("gpsimd", yfd.ap().rearrange("o p t -> p o t")
                   [:, g * 8:(g + 1) * 8, c * 128:(c + 1) * 128],
                   yf_sb[:].rearrange("p (k t) -> p k t", t=128), sem=cm.sG)
    kb.free(acr, wl, wsr, wsrb, srg, S_in, ylb, sgtb, ssq, sqscr, ygn,
            yf_sb, y1)
    kb.free(convw_sb, convb_sb, dtb_sb, alog_sb, dd_row, normw_sb, upper_sb,
            maskui_bf, idf_sb, idb_sb, msel_sb, selbc_sb, ddbc,
            dt_tok, a_tok, sd_tok, dtds2, sdp_tok, dtot_row, dtotbc, pc_row,
            aczero, C_all, S_f, S_bf)
    if upto < 4:
        return

    ySB = [kb.sbt([128, 1024], BF16, "m_ysb") for _ in range(64)]
    xres = [kb.sbt([128, 512], F32, "m_xres") for _ in range(4)]
    xo = [kb.sbt([128, 512], F32, "m_xo") for _ in range(2)]
    kb.wait("gpsimd", cm.sG)
    for kk in range(64):
        kb.dma("gpsimd", ySB[kk][:], yfd.ap()[kk], sem=cm.sW)
    ldw3 = cm.sW.n
    kb.wait("tensor", cm.sW, ldw3)
    sEvo = kb.sem("m_sEvo")
    xb2 = cm.sX.n
    for jj in range(3):
        of, sf = divmod(jj, 2)
        kb.dma("sync", xres[jj % 4][:],
               io["xT"].ap()[of][:, HALO + sf * 512:HALO + (sf + 1) * 512],
               sem=cm.sX)

    def evac_o(j, o, s0, sl, ps):
        jf = j + 3
        if jf < 64:
            of, sf = divmod(jf, 2)
            if j > 0:
                kb.wait("sync", sEvo, j)
            kb.dma("sync", xres[jf % 4][:],
                   io["xT"].ap()[of][:, HALO + sf * 512:HALO + (sf + 1) * 512],
                   sem=cm.sX)
        kb.wait("vector", cm.sT)
        kb.wait("vector", cm.sX, xb2 + 16 * (j + 1))
        if j >= 2:
            kb.wait("vector", cm.sS, 16 * (j - 1))
        kb.o("vector", "tensor_tensor", xo[j % 2][:, 0:sl], ps,
             xres[j % 4][:, 0:sl], ALU.add, sem=sEvo)
        kb.wait("sync", sEvo, sEvo.n)
        kb.dma("sync", io["xout"].ap()[o][:, s0:s0 + sl], xo[j % 2][:, 0:sl],
               sem=cm.sS)

    emit_dense(kb, cm, sEvo, io["opw"], 64, KH,
               lambda k, s0, sl: ySB[k][:, s0:s0 + sl], T, evac_o,
               dbanks=(4, 5))


def host_prep_l0attn(inputs, x):
    W1 = np.ascontiguousarray(inputs["L0_attn_in_proj_w"].T)
    w1tiles = tile_w_omajor(W1)
    order = [144]
    for g in range(8):
        order += [g * 8 + j for j in range(8)]
        order += [64 + g * 8 + j for j in range(8)]
        order += [128 + g, 136 + g]
    w1t = np.ascontiguousarray(w1tiles[order])
    cw = inputs["L0_attn_conv_w"][:, 0, :]
    cb = inputs["L0_attn_conv_b"]
    chan = []
    for g in range(8):
        chan += list(range(g * 1024, (g + 1) * 1024))
        chan += list(range(8192 + g * 128, 8192 + (g + 1) * 128))
        chan += list(range(9216 + g * 128, 9216 + (g + 1) * 128))
    chan = np.array(chan)
    cwr = cw[chan].reshape(80, 128, 4)
    convw_t = np.ascontiguousarray(cwr.transpose(1, 0, 2).reshape(128, 320))
    convb_t = np.ascontiguousarray(cb[chan].reshape(80, 128).T)
    dtb_t = inputs["L0_attn_dt_bias"].reshape(128, 1).astype(np.float32)
    alog_t = inputs["L0_attn_A_log"].reshape(128, 1).astype(np.float32)
    dd_t = inputs["L0_attn_D"].reshape(1, 128).astype(np.float32)
    normw_t = col_layout(inputs["L0_attn_norm_w"])
    opw_t = tile_w_omajor(np.ascontiguousarray(inputs["L0_attn_out_proj_w"].T))
    m_idx = np.arange(128)
    upper_t = (m_idx[:, None] > m_idx[None, :]).astype(np.float32)
    maskui_t = (m_idx[None, :] >= m_idx[:, None]).astype(np.float32)
    idf_t = np.eye(128, dtype=np.float32)
    maps = []
    for c in range(NCORES):
        b, krank = divmod(c, 4)
        t0 = krank * T
        sl = x[b, max(0, t0 - HALO):t0 + T, :]
        if t0 == 0:
            sl = np.concatenate([np.zeros((HALO, H), np.float32), sl], 0)
        xT_t = np.ascontiguousarray(sl.T).reshape(KH, 128, TH)
        mselm = np.zeros((4, 4), np.float32)
        selv = np.zeros(4, np.float32)
        for r in range(4):
            if r < krank:
                selv[r] = 1.0
                for m in range(r + 1, krank):
                    mselm[r, m] = 1.0
        msel_t = np.broadcast_to(mselm.reshape(1, 16), (128, 16)).astype(np.float32)
        selbc_t = np.broadcast_to(selv.reshape(1, 4), (128, 4)).astype(np.float32)
        maps.append({
            "xT": xT_t, "w1t": w1t, "convw": convw_t, "convb": convb_t,
            "dtb": dtb_t, "alog": alog_t, "DD": dd_t, "normw": normw_t,
            "opw": opw_t, "upper": upper_t, "maskui": maskui_t, "idf": idf_t,
            "msel": np.ascontiguousarray(msel_t),
            "selbc": np.ascontiguousarray(selbc_t),
        })
    return maps


# version 02
# speedup vs baseline: 1.0892x; 1.0892x over previous
import numpy as np
import ml_dtypes
NO_PSL = False
PH4 = 9
DETECT_RACES = True
from contextlib import ExitStack

import concourse.bass as bass
import concourse.mybir as mybir
from concourse.bass_utils import run_bass_kernel_spmd

F32 = mybir.dt.float32
BF16 = mybir.dt.bfloat16
AF = mybir.ActivationFunctionType
ALU = mybir.AluOpType
AX = mybir.AxisListType.X

NCORES = 8
T = 1024
HALO = 4
TH = T + HALO
H = 4096
KH = H // 128

BF = ml_dtypes.bfloat16


def tile_w_omajor(w):
    IN, OUT = w.shape
    t = w.reshape(IN // 128, 128, OUT // 128, 128).transpose(2, 0, 1, 3)
    return np.ascontiguousarray(t.reshape(OUT // 128, IN, 128))


def col_layout(v):
    return np.ascontiguousarray(np.asarray(v, np.float32).reshape(-1, 128).T)


class Sem:
    __slots__ = ("h", "n")

    def __init__(self, h):
        self.h = h
        self.n = 0


class KB:
    ENGS = ("tensor", "vector", "scalar", "gpsimd", "sync")

    def __init__(self, nc):
        self.nc = nc
        self.es = ExitStack()
        self.ops = {k: [] for k in self.ENGS}
        self._names = {}
        self._bars = {}
        self._open = {}
        self._seq = 0

    def name(self, p):
        i = self._names.get(p, 0)
        self._names[p] = i + 1
        return f"{p}_{i}"

    def sem(self, p="s"):
        return Sem(self.es.enter_context(self.nc.semaphore(self.name(p))))

    def sb(self, shape, dtype, p="t"):
        return self.es.enter_context(self.nc.sbuf_tensor(self.name(p), shape, dtype))

    def sbt(self, shape, dtype, p="t"):
        g = self.nc.sbuf_tensor(self.name(p), shape, dtype, side="right")
        h = g.__enter__()
        self._seq += 1
        self._open[id(h)] = (self._seq, g)
        return h

    def free(self, *hs):
        items = sorted((self._open.pop(id(h)) for h in hs), reverse=True,
                       key=lambda x: x[0])
        for _, g in items:
            g.__exit__(None, None, None)

    def psum(self, shape, dtype=F32, p="ps"):
        return self.es.enter_context(self.nc.psum_tensor(self.name(p), shape, dtype))

    def _push(self, eng, fn):
        self.ops[eng].append(fn)

    def wait(self, eng, sem, val=None):
        v = sem.n if val is None else val
        self._push(eng, lambda e, h=sem.h, v=v: e.wait_ge(h, v))

    def o(self, eng, meth, *a, sem=None, inc=1, bar=False, **kw):
        bsem = bval = None
        if bar and sem is None:
            if eng not in self._bars:
                self._bars[eng] = self.sem(f"bar_{eng}")
            sem, inc = self._bars[eng], 1
        if sem is not None:
            sem.n += inc
            if bar:
                bsem, bval = sem, sem.n
        def f(e, meth=meth, a=a, kw=kw, sem=sem, inc=inc, bsem=bsem, bval=bval):
            r = getattr(e, meth)(*a, **kw)
            if sem is not None:
                r.then_inc(sem.h, inc)
            if bsem is not None:
                e.wait_ge(bsem.h, bval)
        self._push(eng, f)

    def dma(self, eng, out, in_, sem=None, **kw):
        if sem is not None:
            sem.n += 16
        def f(e, out=out, in_=in_, kw=kw, sem=sem):
            r = e.dma_start(out=out, in_=in_, **kw)
            if sem is not None:
                r.then_inc(sem.h, 16)
        self._push(eng, f)

    def cc(self, kind, op, groups, ins, outs, sem):
        sem.n += 1
        def f(e, kind=kind, op=op, groups=groups, ins=ins, outs=outs, sem=sem):
            e.collective_compute(kind, op, replica_groups=groups,
                                 ins=ins, outs=outs).then_inc(sem.h, 1)
        self._push("gpsimd", f)

    def finalize(self):
        ops = self.ops
        with self.nc.Block() as blk:
            @blk.tensor
            def _(e):
                for f in ops["tensor"]:
                    f(e)

            @blk.vector
            def _(e):
                for f in ops["vector"]:
                    f(e)

            @blk.scalar
            def _(e):
                for f in ops["scalar"]:
                    f(e)

            @blk.gpsimd
            def _(e):
                for f in ops["gpsimd"]:
                    f(e)

            @blk.sync
            def _(e):
                for f in ops["sync"]:
                    f(e)
        for _, g in sorted(self._open.values(), reverse=True,
                           key=lambda x: x[0]):
            g.__exit__(None, None, None)
        self._open.clear()
        self.es.close()
        return self.nc


class Common:

    def __init__(self, kb, groups=(1, 1, 1, 1, 1, 1, 1, 1)):
        self.kb = kb
        self.sW = kb.sem("sW")
        self.sX = kb.sem("sX")
        self.sT = kb.sem("sT")
        self.sV = kb.sem("sV")
        self.sA = kb.sem("sA")
        self.sS = kb.sem("sS")
        self.sCC = kb.sem("sCC")
        self.sG = kb.sem("sG")
        self.sM = kb.sem("sM")
        assert sum(groups) == 8
        self.tg = [kb.psum([128, 512 * gsz], F32, f"ps{i}")
                   for i, gsz in enumerate(groups)]
        self.pb = []
        for t, gsz in zip(self.tg, groups):
            for i in range(gsz):
                self.pb.append(t[:, i * 512:(i + 1) * 512])
        self.ones_col_bf = kb.sb([128, 1], BF16, "ones_col_bf")
        self.one_row = kb.sb([1, 128], F32, "one_row")
        self.cst = kb.sb([128, 4], F32, "cst")
        kb.o("gpsimd", "memset", self.ones_col_bf[:], 1.0)
        kb.o("gpsimd", "memset", self.one_row[:], 1.0)
        kb.o("gpsimd", "memset", self.cst[:, 0:1], 1.0)
        kb.o("gpsimd", "memset", self.cst[:, 1:2], 1e-5)
        kb.o("gpsimd", "memset", self.cst[:, 2:3], -1.0)
        kb.o("gpsimd", "memset", self.cst[:, 3:4], 0.0, sem=self.sM)

    def pub(self, eng, sem, n=1):
        meth = "mul" if eng == "scalar" else "tensor_scalar_mul"
        for i in range(n):
            self.kb.o(eng, meth, self.cst[:, 3:4], self.cst[:, 3:4], 1.0,
                      sem=(sem if i == n - 1 else None), bar=True)

    def c1(self):
        return self.cst[:, 0:1]

    def ceps(self):
        return self.cst[:, 1:2]


def emit_ln(kb, cm, xb, n_tok, mb_sb, rb_sb, banks=(0, 1, 2, 3),
            keep_rows=False):
    nk = len(xb)
    slabs = [(s, min(512, n_tok - s)) for s in range(0, n_tok, 512)]
    mrow = kb.sb([1, n_tok], F32, "ln_mrow") if keep_rows else         kb.sbt([1, n_tok], F32, "ln_mrow")
    vrow = kb.sb([1, n_tok], F32, "ln_vrow") if keep_rows else         kb.sbt([1, n_tok], F32, "ln_vrow")
    sq = [kb.sbt([128, 512], BF16, "ln_sq") for _ in range(2)]
    m2 = kb.sbt([1, 512], F32, "ln_m2")
    rrow = kb.sbt([1, n_tok], F32, "ln_rrow")
    ps_a = cm.pb[banks[0]]
    ps_b = cm.pb[banks[1]]
    ps_bc = cm.pb[banks[2]]
    ps_bc2 = cm.pb[banks[3]]
    tvals = {}
    prev_stats_v = 0
    for si, (s0, sl) in enumerate(slabs):
        for k in range(nk):
            bq = k % 2
            pk = (si, k - 2) if k >= 2 else ((si - 1, nk - 2 + k) if si > 0 else None)
            if pk is not None:
                kb.wait("scalar", cm.sT, tvals[pk])
            kb.o("scalar", "activation", sq[bq][:, 0:sl], xb[k][:, s0:s0 + sl],
                 AF.Square, sem=cm.sA)
            kb.wait("tensor", cm.sA)
            if k == 0 and si > 0:
                kb.wait("tensor", cm.sV, prev_stats_v)
            kb.o("tensor", "matmul", ps_a[0:1, 0:sl], cm.ones_col_bf[:],
                 xb[k][:, s0:s0 + sl], start=(k == 0), stop=(k == nk - 1))
            kb.o("tensor", "matmul", ps_b[0:1, 0:sl], cm.ones_col_bf[:],
                 sq[bq][:, 0:sl], start=(k == 0), stop=(k == nk - 1), sem=cm.sT)
            tvals[(si, k)] = cm.sT.n
        kb.wait("vector", cm.sT)
        kb.o("vector", "tensor_scalar", mrow[0:1, s0:s0 + sl], ps_a[0:1, 0:sl],
             1.0 / H, None, ALU.mult, sem=cm.sV)
        kb.wait("scalar", cm.sV)
        kb.o("scalar", "activation", m2[0:1, 0:sl], mrow[0:1, s0:s0 + sl],
             AF.Square, sem=cm.sA)
        kb.wait("vector", cm.sA)
        kb.o("vector", "scalar_tensor_tensor", vrow[0:1, s0:s0 + sl],
             ps_b[0:1, 0:sl], 1.0 / H, m2[0:1, 0:sl], ALU.mult, ALU.subtract,
             sem=cm.sV)
        prev_stats_v = cm.sV.n
        kb.wait("scalar", cm.sV)
        kb.o("scalar", "activation", rrow[0:1, s0:s0 + sl],
             vrow[0:1, s0:s0 + sl], AF.Sqrt, bias=cm.ceps()[0:1, :], sem=cm.sA)
        kb.wait("vector", cm.sA)
        kb.o("vector", "reciprocal", rrow[0:1, s0:s0 + sl],
             rrow[0:1, s0:s0 + sl], sem=cm.sV)
        kb.wait("tensor", cm.sV)
        kb.o("tensor", "matmul", ps_bc[:, 0:sl], cm.one_row[:],
             mrow[0:1, s0:s0 + sl], start=True, stop=True)
        kb.o("tensor", "matmul", ps_bc2[:, 0:sl], cm.one_row[:],
             rrow[0:1, s0:s0 + sl], start=True, stop=True, sem=cm.sT)
        kb.wait("scalar", cm.sT)
        kb.o("scalar", "copy", mb_sb[:, s0:s0 + sl], ps_bc[:, 0:sl])
        kb.o("scalar", "copy", rb_sb[:, s0:s0 + sl], ps_bc2[:, 0:sl], sem=cm.sA)
        kb.wait("vector", cm.sA)
        for k in range(nk):
            kb.o("vector", "tensor_sub", xb[k][:, s0:s0 + sl],
                 xb[k][:, s0:s0 + sl], mb_sb[:, s0:s0 + sl])
            kb.o("vector", "tensor_mul", xb[k][:, s0:s0 + sl],
                 xb[k][:, s0:s0 + sl], rb_sb[:, s0:s0 + sl],
                 sem=(cm.sV if k == nk - 1 else None))
        if si == len(slabs) - 1:
            pass
    if keep_rows:
        kb.free(*sq, m2, rrow)
    else:
        kb.free(*sq, m2, rrow, mrow, vrow)
    return mrow, vrow


def emit_dense(kb, cm, sEv, wdram, n_k, n_o, rhs_fn, n_tok, evac_fn,
               lora=None, o_order=None, dbanks=(0, 1), nbuf=2, wb=None):
    slabs = [(s, min(512, n_tok - s)) for s in range(0, n_tok, 512)]
    ns = len(slabs)
    if wb is None:
        wb = [kb.sb([128, n_k * 128], BF16, "dw") for _ in range(nbuf)]
    pss = [cm.pb[dbanks[0]], cm.pb[dbanks[1]]]
    o_order = list(range(n_o)) if o_order is None else o_order
    base = sEv.n
    j = 0
    for i, o in enumerate(o_list := o_order):
        b = i % nbuf
        if i == 0:
            kb.wait("gpsimd", cm.sW)
        if i >= nbuf:
            kb.wait("gpsimd", sEv, base + (i - nbuf + 1) * ns)
        kb.dma("gpsimd", wb[b][:].rearrange("p (k c) -> p k c", c=128),
               wdram.ap()[o].rearrange("(k p) c -> p k c", p=128), sem=cm.sW)
        wv = cm.sW.n
        for si, (s0, sl) in enumerate(slabs):
            ip = j % 2
            kb.wait("tensor", cm.sW, wv)
            if j >= 2:
                kb.wait("tensor", sEv, base + j - 1)
            for k in range(n_k):
                kb.o("tensor", "matmul", pss[ip][:, 0:sl],
                     wb[b][:, k * 128:(k + 1) * 128], rhs_fn(k, s0, sl),
                     start=(k == 0), stop=(k == n_k - 1 and lora is None),
                     sem=(cm.sT if (k == n_k - 1 and lora is None) else None))
            if lora is not None:
                kb.o("tensor", "matmul", pss[ip][:, 0:sl], lora[0](o),
                     lora[1](s0, sl), start=False, stop=True, sem=cm.sT)
            evac_fn(j, o, s0, sl, pss[ip][:, 0:sl])
            j += 1
    return j


def build_l0ffn(debug=False):
    nc = bass.Bass("TRN2", target_bir_lowering=False)
    xin = nc.dram_tensor("xin", [KH, 128, T], F32, kind="ExternalInput")
    wbase = nc.dram_tensor("wbase", [KH, KH * 128, 128], F32, kind="ExternalInput")
    bias = nc.dram_tensor("bias", [128, KH], F32, kind="ExternalInput")
    wr = nc.dram_tensor("wr", [KH * 128, 8], F32, kind="ExternalInput")
    wa = nc.dram_tensor("wa", [KH * 128, 128], F32, kind="ExternalInput")
    wbm = nc.dram_tensor("wbm", [32, 128, 128], F32, kind="ExternalInput")
    e16 = nc.dram_tensor("e16", [8, 128], F32, kind="ExternalInput")
    idf = nc.dram_tensor("idf", [128, 128], F32, kind="ExternalInput")
    xout = nc.dram_tensor("xout", [KH, 128, T], F32, kind="ExternalOutput")
    dbg = {}
    if debug:
        dbg["dxln"] = nc.dram_tensor("dxln", [KH, 128, T], F32, kind="ExternalOutput")
        dbg["dlg"] = nc.dram_tensor("dlg", [8, T], F32, kind="ExternalOutput")
        dbg["dwfm"] = nc.dram_tensor("dwfm", [8, T], F32, kind="ExternalOutput")
        dbg["dhw"] = nc.dram_tensor("dhw", [128, T], F32, kind="ExternalOutput")
        dbg["dwt"] = nc.dram_tensor("dwt", [128, 64], F32, kind="ExternalOutput")
        dbg["dlgt"] = nc.dram_tensor("dlgt", [128, 64], F32, kind="ExternalOutput")
        dbg["dcol"] = nc.dram_tensor("dcol", [128, 6], F32, kind="ExternalOutput")
        dbg["dstep"] = nc.dram_tensor("dstep", [128, 32], F32, kind="ExternalOutput")

    kb = KB(nc)
    cm = Common(kb)
    emit_l0ffn(kb, cm, dict(xin=xin, wbase=wbase, bias=bias, wr=wr, wa=wa,
                            wbm=wbm, e16=e16, idf=idf, xout=xout), dbg=dbg)
    kb.wait("sync", cm.sS)
    kb.finalize()
    return nc


def emit_l0ffn(kb, cm, io, dbg=None):
    xin, wbase, bias = io["xin"], io["wbase"], io["bias"]
    wr, wa, wbm, e16, idf, xout = (io["wr"], io["wa"], io["wbm"], io["e16"],
                                   io["idf"], io["xout"])

    xb = [kb.sb([128, T], BF16, "f_xb") for _ in range(KH)]
    mb = kb.sb([128, T], BF16, "f_mb")
    rb = kb.sb([128, T], BF16, "f_rb")
    wr_sb = kb.sb([128, KH * 8], BF16, "f_wr")
    wa_sb = kb.sb([128, KH * 128], BF16, "f_wa")
    wbm_sb = kb.sb([128, 32 * 128], BF16, "f_wbm")
    e16_sb = kb.sb([8, 128], BF16, "f_e16")
    idf_sb = kb.sb([128, 128], F32, "f_idf")
    bias_sb = kb.sb([128, KH], F32, "f_bias")
    lg = kb.sb([8, T], F32, "f_lg")
    hA = kb.sb([128, T], F32, "f_hA")
    hAw = kb.sb([128, T], BF16, "f_hAw")
    lgt = kb.sb([128, 8], F32, "f_lgt")
    wtok = kb.sb([128, 8], F32, "f_wtok")
    wfm = kb.sb([8, T], BF16, "f_wfm")
    scr = kb.sb([128, 8], F32, "f_scr")
    col = kb.sb([128, 6], F32, "f_col")
    xres = [kb.sb([128, 512], F32, "f_xres") for _ in range(4)]
    dbg_wt = kb.sb([128, 64], F32, "f_dbgwt")
    dbg_st = kb.sb([128, 32], F32, "f_dbgst")
    dbg_lgt = kb.sb([128, 64], F32, "f_dbglgt")
    xo = [kb.sb([128, 512], F32, "f_xo") for _ in range(2)]

    ps8 = [cm.pb[4], cm.pb[5]]
    psA = [cm.pb[6], cm.pb[7]]
    ps_t = cm.pb[4]
    ps_t2 = cm.pb[5]
    psW = cm.pb[6]

    for k in range(KH):
        kb.dma("gpsimd", xb[k][:], xin.ap()[k], sem=cm.sW)
    kb.dma("gpsimd", wr_sb[:].rearrange("p (k e) -> p k e", e=8),
           wr.ap().rearrange("(k p) e -> p k e", p=128), sem=cm.sW)
    kb.dma("gpsimd", wa_sb[:].rearrange("p (k c) -> p k c", c=128),
           wa.ap().rearrange("(k p) c -> p k c", p=128), sem=cm.sW)
    kb.dma("gpsimd", wbm_sb[:].rearrange("p (o c) -> p o c", c=128),
           wbm.ap().rearrange("o p c -> p o c"), sem=cm.sW)
    kb.dma("gpsimd", e16_sb[:], e16.ap(), sem=cm.sW)
    kb.dma("sync", idf_sb[:], idf.ap(), sem=cm.sX)
    kb.dma("sync", bias_sb[:], bias.ap(), sem=cm.sX)
    ldw, ldx = cm.sW.n, cm.sX.n
    for eng in ("tensor", "vector", "scalar"):
        kb.wait(eng, cm.sW, ldw)
        kb.wait(eng, cm.sX, ldx)
        kb.wait(eng, cm.sM)

    emit_ln(kb, cm, xb, T, mb, rb)

    if dbg:
        kb.wait("gpsimd", cm.sV)
        for k in range(KH):
            kb.dma("gpsimd", dbg["dxln"].ap()[k], xb[k][:], sem=cm.sS)

    kb.wait("tensor", cm.sV)
    for si in range(2):
        s0 = si * 512
        for k in range(KH):
            kb.o("tensor", "matmul", ps8[si][0:8, :],
                 wr_sb[:, k * 8:(k + 1) * 8], xb[k][:, s0:s0 + 512],
                 start=(k == 0), stop=(k == KH - 1),
                 sem=(cm.sT if k == KH - 1 else None))
        for k in range(KH):
            kb.o("tensor", "matmul", psA[si][:],
                 wa_sb[:, k * 128:(k + 1) * 128], xb[k][:, s0:s0 + 512],
                 start=(k == 0), stop=(k == KH - 1),
                 sem=(cm.sT if k == KH - 1 else None))
    kb.wait("scalar", cm.sT)
    for si in range(2):
        s0 = si * 512
        kb.o("scalar", "copy", lg[0:8, s0:s0 + 512], ps8[si][0:8, :])
        kb.o("scalar", "copy", hA[:, s0:s0 + 512], psA[si][:],
             sem=(cm.sA if si == 1 else None))

    for c in range(8):
        cs = slice(c * 128, (c + 1) * 128)
        kb.wait("tensor", cm.sA)
        kb.o("tensor", "transpose", ps_t[:, 0:8], lg[0:8, cs], idf_sb[0:8, 0:8],
             sem=cm.sT)
        kb.wait("scalar", cm.sT)
        kb.o("scalar", "copy", lgt[:], ps_t[:, 0:8], sem=cm.sA)
        kb.wait("vector", cm.sA)
        kb.o("vector", "reduce_max", col[:, 0:1], lgt[:], axis=AX, bar=True)
        kb.o("vector", "tensor_scalar", scr[:], lgt[:], col[:, 0:1], None,
             ALU.is_equal, bar=True)
        kb.o("vector", "scalar_tensor_tensor", scr[:], scr[:], -1e30, lgt[:],
             ALU.mult, ALU.add, bar=True)
        kb.o("vector", "reduce_max", col[:, 1:2], scr[:], axis=AX, bar=True)
        kb.o("vector", "tensor_sub", col[:, 2:3], col[:, 1:2], col[:, 0:1],
             bar=True)
        kb.o("vector", "tensor_scalar", col[:, 3:4], col[:, 0:1], -1.0, None,
             ALU.mult, sem=cm.sV, bar=True)
        kb.wait("scalar", cm.sV)
        kb.o("scalar", "activation", col[:, 5:6], col[:, 2:3], AF.Sigmoid,
             scale=-1.0, bar=True)
        kb.o("scalar", "activation", scr[:], lgt[:], AF.Exp,
             bias=col[:, 3:4], bar=True)
        cm.pub("scalar", cm.sA, n=2)
        kb.wait("vector", cm.sA)
        kb.o("vector", "tensor_scalar", wtok[:], lgt[:], col[:, 1:2], None,
             ALU.is_ge, bar=True)
        kb.o("vector", "tensor_mul", wtok[:], wtok[:], scr[:], bar=True)
        kb.o("vector", "tensor_scalar", wtok[:], wtok[:], col[:, 5:6], None,
             ALU.mult, bar=True)
        cm.pub("vector", cm.sV)
        if dbg:
            kb.o("vector", "tensor_copy", dbg_wt[:, c * 8:(c + 1) * 8], wtok[:])
            kb.o("vector", "tensor_copy", dbg_lgt[:, c * 8:(c + 1) * 8], lgt[:],
                 sem=cm.sV)
        kb.wait("tensor", cm.sV)
        kb.o("tensor", "transpose", ps_t2[0:8, 0:128], wtok[:], idf_sb[:], sem=cm.sT)
        kb.wait("scalar", cm.sT)
        kb.o("scalar", "copy", wfm[0:8, cs], ps_t2[0:8, 0:128], sem=cm.sA)

    if dbg:
        kb.wait("gpsimd", cm.sA)
        kb.dma("gpsimd", dbg["dlg"].ap(), lg[0:8, :], sem=cm.sS)

    for si in range(2):
        s0 = si * 512
        kb.wait("tensor", cm.sA)
        if si == 1:
            kb.wait("tensor", cm.sV)
        kb.o("tensor", "matmul", psW[:], e16_sb[:], wfm[0:8, s0:s0 + 512],
             start=True, stop=True, sem=cm.sT)
        kb.wait("vector", cm.sT)
        kb.o("vector", "scalar_tensor_tensor", hAw[:, s0:s0 + 512],
             hA[:, s0:s0 + 512], 2.0, psW[:], ALU.mult, ALU.mult, sem=cm.sV)

    if dbg:
        kb.wait("gpsimd", cm.sV)
        kb.dma("gpsimd", dbg["dwt"].ap(), dbg_wt[:], sem=cm.sS)
        kb.dma("gpsimd", dbg["dcol"].ap(), col[:], sem=cm.sS)
        kb.dma("gpsimd", dbg["dstep"].ap(), dbg_st[:], sem=cm.sS)
        kb.dma("gpsimd", dbg["dlgt"].ap(), dbg_lgt[:], sem=cm.sS)
        kb.dma("gpsimd", dbg["dwfm"].ap(), wfm[0:8, :], sem=cm.sS)
        kb.dma("gpsimd", dbg["dhw"].ap(), hAw[:], sem=cm.sS)

    sEv = kb.sem("f_sEv")
    for jj in range(3):
        o0, s0 = divmod(jj, 2)
        kb.dma("sync", xres[jj % 4][:], xin.ap()[o0][:, s0 * 512:(s0 + 1) * 512],
               sem=cm.sX)

    def evac(j, o, s0, sl, ps):
        jf = j + 3
        if jf < 64:
            of, sf = divmod(jf, 2)
            if j > 0:
                kb.wait("sync", sEv, j)
            kb.dma("sync", xres[jf % 4][:],
                   xin.ap()[of][:, sf * 512:(sf + 1) * 512], sem=cm.sX)
        kb.wait("vector", cm.sT)
        kb.wait("vector", cm.sX, ldx + 16 * (j + 1))
        if j >= 2:
            kb.wait("vector", cm.sS, 16 * (j - 1))
        kb.o("vector", "scalar_tensor_tensor", xo[j % 2][:, 0:sl], ps,
             bias_sb[:, o:o + 1], xres[j % 4][:, 0:sl], ALU.add, ALU.add,
             sem=sEv)
        kb.wait("sync", sEv, sEv.n)
        kb.dma("sync", xout.ap()[o][:, s0:s0 + sl], xo[j % 2][:, 0:sl],
               sem=cm.sS)

    kb.wait("tensor", cm.sV)
    emit_dense(kb, cm, sEv, wbase, KH, KH,
               lambda k, s0, sl: xb[k][:, s0:s0 + sl], T, evac,
               lora=(lambda o: wbm_sb[:, o * 128:(o + 1) * 128],
                     lambda s0, sl: hAw[:, s0:s0 + sl]), dbanks=(0, 1))


def host_prep_l0ffn(inputs, x_fm):
    wbase_t = tile_w_omajor(np.ascontiguousarray(inputs["L0_ffn_Wb"].T))
    wr_t = np.ascontiguousarray(inputs["L0_ffn_Wr"].T)
    wa_t = np.ascontiguousarray(inputs["L0_ffn_A"].reshape(128, H).T)
    wbm_t = np.ascontiguousarray(
        inputs["L0_ffn_Bm"].transpose(0, 2, 1).reshape(128, H)
        .reshape(128, KH, 128).transpose(1, 0, 2))
    bias_t = col_layout(inputs["L0_ffn_bias"])
    e16_t = np.zeros((8, 128), np.float32)
    for e in range(8):
        e16_t[e, e * 16:(e + 1) * 16] = 1.0
    idf_t = np.eye(128, dtype=np.float32)
    maps = []
    for c in range(NCORES):
        maps.append({
            "xin": np.ascontiguousarray(x_fm[c]),
            "wbase": wbase_t, "bias": bias_t, "wr": wr_t, "wa": wa_t,
            "wbm": wbm_t, "e16": e16_t, "idf": idf_t,
        })
    return maps


def x_to_fm(x):
    out = np.empty((NCORES, KH, 128, T), np.float32)
    for c in range(NCORES):
        b, t0 = divmod(c, 4)
        sl = x[b, t0 * T:(t0 + 1) * T, :]
        out[c] = np.ascontiguousarray(sl.T).reshape(KH, 128, T)
    return out


def fm_to_x(fm):
    x = np.empty((2, 4 * T, H), np.float32)
    for c in range(NCORES):
        b, t0 = divmod(c, 4)
        x[b, t0 * T:(t0 + 1) * T, :] = fm[c].reshape(H, T).T
    return x


def kernel(**inputs):
    cores = list(range(8))
    maps0 = host_prep_l0attn(inputs, inputs["L0_attn_hidden_states"])
    res0 = run_bass_kernel_spmd(build_l0attn(), maps0, core_ids=cores)
    x1_fm = np.stack([res0.results[c]["xout"] for c in range(8)])
    maps1 = host_prep_l0ffn(inputs, x1_fm)
    res1 = run_bass_kernel_spmd(build_l0ffn(), maps1, core_ids=cores)
    x2_fm = np.stack([res1.results[c]["xout"] for c in range(8)])
    maps2 = host_prep_l1attn(inputs, x2_fm)
    mapsA = [{k: m[k] for k in ("xin", "wqkv", "bqkv", "cosd", "sind")}
             for m in maps2]
    resA = run_bass_kernel_spmd(build_l1attn_a(), mapsA, core_ids=cores)
    qkv = [resA.results[c]["qkvout"] for c in range(8)]
    kvin = [np.concatenate([qkv[r][c * 768:(c + 1) * 768] for r in range(8)],
                           0) for c in range(8)]
    mapsB = [{"kvin": kvin[c], "masks": maps2[c]["masks"],
              "idf": maps2[c]["idf"]} for c in range(8)]
    resB = run_bass_kernel_spmd(build_l1attn_b(), mapsB, core_ids=cores)
    ctx = [resB.results[c]["ctxout"] for c in range(8)]
    ctxin = [np.concatenate([ctx[r][c * 512:(c + 1) * 512] for r in range(8)],
                            0) for c in range(8)]
    mapsC = [{"xin": maps2[c]["xin"], "ctxin": ctxin[c],
              "wdo": maps2[c]["wdo"], "bd": maps2[c]["bd"]}
             for c in range(8)]
    resC = run_bass_kernel_spmd(build_l1attn_c(), mapsC, core_ids=cores)
    x3_fm = np.stack([resC.results[c]["xout"] for c in range(8)])
    maps3 = host_prep_l1ffn(inputs, x3_fm)
    res3 = run_bass_kernel_spmd(build_l1ffn(), maps3, core_ids=cores)
    x4_fm = np.stack([res3.results[c]["xout"] for c in range(8)])
    return fm_to_x(x4_fm)


def build_l1ffn(debug=False, upto=99):
    nc = bass.Bass("TRN2", target_bir_lowering=False)
    xin = nc.dram_tensor("xin", [KH, 128, T], F32, kind="ExternalInput")
    wq = nc.dram_tensor("wq", [KH * 128, 64], F32, kind="ExternalInput")
    wrp = nc.dram_tensor("wrp", [64, 48], F32, kind="ExternalInput")
    akT = nc.dram_tensor("akT", [64, 1024], F32, kind="ExternalInput")
    av = nc.dram_tensor("av", [8, 128, H], F32, kind="ExternalInput")
    wvd = nc.dram_tensor("wvd", [KH * 128, 256], F32, kind="ExternalInput")
    wvu = nc.dram_tensor("wvu", [2 * 128, H], F32, kind="ExternalInput")
    wu1o = nc.dram_tensor("wu1o", [8, KH * 128, 128], F32, kind="ExternalInput")
    wu2 = nc.dram_tensor("wu2", [128, 8], F32, kind="ExternalInput")
    bu1c = nc.dram_tensor("bu1c", [128, 8], F32, kind="ExternalInput")
    sc3 = nc.dram_tensor("sc3", [1, 4], F32, kind="ExternalInput")
    lr = nc.dram_tensor("lr", [1, 1024], F32, kind="ExternalInput")
    idf = nc.dram_tensor("idf", [128, 128], F32, kind="ExternalInput")
    xout = nc.dram_tensor("xout", [KH, 128, T], F32, kind="ExternalOutput")
    dbg = {}
    if debug:
        dbg["dgate"] = nc.dram_tensor("dgate", [1, T], F32, kind="ExternalOutput")
        dbg["dm8"] = nc.dram_tensor("dm8", [128, 8], F32, kind="ExternalOutput")
        dbg["daux"] = nc.dram_tensor("daux", [128, KH * 128], F32, kind="ExternalOutput")

    kb = KB(nc)
    cm = Common(kb)
    emit_l1ffn(kb, cm, dict(xin=xin, wq=wq, wrp=wrp, akT=akT, av=av, wvd=wvd,
                            wvu=wvu, wu1o=wu1o, wu2=wu2, bu1c=bu1c, sc3=sc3,
                            lr=lr, idf=idf, xout=xout), dbg=dbg, upto=upto)
    kb.wait("sync", cm.sS)
    kb.finalize()
    return nc


def emit_l1ffn(kb, cm, io, dbg=None, upto=99):
    xin, xout = io["xin"], io["xout"]
    RD = np.sqrt(48.0)

    xb = [kb.sb([128, T], BF16, "g_xb") for _ in range(KH)]
    mb = kb.sb([128, T], BF16, "g_mb")
    rb = kb.sb([128, T], BF16, "g_rb")
    idb_sb = kb.sb([128, 128], BF16, "g_idb")
    wq_sb = kb.sb([128, KH * 64], BF16, "g_wq")
    wrp_sb = kb.sb([64, 48], BF16, "g_wrp")
    akT_sb = kb.sb([64, 1024], BF16, "g_akT")
    avbuf = [kb.sb([128, H], BF16, "g_avb") for _ in range(2)]
    wvd_sb = kb.sb([128, KH * 256], BF16, "g_wvd")
    wvu_sb = kb.sb([128, 2 * H], BF16, "g_wvu")
    wu2_sb = kb.sb([128, 8], BF16, "g_wu2")
    bu1_sb = kb.sb([128, 8], F32, "g_bu1")
    sc3_sb = kb.sb([1, 4], F32, "g_sc3")
    lr_sb = kb.sb([1, 1024], F32, "g_lr")
    q_sb = kb.sb([64, T], BF16, "g_q")
    rq_sb = kb.sb([48, T], BF16, "g_rq")
    rk_sb = kb.sb([48, 1024], BF16, "g_rk")
    u1tmp = kb.sb([128, 512], BF16, "g_u1t")
    g0 = kb.sb([128, 512], F32, "g_g0")
    g1 = kb.sb([128, 512], F32, "g_g1")
    g2 = kb.sb([128, 512], F32, "g_g2")
    rowz = kb.sb([1, 16], F32, "g_rowz")
    urow = kb.sb([1, T], F32, "g_urow")
    u2row = kb.sb([1, T], F32, "g_u2row")
    gb_sb = kb.sb([128, T], F32, "g_gb")
    s_orig = kb.sb([128, 1024], F32, "g_sorig")
    eqt = kb.sb([128, 1024], F32, "g_eqt")
    s_work = kb.sb([128, 1024], F32, "g_swork")
    sa_sb = kb.sb([128, 1024], F32, "g_sa")
    p_sb = kb.sb([128, 1024], BF16, "g_p")
    pT_sb = kb.sb([128, 1024], BF16, "g_pT")
    aux_tc = kb.sb([128, KH * 128], BF16, "g_aux")
    vd_sb = kb.sb([128, 256], BF16, "g_vd")
    col = kb.sb([128, 4], F32, "g_col")
    xres = [kb.sb([128, 128], F32, "g_xres") for _ in range(4)]
    xo = [kb.sb([128, 128], F32, "g_xo") for _ in range(2)]

    ar_in = kb.nc.dram_tensor(kb.name("g_arin"), [1, 16], F32, kind="Internal")
    ar_out = kb.nc.dram_tensor(kb.name("g_arout"), [1, 16], F32, kind="Internal")

    for k in range(KH):
        kb.dma("gpsimd", xb[k][:], xin.ap()[k], sem=cm.sW)
    kb.dma("gpsimd", idb_sb[:], io["idf"].ap(), sem=cm.sW)
    kb.dma("gpsimd", wq_sb[:].rearrange("p (k c) -> p k c", c=64),
           io["wq"].ap().rearrange("(k p) c -> p k c", p=128), sem=cm.sW)
    kb.dma("gpsimd", wrp_sb[:], io["wrp"].ap(), sem=cm.sW)
    kb.dma("gpsimd", akT_sb[:], io["akT"].ap(), sem=cm.sW)
    kb.dma("gpsimd", wvd_sb[:].rearrange("p (k c) -> p k c", c=256),
           io["wvd"].ap().rearrange("(k p) c -> p k c", p=128), sem=cm.sW)
    kb.dma("gpsimd", wvu_sb[:].rearrange("p (k c) -> p k c", c=H),
           io["wvu"].ap().rearrange("(k p) c -> p k c", p=128), sem=cm.sW)
    kb.dma("gpsimd", wu2_sb[:], io["wu2"].ap(), sem=cm.sW)
    kb.dma("sync", bu1_sb[:], io["bu1c"].ap(), sem=cm.sX)
    kb.dma("sync", sc3_sb[:], io["sc3"].ap(), sem=cm.sX)
    kb.dma("sync", lr_sb[:], io["lr"].ap(), sem=cm.sX)
    ldw, ldx = cm.sW.n, cm.sX.n
    for eng in ("tensor", "vector", "scalar", "gpsimd"):
        kb.wait(eng, cm.sW, ldw)
        kb.wait(eng, cm.sX, ldx)
        kb.wait(eng, cm.sM)

    mrow, vrow = emit_ln(kb, cm, xb, T, mb, rb)

    if upto < 2:
        return
    kb.o("vector", "memset", rowz[:], 0.0)
    kb.o("vector", "tensor_scalar", urow[0:1, :], vrow[0:1, :], 1e-5, None,
         ALU.add, bar=True)
    kb.o("vector", "reciprocal", urow[0:1, :], urow[0:1, :], bar=True)
    kb.o("vector", "tensor_mul", urow[0:1, :], urow[0:1, :], vrow[0:1, :],
         bar=True)
    kb.wait("scalar", cm.sV, cm.sV.n)
    cm.pub("vector", cm.sV)
    kb.wait("scalar", cm.sV)
    kb.o("scalar", "activation", u2row[0:1, :], urow[0:1, :], AF.Ln,
         bias=cm.c1()[0:1, :], bar=True)
    cm.pub("scalar", cm.sA)
    kb.wait("vector", cm.sA)
    kb.o("vector", "reduce_sum", rowz[0:1, 0:1], u2row[0:1, :], axis=AX,
         bar=True)
    cm.pub("vector", cm.sV)
    kb.wait("gpsimd", cm.sV)
    kb.dma("gpsimd", ar_in.ap(), rowz[:], sem=cm.sG)
    kb.wait("gpsimd", cm.sG)
    kb.cc("AllReduce", ALU.add, [list(range(8))], [ar_in.ap().opt()],
          [ar_out.ap().opt()], cm.sCC)
    kb.wait("gpsimd", cm.sCC)
    kb.dma("gpsimd", rowz[:], ar_out.ap(), sem=cm.sG)

    if upto < 3:
        return
    kb.wait("tensor", cm.sV)
    psQ = [cm.pb[0], cm.pb[1]]
    for si in range(2):
        s0 = si * 512
        for k in range(KH):
            kb.o("tensor", "matmul", psQ[si][0:64, :],
                 wq_sb[:, k * 64:(k + 1) * 64], xb[k][:, s0:s0 + 512],
                 start=(k == 0), stop=(k == KH - 1),
                 sem=(cm.sT if k == KH - 1 else None))
    kb.wait("vector", cm.sT)
    for si in range(2):
        kb.o("vector", "tensor_scalar", q_sb[0:64, si * 512:(si + 1) * 512],
             psQ[si][0:64, :], 0.125, None, ALU.mult,
             sem=(cm.sA if si == 1 else None))
    kb.wait("tensor", cm.sA)
    psR = [cm.pb[2], cm.pb[3]]
    for si in range(2):
        s0 = si * 512
        kb.o("tensor", "matmul", psR[si][0:48, :], wrp_sb[:],
             q_sb[0:64, s0:s0 + 512], start=True, stop=True,
             sem=(cm.sT if si == 1 else None))
    kb.wait("vector", cm.sT)
    for si in range(2):
        s0 = si * 512
        kb.o("vector", "tensor_scalar", rq_sb[0:48, s0:s0 + 512],
             psR[si][0:48, :], 8.0 / RD, None, ALU.mult,
             sem=(cm.sV if si == 1 else None))
    kb.wait("tensor", cm.sV)
    psK = [cm.pb[4], cm.pb[5]]
    for si in range(2):
        s0 = si * 512
        kb.o("tensor", "matmul", psK[si][0:48, :], wrp_sb[:],
             akT_sb[:, s0:s0 + 512], start=True, stop=True,
             sem=(cm.sT if si == 1 else None))
    kb.wait("scalar", cm.sT)
    for si in range(2):
        s0 = si * 512
        kb.o("scalar", "copy", rk_sb[0:48, s0:s0 + 512], psK[si][0:48, :],
             sem=(cm.sA if si == 1 else None))

    if upto < 4:
        return
    sEv1 = kb.sem("g_sEv1")

    def evac_u1(j, o, s0, sl, ps):
        if PH4 < 2:
            kb.wait("vector", cm.sT)
            kb.o("vector", "tensor_scalar", g0[:, 0:sl], ps, 1.0, None,
                 ALU.mult, sem=sEv1)
            return
        kb.wait("vector", cm.sT)
        kb.o("vector", "tensor_scalar", g0[:, 0:sl], ps, bu1_sb[:, o:o + 1],
             None, ALU.add, sem=cm.sV)
        kb.wait("scalar", cm.sV)
        kb.o("scalar", "activation", g1[:, 0:sl], g0[:, 0:sl], AF.Square,
             sem=cm.sA)
        kb.wait("vector", cm.sA)
        kb.o("vector", "tensor_mul", g1[:, 0:sl], g1[:, 0:sl], g0[:, 0:sl])
        kb.o("vector", "scalar_tensor_tensor", g1[:, 0:sl], g1[:, 0:sl],
             0.044715, g0[:, 0:sl], ALU.mult, ALU.add, sem=cm.sV)
        kb.wait("scalar", cm.sV)
        kb.o("scalar", "activation", g2[:, 0:sl], g1[:, 0:sl], AF.Tanh,
             scale=0.7978845608028654, sem=cm.sA)
        kb.wait("vector", cm.sA)
        kb.o("vector", "tensor_scalar", g2[:, 0:sl], g2[:, 0:sl], 1.0, 0.5,
             ALU.add, ALU.mult)
        kb.o("vector", "tensor_tensor", u1tmp[:, 0:sl], g2[:, 0:sl],
             g0[:, 0:sl], ALU.mult, bar=True)
        cm.pub("vector", cm.sV)
        kb.wait("tensor", cm.sV)
        if NO_PSL:
            kb.o("tensor", "matmul", psL[j % 2][0:1, 0:sl], wu2_sb[:, o:o + 1],
                 u1tmp[:, 0:sl], start=True, stop=True, sem=sEv1)
        else:
            kb.o("tensor", "matmul", psL[j % 2][0:1, 0:sl], wu2_sb[:, o:o + 1],
                 u1tmp[:, 0:sl], start=(o == 0), stop=(o == 7), sem=sEv1)

    psL = [cm.pb[0], cm.pb[1]]
    kb.wait("tensor", cm.sA)
    emit_dense(kb, cm, sEv1, io["wu1o"], KH, 8,
               lambda k, s0, sl: xb[k][:, s0:s0 + sl], T, evac_u1,
               dbanks=(6, 7), nbuf=1)

    if PH4 < 3:
        return
    kb.wait("vector", sEv1)
    for si in range(2):
        s0 = si * 512
        kb.o("vector", "tensor_scalar", urow[0:1, s0:s0 + 512],
             psL[si][0:1, :], sc3_sb[0:1, 0:1], None, ALU.add,
             sem=(cm.sV if si == 1 else None), bar=True)
    if PH4 < 4:
        return
    kb.wait("vector", cm.sG)
    kb.o("vector", "tensor_scalar", rowz[0:1, 0:1], rowz[0:1, 0:1],
         1.0 / 8192.0, 1e-6, ALU.mult, ALU.add, bar=True)
    kb.o("vector", "reciprocal", rowz[0:1, 1:2], rowz[0:1, 0:1], bar=True)
    cm.pub("vector", cm.sV)
    kb.wait("scalar", cm.sV)
    kb.o("scalar", "activation", urow[0:1, :], urow[0:1, :], AF.Sigmoid,
         bar=True)
    cm.pub("scalar", cm.sA)
    kb.wait("vector", cm.sA)
    kb.o("vector", "tensor_scalar", u2row[0:1, :], u2row[0:1, :],
         rowz[0:1, 1:2], 0.5, ALU.mult, ALU.mult, bar=True)
    kb.o("vector", "tensor_scalar", urow[0:1, :], urow[0:1, :], 2.5, None,
         ALU.mult, bar=True)
    kb.o("vector", "tensor_add", urow[0:1, :], urow[0:1, :], u2row[0:1, :],
         bar=True)
    kb.o("vector", "tensor_scalar", urow[0:1, :], urow[0:1, :], 0.0, 5.0,
         ALU.max, ALU.min, bar=True)
    kb.o("vector", "tensor_scalar", urow[0:1, :], urow[0:1, :], -0.5,
         1.0 / 1.5, ALU.add, ALU.mult, bar=True)
    kb.o("vector", "tensor_scalar", urow[0:1, :], urow[0:1, :], 0.0, 1.0,
         ALU.max, ALU.min, bar=True)
    cm.pub("vector", cm.sV)
    kb.wait("scalar", cm.sV)
    kb.o("scalar", "activation", urow[0:1, :], urow[0:1, :], AF.Sigmoid,
         bias=sc3_sb[0:1, 2:3], scale=sc3_sb[0:1, 1:2], bar=True)
    cm.pub("scalar", cm.sA)
    kb.wait("vector", cm.sA)
    kb.o("vector", "tensor_scalar", u2row[0:1, :], urow[0:1, :], 0.05, None,
         ALU.is_ge, bar=True)
    kb.o("vector", "tensor_mul", urow[0:1, :], urow[0:1, :], u2row[0:1, :],
         bar=True)
    cm.pub("vector", cm.sV)
    if dbg:
        kb.wait("gpsimd", cm.sV)
        kb.dma("gpsimd", dbg["dgate"].ap(), urow[0:1, :], sem=cm.sS)
    kb.wait("tensor", cm.sV)
    for si in range(2):
        s0 = si * 512
        if si == 1:
            kb.wait("tensor", cm.sA)
        kb.o("tensor", "matmul", cm.pb[2][:], cm.one_row[:],
             urow[0:1, s0:s0 + 512], start=True, stop=True, sem=cm.sT)
        kb.wait("scalar", cm.sT)
        kb.o("scalar", "copy", gb_sb[:, s0:s0 + 512], cm.pb[2][:], sem=cm.sA)

    if upto < 5:
        return
    sEv2 = kb.sem("g_sEv2")
    avals = {}
    avw = {}
    avmm = {}
    for jj in range(3):
        o0 = jj
        kb.dma("sync", xres[jj % 4][:], xin.ap()[o0][:, 0:128], sem=cm.sX)
    for tc in range(8):
        ts0 = tc * 128
        kb.wait("tensor", cm.sA)
        kb.wait("tensor", cm.sV)
        for si in range(2):
            s0 = si * 512
            kb.o("tensor", "matmul", cm.pb[si][:], rq_sb[0:48, ts0:ts0 + 128],
                 rk_sb[0:48, s0:s0 + 512], start=True, stop=False)
            kb.o("tensor", "matmul", cm.pb[si][:], cm.one_row[:],
                 lr_sb[0:1, s0:s0 + 512], start=False, stop=True,
                 sem=cm.sT)
            kb.o("tensor", "matmul", cm.pb[2 + si][:],
                 q_sb[0:64, ts0:ts0 + 128], akT_sb[:, s0:s0 + 512],
                 start=True, stop=False)
            kb.o("tensor", "matmul", cm.pb[2 + si][:], cm.one_row[:],
                 lr_sb[0:1, s0:s0 + 512], start=False, stop=True, sem=cm.sT)
        kb.wait("scalar", cm.sT)
        for si in range(2):
            s0 = si * 512
            kb.o("scalar", "copy", s_orig[:, s0:s0 + 512], cm.pb[si][:])
            kb.o("scalar", "copy", s_work[:, s0:s0 + 512], cm.pb[si][:],
                 sem=(cm.sA if si == 1 else None))
        kb.wait("vector", cm.sA)
        for si in range(2):
            s0 = si * 512
            kb.o("vector", "tensor_scalar", sa_sb[:, s0:s0 + 512],
                 cm.pb[2 + si][:], 1.0, None, ALU.mult,
                 sem=(cm.sV if si == 1 else None))
        for it in range(8):
            kb.o("vector", "reduce_max", col[:, 0:1], s_work[:], axis=AX,
                 bar=True)
            if it < 7:
                kb.o("vector", "tensor_scalar", eqt[:], s_work[:],
                     col[:, 0:1], None, ALU.is_equal, bar=True)
                kb.o("vector", "scalar_tensor_tensor", s_work[:], eqt[:],
                     -1e30, s_work[:], ALU.mult, ALU.add, bar=True)
        kb.o("vector", "tensor_scalar", s_work[:], s_orig[:], col[:, 0:1],
             None, ALU.is_ge, bar=True)
        kb.o("vector", "tensor_scalar", sa_sb[:], sa_sb[:], 1e4, None,
             ALU.add)
        kb.o("vector", "tensor_mul", sa_sb[:], sa_sb[:], s_work[:], bar=True)
        kb.o("vector", "tensor_scalar", sa_sb[:], sa_sb[:], -1e4, None,
             ALU.add, bar=True)
        kb.o("vector", "reduce_max", col[:, 1:2], sa_sb[:], axis=AX, bar=True)
        kb.o("vector", "tensor_scalar", col[:, 2:3], col[:, 1:2], -1.0, None,
             ALU.mult, bar=True)
        cm.pub("vector", cm.sV)
        kb.wait("scalar", cm.sV)
        kb.o("scalar", "activation", p_sb[:], sa_sb[:], AF.Exp,
             bias=col[:, 2:3], accum_out=col[:, 3:4], bar=True)
        cm.pub("scalar", cm.sA)
        kb.wait("vector", cm.sA)
        kb.o("vector", "reciprocal", col[:, 3:4], col[:, 3:4], bar=True)
        kb.o("vector", "tensor_scalar", p_sb[:], p_sb[:], col[:, 3:4], None,
             ALU.mult, bar=True)
        cm.pub("vector", cm.sV)
        kb.wait("tensor", cm.sV)
        for sblk in range(8):
            kb.o("tensor", "transpose",
                 cm.pb[4][:, 64 * (sblk % 2):64 * (sblk % 2) + 64].bitcast(BF16),
                 p_sb[:, sblk * 128:(sblk + 1) * 128], idb_sb[:], sem=cm.sT)
            kb.wait("scalar", cm.sT)
            kb.o("scalar", "copy", pT_sb[:, sblk * 128:(sblk + 1) * 128],
                 cm.pb[4][:, 64 * (sblk % 2):64 * (sblk % 2) + 64].bitcast(BF16),
                 sem=cm.sA)
            if sblk < 7:
                kb.wait("tensor", cm.sA)
        if tc == 0:
            for pre in range(2):
                kb.dma("gpsimd", avbuf[pre][:].rearrange(
                           "p (k c) -> p k c", c=512),
                       io["av"].ap()[:, :, pre * 512:(pre + 1) * 512]
                       .rearrange("k p c -> p k c"), sem=cm.sW)
        kb.wait("tensor", cm.sA)
        kb.wait("tensor", sEv2)
        for og in range(8):
            pbk = cm.pb[5 + og % 2]
            if og >= 2:
                kb.wait("tensor", cm.sA, avals[og - 2])
            kb.wait("tensor", cm.sW, avw[(tc, og)] if (tc, og) in avw
                    else cm.sW.n)
            for oo in range(4):
                o = og * 4 + oo
                for k in range(8):
                    kb.o("tensor", "matmul", pbk[:, oo * 128:oo * 128 + 128],
                         avbuf[og % 2][:, k * 512 + oo * 128:k * 512 + (oo + 1) * 128],
                         pT_sb[:, k * 128:(k + 1) * 128],
                         start=(k == 0 and oo == 0), stop=(k == 7 and oo == 3),
                         sem=(cm.sT if (k == 7 and oo == 3) else None))
            avmm[(tc, og)] = cm.sT.n
            ng, ntc = og + 2, tc
            if ng >= 8:
                ng, ntc = ng - 8, tc + 1
            if ntc < 8:
                kb.wait("gpsimd", cm.sT, avmm[(tc, og)])
                kb.dma("gpsimd", avbuf[og % 2][:].rearrange(
                           "p (k c) -> p k c", c=512),
                       io["av"].ap()[:, :, ng * 512:(ng + 1) * 512]
                       .rearrange("k p c -> p k c"), sem=cm.sW)
                avw[(ntc, ng)] = cm.sW.n
            kb.wait("scalar", cm.sT, avmm[(tc, og)])
            kb.o("scalar", "copy", aux_tc[:, og * 512:(og + 1) * 512], pbk[:],
                 sem=cm.sA)
            avals[og] = cm.sA.n
        kb.wait("tensor", cm.sA)
        for o2 in range(2):
            for k in range(KH):
                kb.o("tensor", "matmul",
                     cm.pb[7][:, o2 * 128:o2 * 128 + 128],
                     wvd_sb[:, k * 256 + o2 * 128:k * 256 + o2 * 128 + 128],
                     aux_tc[:, k * 128:(k + 1) * 128],
                     start=(k == 0 and o2 == 0), stop=(k == KH - 1 and o2 == 1),
                     sem=(cm.sT if (k == KH - 1 and o2 == 1) else None))
        kb.wait("scalar", cm.sT)
        kb.o("scalar", "copy", vd_sb[:], cm.pb[7][:, 0:256], sem=cm.sA)
        kb.wait("tensor", cm.sA)
        for o in range(KH):
            j = tc * KH + o
            pbk = cm.pb[5 + j % 2]
            if j >= 2:
                kb.wait("tensor", sEv2, j - 1)
            for k2 in range(2):
                kb.o("tensor", "matmul", pbk[:, 0:128],
                     wvu_sb[:, k2 * H + o * 128:k2 * H + (o + 1) * 128],
                     vd_sb[:, k2 * 128:(k2 + 1) * 128],
                     start=(k2 == 0), stop=(k2 == 1),
                     sem=(cm.sT if k2 == 1 else None))
            jf = j + 3
            if jf < 256:
                tf, of = divmod(jf, KH)
                if j > 0:
                    kb.wait("sync", sEv2, j)
                kb.dma("sync", xres[jf % 4][:],
                       xin.ap()[of][:, tf * 128:(tf + 1) * 128], sem=cm.sX)
            kb.wait("vector", cm.sT)
            kb.wait("vector", cm.sX, ldx + 16 * (j + 1))
            if j >= 2:
                kb.wait("vector", cm.sS, 16 * (j - 1))
            kb.o("vector", "tensor_tensor", xo[j % 2][:], pbk[:, 0:128],
                 gb_sb[:, ts0:ts0 + 128], ALU.mult)
            kb.o("vector", "tensor_tensor", xo[j % 2][:], xo[j % 2][:],
                 xb[o][:, ts0:ts0 + 128], ALU.add)
            kb.o("vector", "tensor_tensor", xo[j % 2][:], xo[j % 2][:],
                 xres[j % 4][:], ALU.add, sem=sEv2)
            kb.wait("sync", sEv2, sEv2.n)
            kb.dma("sync", xout.ap()[o][:, ts0:ts0 + 128], xo[j % 2][:],
                   sem=cm.sS)
        if dbg and tc == 0:
            kb.wait("gpsimd", cm.sA)
            kb.dma("gpsimd", dbg["daux"].ap(), aux_tc[:], sem=cm.sS)


def host_prep_l1ffn(inputs, x_fm):
    wq_t = np.ascontiguousarray(inputs["L1_ffn_Wq"].T)
    wrp_t = np.ascontiguousarray(inputs["L1_ffn_Wr"].T)
    akT_t = np.ascontiguousarray(inputs["L1_ffn_aux_keys"].T)
    av_t = np.ascontiguousarray(
        inputs["L1_ffn_aux_values"].reshape(8, 128, H))
    wvd_t = np.ascontiguousarray(inputs["L1_ffn_Wvd"].T)
    wvu_t = np.ascontiguousarray(inputs["L1_ffn_Wvu"].T)
    wu1o_t = tile_w_omajor(np.ascontiguousarray(inputs["L1_ffn_Wu1"].T))
    wu2_t = np.ascontiguousarray(inputs["L1_ffn_Wu2"].reshape(8, 128).T)
    bu1c_t = col_layout(inputs["L1_ffn_bu1"])
    sc3_t = np.array([[inputs["L1_ffn_bu2"].reshape(-1)[0],
                       float(inputs["L1_ffn_gate_w1"]),
                       float(inputs["L1_ffn_gate_bias"]), 0.0]], np.float32)
    lr_t = np.ascontiguousarray(inputs["L1_ffn_log_rel"].reshape(1, 1024))
    idf_t = np.eye(128, dtype=np.float32)
    maps = []
    for c in range(NCORES):
        maps.append({
            "xin": np.ascontiguousarray(x_fm[c]), "wq": wq_t, "wrp": wrp_t,
            "akT": akT_t, "av": av_t, "wvd": wvd_t, "wvu": wvu_t,
            "wu1o": wu1o_t, "wu2": wu2_t, "bu1c": bu1c_t, "sc3": sc3_t,
            "lr": lr_t, "idf": idf_t,
        })
    return maps


def build_l1attn_a():
    nc = bass.Bass("TRN2", target_bir_lowering=False)
    xin = nc.dram_tensor("xin", [KH, 128, T], F32, kind="ExternalInput")
    wqkv = nc.dram_tensor("wqkv", [48, KH * 128, 128], F32, kind="ExternalInput")
    bqkv = nc.dram_tensor("bqkv", [128, 48], F32, kind="ExternalInput")
    cosd = nc.dram_tensor("cosd", [128, T], F32, kind="ExternalInput")
    sind = nc.dram_tensor("sind", [128, T], F32, kind="ExternalInput")
    qkvout = nc.dram_tensor("qkvout", [6144, 1024], BF16, kind="ExternalOutput")
    kb = KB(nc)
    cm = Common(kb, groups=(4, 1, 1, 1, 1))
    emit_l1attn_qkv(kb, cm, dict(xin=xin, wqkv=wqkv, bqkv=bqkv, cosd=cosd,
                                 sind=sind), qkvout)
    kb.wait("sync", cm.sS)
    kb.finalize()
    return nc


def build_l1attn_b():
    nc = bass.Bass("TRN2", target_bir_lowering=False)
    kvin = nc.dram_tensor("kvin", [6144, 1024], BF16, kind="ExternalInput")
    masks = nc.dram_tensor("masks", [6, 128, 512], F32, kind="ExternalInput")
    idf = nc.dram_tensor("idf", [128, 128], F32, kind="ExternalInput")
    ctxout = nc.dram_tensor("ctxout", [4096, 1024], BF16, kind="ExternalOutput")
    kb = KB(nc)
    cm = Common(kb, groups=(4, 1, 1, 1, 1))
    emit_l1attn_att(kb, cm, dict(masks=masks, idf=idf), kvin, ctxout)
    kb.wait("sync", cm.sS)
    kb.finalize()
    return nc


def build_l1attn_c():
    nc = bass.Bass("TRN2", target_bir_lowering=False)
    xin = nc.dram_tensor("xin", [KH, 128, T], F32, kind="ExternalInput")
    ctxin = nc.dram_tensor("ctxin", [4096, 1024], BF16, kind="ExternalInput")
    wdo = nc.dram_tensor("wdo", [KH, KH * 128, 128], F32, kind="ExternalInput")
    bd = nc.dram_tensor("bd", [128, KH], F32, kind="ExternalInput")
    xout = nc.dram_tensor("xout", [KH, 128, T], F32, kind="ExternalOutput")
    kb = KB(nc)
    cm = Common(kb, groups=(4, 1, 1, 1, 1))
    emit_l1attn_dense(kb, cm, dict(xin=xin, wdo=wdo, bd=bd, xout=xout), ctxin)
    kb.wait("sync", cm.sS)
    kb.finalize()
    return nc


def emit_l1attn_qkv(kb, cm, io, a2a_in):
    xin = io["xin"]

    xb = [kb.sbt([128, T], BF16, "a_xb") for _ in range(KH)]
    mb = kb.sbt([128, T], BF16, "a_mb")
    rb = kb.sbt([128, T], BF16, "a_rb")
    cos_sb = kb.sbt([128, T], F32, "a_cos")
    sin_sb = kb.sbt([128, T], F32, "a_sin")
    g0 = kb.sbt([128, 512], F32, "a_g0")
    rtmp = kb.sbt([128, 512], F32, "a_rt")
    qbuf = [kb.sbt([128, 512], BF16, "a_qb") for _ in range(2)]
    bq_sb = kb.sbt([128, 48], F32, "a_bq")

    for k in range(KH):
        kb.dma("gpsimd", xb[k][:], xin.ap()[k], sem=cm.sW)
    kb.dma("sync", cos_sb[:], io["cosd"].ap(), sem=cm.sX)
    kb.dma("sync", sin_sb[:], io["sind"].ap(), sem=cm.sX)
    kb.dma("sync", bq_sb[:], io["bqkv"].ap(), sem=cm.sX)
    ldw, ldx = cm.sW.n, cm.sX.n
    for eng in ("tensor", "vector", "scalar"):
        kb.wait(eng, cm.sW, ldw)
        kb.wait(eng, cm.sX, ldx)
        kb.wait(eng, cm.sM)
    emit_ln(kb, cm, xb, T, mb, rb, banks=(0, 4, 1, 5))

    sEvq = kb.sem("a_sEvq")

    def evac_qkv(j, o, s0, sl, ps):
        rope = (o % 6) < 5
        qb = qbuf[j % 2]
        kb.wait("vector", cm.sT)
        if j >= 2:
            kb.wait("vector", cm.sS, 16 * (j - 1))
        kb.o("vector", "tensor_scalar", g0[:, 0:sl], ps, bq_sb[:, o:o + 1],
             None, ALU.add, sem=cm.sV)
        if rope:
            kb.wait("scalar", cm.sV)
            kb.o("scalar", "copy", rtmp[0:64, 0:sl], g0[64:128, 0:sl])
            kb.o("scalar", "copy", rtmp[64:128, 0:sl], g0[0:64, 0:sl],
                 sem=cm.sA)
            kb.wait("vector", cm.sA)
            kb.o("vector", "tensor_mul", rtmp[:, 0:sl], rtmp[:, 0:sl],
                 sin_sb[:, s0:s0 + sl])
            kb.o("vector", "tensor_mul", g0[:, 0:sl], g0[:, 0:sl],
                 cos_sb[:, s0:s0 + sl])
            kb.o("vector", "tensor_add", qb[:, 0:sl], g0[:, 0:sl],
                 rtmp[:, 0:sl], sem=sEvq)
        else:
            kb.o("vector", "tensor_scalar", qb[:, 0:sl], g0[:, 0:sl], 1.0,
                 None, ALU.mult, sem=sEvq)
        kb.wait("sync", sEvq, sEvq.n)
        kb.dma("sync", a2a_in.ap()[o * 128:(o + 1) * 128, s0:s0 + sl],
               qb[:, 0:sl], sem=cm.sS)

    kb.wait("tensor", cm.sV)
    emit_dense(kb, cm, sEvq, io["wqkv"], KH, 48,
               lambda k, s0, sl: xb[k][:, s0:s0 + sl], T, evac_qkv,
               dbanks=(4, 5))
    kb.free(*xb, mb, rb, cos_sb, sin_sb, g0, rtmp, *qbuf, bq_sb)


def emit_l1attn_att(kb, cm, io, a2a_out, a2b_in):
    Kb_sb = kb.sbt([128, 4096], BF16, "a_K")
    Vt_sb = kb.sbt([128, 4096], BF16, "a_V")
    qI = kb.sbt([128, 8192], BF16, "a_qI")
    ctx_sb = kb.sbt([128, 512], BF16, "a_ctx")
    mks = kb.sbt([128, 6 * 512], F32, "a_mks")
    idf_sb = kb.sbt([128, 128], F32, "a_idf")
    idb_sb = kb.sbt([128, 128], BF16, "a_idb")
    mrow_sb = kb.sbt([1, 512], F32, "a_mrow")
    mbc = kb.sbt([128, 512], F32, "a_mbc")
    rbc = kb.sbt([128, 512], F32, "a_rbc")
    srow = kb.sbt([1, 512], F32, "a_srow")
    ps_big = cm.pb

    kb.dma("sync", idf_sb[:], io["idf"].ap(), sem=cm.sX)
    kb.dma("gpsimd", idb_sb[:], io["idf"].ap(), sem=cm.sW)
    kb.dma("sync", mks[:].rearrange("p (m c) -> p m c", c=512),
           io["masks"].ap().rearrange("m p c -> p m c"), sem=cm.sX)

    kb._mks = mks

    def mk(i):
        return mks[:, i * 512:(i + 1) * 512]

    for eng in ("tensor", "vector", "scalar", "gpsimd", "sync"):
        kb.wait(eng, cm.sX, cm.sX.n)
        kb.wait(eng, cm.sW, cm.sW.n)
        kb.wait(eng, cm.sM)

    emit_l1attn_body(kb, cm, io, a2a_out, a2b_in,
                     Kb_sb, Vt_sb, qI, ctx_sb, mk, idf_sb, idb_sb,
                     mrow_sb, mbc, rbc, srow)



def emit_l1attn_body(kb, cm, io, a2a_out, a2b_in, Kb_sb, Vt_sb, qI,
                     ctx_sb, mk, idf_sb, idb_sb, mrow_sb, mbc, rbc, srow):
    SM = 1.0 / 128.0
    PS = cm.tg[0]
    t_sb = kb.sbt([128, 512], F32, "a_t")
    pT_sb = kb.sbt([128, 512], BF16, "a_pT")
    m4 = kb.sbt([128, 4], F32, "a_m4")
    for b in range(2):
        kb.wait("gpsimd", cm.sT, cm.sT.n)
        kb.dma("gpsimd", Kb_sb[:].rearrange("p (r t) -> p r t", t=1024),
               a2a_out.ap().rearrange("(r q p) t -> q p r t", q=6, p=128)
               [4][:, 4 * b:4 * b + 4], sem=cm.sW)
        kb.dma("gpsimd", Vt_sb[:].rearrange("p (r t) -> p r t", t=1024),
               a2a_out.ap().rearrange("(r q p) t -> q p r t", q=6, p=128)
               [5][:, 4 * b:4 * b + 4], sem=cm.sW)
        wv = cm.sW.n
        for eng in ("tensor", "vector", "scalar"):
            kb.wait(eng, cm.sW, wv)
        for vt in range(32):
            if vt >= 1:
                kb.wait("tensor", cm.sA)
            kb.o("tensor", "transpose",
                 cm.pb[7][:, 64 * (vt % 2):64 * (vt % 2) + 64].bitcast(BF16),
                 Vt_sb[:, vt * 128:(vt + 1) * 128], idb_sb[:], sem=cm.sT)
            kb.wait("scalar", cm.sT)
            kb.o("scalar", "copy", Vt_sb[:, vt * 128:(vt + 1) * 128],
                 cm.pb[7][:, 64 * (vt % 2):64 * (vt % 2) + 64].bitcast(BF16),
                 sem=cm.sA)
        for qh in range(2):
            kb.wait("gpsimd", cm.sT, cm.sT.n)
            for h in range(4):
                for rr in range(2):
                    kb.dma("gpsimd",
                           qI[:].rearrange("p (r blk hh q) -> p hh r blk q",
                                           r=2, blk=8, hh=4)[:, h, rr],
                           a2a_out.ap()
                           .rearrange("(r q p) t -> q p r t", q=6, p=128)
                           [h][:, 4 * b + 2 * qh + rr]
                           .rearrange("p (blk q) -> p blk q", q=128),
                           sem=cm.sW)
            wv2 = cm.sW.n
            kb.wait("tensor", cm.sW, wv2)
            for P in range(16):
                Q = qh * 16 + P
                stri = list(range(3, Q - 8, 4))
                wnd = list(range(max(0, Q - 8), Q + 1))
                tiles = stri + wnd
                nt = len(tiles)

                def tmask(kp):
                    if kp in stri:
                        return 5
                    if Q >= 8 and kp == Q - 8:
                        return 5 if Q % 4 == 3 else 4
                    if kp == Q:
                        return 3
                    return None

                for h in range(4):
                    kb.wait("tensor", cm.sV)
                    kb.wait("tensor", cm.sA)
                    lq = qI[:, P * 512 + h * 128:P * 512 + (h + 1) * 128]
                    for ti, kp in enumerate(tiles):
                        kb.o("tensor", "matmul", PS[:, ti * 128:(ti + 1) * 128],
                             lq, Kb_sb[:, kp * 128:(kp + 1) * 128],
                             start=True, stop=True,
                             sem=(cm.sT if ti == nt - 1 else None))
                    kb.wait("vector", cm.sT)
                    for ti, kp in enumerate(tiles):
                        mi = tmask(kp)
                        reg = PS[:, ti * 128:(ti + 1) * 128]
                        if mi is not None:
                            kb.o("vector", "scalar_tensor_tensor", reg, reg,
                                 SM, mkT(kb, mi), ALU.mult, ALU.add)
                        else:
                            kb.o("vector", "tensor_scalar", reg, reg, SM,
                                 None, ALU.mult)
                    kb.o("vector", "reduce_max", m4[:, h:h + 1],
                         PS[:, 0:nt * 128], axis=AX, bar=True)
                    cm.pub("vector", cm.sV)
                kb.wait("tensor", cm.sV)
                for h in range(4):
                    kb.o("tensor", "transpose",
                         cm.pb[7][0:1, h * 128:(h + 1) * 128],
                         m4[:, h:h + 1], idf_sb[:],
                         sem=(cm.sT if h == 3 else None))
                kb.wait("vector", cm.sT)
                kb.o("vector", "tensor_scalar", mrow_sb[0:1, :],
                     cm.pb[7][0:1, 0:512], -1.0, None, ALU.mult, bar=True)
                cm.pub("vector", cm.sV)
                kb.wait("tensor", cm.sV)
                kb.o("tensor", "matmul", cm.pb[7][:], cm.one_row[:],
                     mrow_sb[0:1, :], start=True, stop=True, sem=cm.sT)
                kb.wait("scalar", cm.sT)
                kb.o("scalar", "copy", mbc[:], cm.pb[7][:], sem=cm.sA)
                kb.wait("vector", cm.sA)
                for ti, kp in enumerate(tiles):
                    if ti > 0:
                        kb.wait("tensor", cm.sA, aexp)
                    else:
                        kb.wait("tensor", cm.sV)
                    kb.o("tensor", "matmul", cm.pb[4][:],
                         Kb_sb[:, kp * 128:(kp + 1) * 128],
                         qI[:, P * 512:(P + 1) * 512],
                         start=True, stop=True, sem=cm.sT)
                    kb.wait("vector", cm.sT)
                    mi = tmask(kp)
                    if mi is not None:
                        kb.o("vector", "scalar_tensor_tensor", t_sb[:],
                             cm.pb[4][:], SM, mk(mi), ALU.mult, ALU.add)
                        kb.o("vector", "tensor_sub", t_sb[:], t_sb[:], mbc[:],
                             sem=cm.sV)
                    else:
                        kb.o("vector", "scalar_tensor_tensor", t_sb[:],
                             cm.pb[4][:], SM, mbc[:], ALU.mult, ALU.subtract,
                             sem=cm.sV)
                    kb.wait("scalar", cm.sV)
                    kb.o("scalar", "activation", pT_sb[:], t_sb[:], AF.Exp,
                         sem=cm.sA)
                    aexp = cm.sA.n
                    kb.wait("tensor", cm.sA, aexp)
                    kb.o("tensor", "matmul", cm.pb[5][:],
                         Vt_sb[:, kp * 128:(kp + 1) * 128], pT_sb[:],
                         start=(ti == 0), stop=(ti == nt - 1))
                    kb.o("tensor", "matmul", cm.pb[6][0:1, 0:512],
                         cm.ones_col_bf[:], pT_sb[:],
                         start=(ti == 0), stop=(ti == nt - 1),
                         sem=(cm.sT if ti == nt - 1 else None))
                kb.wait("vector", cm.sT)
                kb.o("vector", "reciprocal", srow[0:1, :],
                     cm.pb[6][0:1, 0:512], bar=True)
                cm.pub("vector", cm.sV)
                kb.wait("tensor", cm.sV)
                kb.o("tensor", "matmul", cm.pb[7][:], cm.one_row[:],
                     srow[0:1, :], start=True, stop=True, sem=cm.sT)
                kb.wait("scalar", cm.sT)
                kb.o("scalar", "copy", rbc[:], cm.pb[7][:], sem=cm.sA)
                kb.wait("vector", cm.sA)
                if P > 0 or qh > 0 or b > 0:
                    kb.wait("vector", cm.sS, cm.sS.n)
                kb.o("vector", "tensor_tensor", ctx_sb[:], cm.pb[5][:],
                     rbc[:], ALU.mult, sem=cm.sV)
                kb.wait("sync", cm.sV)
                gt0 = qh * 2048 + P * 128
                r = 4 * b + gt0 // 1024
                tl = gt0 % 1024
                kb.dma("sync",
                       a2b_in.ap()[r * 512:(r + 1) * 512, tl:tl + 128]
                       .rearrange("(hh d) q -> d hh q", d=128),
                       ctx_sb[:].rearrange("d (hh q) -> d hh q", q=128),
                       sem=cm.sS)
    kb.free(Kb_sb, Vt_sb, qI, ctx_sb, t_sb, pT_sb, m4, kb._mks,
            idf_sb, idb_sb, mrow_sb, mbc, rbc, srow)


def emit_l1attn_dense(kb, cm, io, a2b_out):
    ctx_all = [kb.sbt([128, T], BF16, "a_ctxall") for _ in range(KH)]
    bd_sb = kb.sbt([128, KH], F32, "a_bd")
    xres = [kb.sbt([128, 512], F32, "a_xres") for _ in range(4)]
    xo = [kb.sbt([128, 512], F32, "a_xo") for _ in range(2)]
    for eng in ("tensor", "vector", "scalar", "gpsimd"):
        kb.wait(eng, cm.sM)
    for k in range(KH):
        kb.dma("gpsimd", ctx_all[k][:], a2b_out.ap()[k * 128:(k + 1) * 128, :],
               sem=cm.sW)
    kb.dma("sync", bd_sb[:], io["bd"].ap(), sem=cm.sX)
    ldw2, ldx2 = cm.sW.n, cm.sX.n
    for eng in ("tensor", "vector"):
        kb.wait(eng, cm.sW, ldw2)
        kb.wait(eng, cm.sX, ldx2)
    sEvd = kb.sem("a_sEvd")
    sbase = cm.sS.n
    xbase = cm.sX.n
    for jj in range(3):
        of, sf = divmod(jj, 2)
        kb.dma("sync", xres[jj % 4][:],
               io["xin"].ap()[of][:, sf * 512:(sf + 1) * 512], sem=cm.sX)

    def evac_d(j, o, s0, sl, ps):
        jf = j + 3
        if jf < 64:
            of, sf = divmod(jf, 2)
            if j > 0:
                kb.wait("sync", sEvd, j)
            kb.dma("sync", xres[jf % 4][:],
                   io["xin"].ap()[of][:, sf * 512:(sf + 1) * 512], sem=cm.sX)
        kb.wait("vector", cm.sT)
        kb.wait("vector", cm.sX, xbase + 16 * (j + 1))
        if j >= 2:
            kb.wait("vector", cm.sS, sbase + 16 * (j - 1))
        kb.o("vector", "scalar_tensor_tensor", xo[j % 2][:, 0:sl], ps,
             bd_sb[:, o:o + 1], xres[j % 4][:, 0:sl], ALU.add, ALU.add,
             sem=sEvd)
        kb.wait("sync", sEvd, sEvd.n)
        kb.dma("sync", io["xout"].ap()[o][:, s0:s0 + sl], xo[j % 2][:, 0:sl],
               sem=cm.sS)

    emit_dense(kb, cm, sEvd, io["wdo"], KH, KH,
               lambda k, s0, sl: ctx_all[k][:, s0:s0 + sl], T, evac_d,
               dbanks=(4, 5))


def mkT(kb, mi):
    return kb._mks[:, (mi - 3) * 512:(mi - 3) * 512 + 128]


def _bs_masks():
    NEG = -1e30
    k = np.arange(128)[:, None]
    q = np.arange(128)[None, :]
    vis_diag = ((k < 64) & (q < 64) & (k <= q)) | ((k < 64) & (q >= 64)) | \
               ((k >= 64) & (q >= 64) & (k - 64 <= q - 64))
    vis_edge = (k >= 64) & (q < 64)
    vis_stride = np.broadcast_to(k >= 64, (128, 128))
    out = np.zeros((6, 128, 512), np.float32)
    for i, vis in enumerate([vis_diag, vis_edge, vis_stride]):
        m = np.where(vis, 0.0, NEG).astype(np.float32)
        out[i, :, 0:128] = m.T
        out[3 + i] = np.tile(m, (1, 4))
    return out


def host_prep_l1attn(inputs, x_fm):
    wqkv_t = tile_w_omajor(np.ascontiguousarray(inputs["L1_attn_w_qkv"].T))
    bqkv_t = col_layout(inputs["L1_attn_b_qkv"])
    wdo_t = tile_w_omajor(np.ascontiguousarray(inputs["L1_attn_w_dense"].T))
    bd_t = col_layout(inputs["L1_attn_b_dense"])
    inv_freq = 1.0 / (1e6 ** (np.arange(0, 128, 2, np.float32) / 128.0))
    masks_t = _bs_masks()
    idf_t = np.eye(128, dtype=np.float32)
    maps = []
    for c in range(NCORES):
        t0 = (c % 4) * T
        pos = np.arange(t0, t0 + T, dtype=np.float32)
        fr = np.outer(inv_freq, pos)
        cos_t = np.concatenate([np.cos(fr), np.cos(fr)], 0).astype(np.float32)
        sin_t = np.concatenate([-np.sin(fr), np.sin(fr)], 0).astype(np.float32)
        maps.append({
            "xin": np.ascontiguousarray(x_fm[c]), "wqkv": wqkv_t,
            "bqkv": bqkv_t, "wdo": wdo_t, "bd": bd_t, "cosd": cos_t,
            "sind": sin_t, "masks": masks_t, "idf": idf_t,
        })
    return maps


def build_l0attn(debug=False, upto=99):
    nc = bass.Bass("TRN2", target_bir_lowering=False,
                   detect_race_conditions=DETECT_RACES)
    xT = nc.dram_tensor("xT", [KH, 128, TH], F32, kind="ExternalInput")
    w1t = nc.dram_tensor("w1t", [145, KH * 128, 128], F32, kind="ExternalInput")
    convw = nc.dram_tensor("convw", [128, 80 * 4], F32, kind="ExternalInput")
    convb = nc.dram_tensor("convb", [128, 80], F32, kind="ExternalInput")
    dtb = nc.dram_tensor("dtb", [128, 1], F32, kind="ExternalInput")
    alog = nc.dram_tensor("alog", [128, 1], F32, kind="ExternalInput")
    DD = nc.dram_tensor("DD", [1, 128], F32, kind="ExternalInput")
    normw = nc.dram_tensor("normw", [128, 64], F32, kind="ExternalInput")
    opw = nc.dram_tensor("opw", [KH, 64 * 128, 128], F32, kind="ExternalInput")
    upper = nc.dram_tensor("upper", [128, 128], F32, kind="ExternalInput")
    maskui = nc.dram_tensor("maskui", [128, 128], F32, kind="ExternalInput")
    idf = nc.dram_tensor("idf", [128, 128], F32, kind="ExternalInput")
    msel = nc.dram_tensor("msel", [128, 16], F32, kind="ExternalInput")
    selbc = nc.dram_tensor("selbc", [128, 4], F32, kind="ExternalInput")
    xout = nc.dram_tensor("xout", [KH, 128, T], F32, kind="ExternalOutput")
    kb = KB(nc)
    cm = Common(kb, groups=(2, 2, 1, 1, 2))
    emit_l0attn(kb, cm, dict(xT=xT, w1t=w1t, convw=convw, convb=convb, dtb=dtb,
                             alog=alog, DD=DD, normw=normw, opw=opw,
                             upper=upper, maskui=maskui, idf=idf, msel=msel,
                             selbc=selbc, xout=xout), upto=upto)
    kb.wait("sync", cm.sS)
    kb.finalize()
    return nc


def emit_l0attn(kb, cm, io, upto=99):
    nc = kb.nc
    ag_in = nc.dram_tensor(kb.name("m_agin"), [128, 8192], F32, kind="Internal")
    ag_out = nc.dram_tensor(kb.name("m_agout"), [512, 8192], F32, kind="Internal")
    aga_in = nc.dram_tensor(kb.name("m_againa"), [128, 64], F32, kind="Internal")
    aga_out = nc.dram_tensor(kb.name("m_agouta"), [512, 64], F32, kind="Internal")
    y3loc = nc.dram_tensor(kb.name("m_y3"), [64, 128, 1024], BF16, kind="Internal")
    sgt = nc.dram_tensor(kb.name("m_sgt"), [64, 128, 1024], BF16, kind="Internal")
    yfd = nc.dram_tensor(kb.name("m_yf"), [64, 128, 1024], BF16, kind="Internal")

    convw_sb = kb.sbt([128, 320], F32, "m_cw")
    convb_sb = kb.sbt([128, 80], F32, "m_cb")
    dtb_sb = kb.sbt([128, 1], F32, "m_dtb")
    alog_sb = kb.sbt([128, 1], F32, "m_alog")
    dd_row = kb.sbt([1, 128], F32, "m_ddrow")
    normw_sb = kb.sbt([128, 64], F32, "m_nw")
    upper_sb = kb.sbt([128, 128], BF16, "m_upb")
    maskui_bf = kb.sbt([128, 128], BF16, "m_muib")
    idf_sb = kb.sbt([128, 128], F32, "m_idf")
    idb_sb = kb.sbt([128, 128], BF16, "m_idb")
    msel_sb = kb.sbt([128, 16], F32, "m_msel")
    selbc_sb = kb.sbt([128, 4], F32, "m_selbc")
    ddbc = kb.sbt([128, 128], F32, "m_ddbc")
    dt_tok = kb.sbt([128, T], F32, "m_dttok")
    a_tok = kb.sbt([128, T], F32, "m_atok")
    sd_tok = kb.sbt([128, T], F32, "m_sdtok")
    dtds2 = kb.sbt([128, T], F32, "m_dtds2")
    sdp_tok = kb.sbt([128, T], F32, "m_sdp")
    dtot_row = kb.sbt([1, 1024], F32, "m_dtot")
    dtotbc = kb.sbt([128, 1024], F32, "m_dtotbc")
    pc_row = kb.sbt([1, 1024], F32, "m_pc")
    aczero = kb.sbt([128, 64], F32, "m_acz")
    C_all = kb.sbt([128, 8 * 1024], BF16, "m_call")
    S_f = kb.sbt([128, 1024], F32, "m_sf")
    S_bf = kb.sbt([128, 1024], BF16, "m_sbf")
    wdense = [kb.sbt([128, KH * 128], BF16, "m_dw") for _ in range(2)]
    xb = [kb.sbt([128, TH], BF16, "m_xb") for _ in range(KH)]
    mb = kb.sbt([128, TH], BF16, "m_mb")
    rb = kb.sbt([128, TH], BF16, "m_rb")

    for k in range(KH):
        kb.dma("gpsimd", xb[k][:], io["xT"].ap()[k], sem=cm.sW)
    kb.dma("gpsimd", upper_sb[:], io["upper"].ap(), sem=cm.sW)
    kb.dma("gpsimd", idb_sb[:], io["idf"].ap(), sem=cm.sW)
    kb.dma("gpsimd", maskui_bf[:], io["maskui"].ap(), sem=cm.sW)
    for nm, t in (("convw", convw_sb), ("convb", convb_sb), ("dtb", dtb_sb),
                  ("alog", alog_sb), ("DD", dd_row), ("normw", normw_sb),
                  ("idf", idf_sb), ("msel", msel_sb),
                  ("selbc", selbc_sb)):
        kb.dma("sync", t[:], io[nm].ap(), sem=cm.sX)
    ldw, ldx = cm.sW.n, cm.sX.n
    for eng in ("tensor", "vector", "scalar"):
        kb.wait(eng, cm.sW, ldw)
        kb.wait(eng, cm.sX, ldx)
        kb.wait(eng, cm.sM)

    emit_ln(kb, cm, xb, TH, mb, rb, banks=(0, 2, 4, 5))
    kb.wait("vector", cm.sV)
    kb.free(rb, mb)

    if upto < 1.2:
        return
    kb.wait("tensor", cm.sV)
    kb.o("tensor", "matmul", cm.pb[4][:, 0:128], cm.one_row[:], dd_row[0:1, :],
         start=True, stop=True, sem=cm.sT)
    kb.wait("scalar", cm.sT)
    kb.o("scalar", "copy", ddbc[:], cm.pb[4][:, 0:128], sem=cm.sA)

    dtraw = kb.sbt([128, TH], F32, "m_dtraw")
    clr = kb.sbt([1, 1024], F32, "m_clr")
    prow = [kb.sbt([1, 1024], F32, "m_pr") for _ in range(2)]
    dt_fm = kb.sbt([128, T], F32, "m_dtfm")
    a_fm = kb.sbt([128, T], F32, "m_afm")
    cum_fm = kb.sbt([128, T], F32, "m_cumfm")
    zer = kb.sbt([128, 128], F32, "m_zer")
    scr2 = kb.sbt([128, 128], F32, "m_scr2")
    kb.o("gpsimd", "memset", zer[:], 0.0, sem=cm.sM)
    slabs = [(s, min(512, TH - s)) for s in range(0, TH, 512)]
    if upto < 1.3:
        return
    kb.wait("gpsimd", cm.sW)
    kb.dma("gpsimd", wdense[0][:].rearrange("p (k c) -> p k c", c=128),
           io["w1t"].ap()[0].rearrange("(k p) c -> p k c", p=128), sem=cm.sW)
    wv = cm.sW.n
    kb.wait("tensor", cm.sW, wv)
    kb.wait("tensor", cm.sA)
    for si, (s0, sl) in enumerate(slabs):
        ip = 4 + si % 2
        if si >= 2:
            kb.wait("tensor", cm.sA)
        for k in range(KH):
            kb.o("tensor", "matmul", cm.pb[ip][:, 0:sl],
                 wdense[0][:, k * 128:(k + 1) * 128], xb[k][:, s0:s0 + sl],
                 start=(k == 0), stop=(k == KH - 1),
                 sem=(cm.sT if k == KH - 1 else None))
        kb.wait("scalar", cm.sT)
        kb.o("scalar", "copy", dtraw[:, s0:s0 + sl], cm.pb[ip][:, 0:sl],
             sem=cm.sA)
    if upto < 1.35:
        return
    kb.wait("vector", cm.sA)
    kb.o("vector", "tensor_scalar", dtraw[:, 0:T], dtraw[:, HALO:TH],
         dtb_sb[:, 0:1], None, ALU.add, bar=True)
    kb.o("vector", "tensor_scalar", dt_fm[:], dtraw[:, 0:T], 0.0, None,
         ALU.max, bar=True)
    cm.pub("vector", cm.sV)
    kb.wait("scalar", cm.sV)
    kb.o("scalar", "activation", a_fm[:], dtraw[:, 0:T], AF.Abs, bar=True)
    kb.o("scalar", "activation", a_fm[:], a_fm[:], AF.Exp, scale=-1.0,
         bar=True)
    kb.o("scalar", "activation", a_fm[:], a_fm[:], AF.Ln,
         bias=cm.c1(), bar=True)
    kb.o("scalar", "activation", scr2[:, 0:1], alog_sb[:], AF.Exp, sem=cm.sA)
    kb.wait("vector", cm.sA)
    kb.o("vector", "tensor_add", dt_fm[:], dt_fm[:], a_fm[:], bar=True)
    kb.o("vector", "tensor_scalar", scr2[:, 1:2], scr2[:, 0:1], -1.0, None,
         ALU.mult, bar=True)
    kb.o("vector", "tensor_scalar", a_fm[:], dt_fm[:], scr2[:, 1:2], None,
         ALU.mult, bar=True)
    cm.pub("vector", cm.sV)
    for c in range(8):
        kb.o("vector", "tensor_tensor_scan", cum_fm[:, c * 128:(c + 1) * 128],
             a_fm[:, c * 128:(c + 1) * 128], zer[:], 0.0, ALU.add, ALU.add,
             sem=(cm.sV if c == 7 else None))
    kb.wait("tensor", cm.sV)
    tts = [(a_fm, a_tok), (dt_fm, dt_tok), (cum_fm, None)]
    cum_tok = kb.sbt([128, T], F32, "m_cumtok")
    tts[2] = (cum_fm, cum_tok)
    for src, dst in tts:
        for c in range(8):
            kb.wait("tensor", cm.sA)
            kb.o("tensor", "transpose", cm.pb[4][:, 0:128],
                 src[:, c * 128:(c + 1) * 128], idf_sb[:], sem=cm.sT)
            kb.wait("scalar", cm.sT)
            kb.o("scalar", "copy", dst[:, c * 128:(c + 1) * 128],
                 cm.pb[4][:, 0:128], sem=cm.sA)
    if upto < 1.4:
        return
    kb.wait("gpsimd", cm.sA)
    kb.dma("gpsimd", clr[:], cum_tok[127:128, :], sem=cm.sG)
    kb.wait("scalar", cm.sG)
    kb.o("scalar", "activation", dtot_row[0:1, :], clr[0:1, :], AF.Exp,
         sem=cm.sA)
    for si in range(2):
        kb.wait("tensor", cm.sA)
        kb.o("tensor", "matmul", cm.pb[6 + si][:], cm.one_row[:],
             dtot_row[0:1, si * 512:(si + 1) * 512], start=True, stop=True,
             sem=cm.sT)
        kb.wait("scalar", cm.sT)
        kb.o("scalar", "copy", dtotbc[:, si * 512:(si + 1) * 512],
             cm.pb[6 + si][:], sem=cm.sA)
    kb.wait("vector", cm.sG)
    kb.o("vector", "tensor_scalar", prow[0][0:1, :], clr[0:1, :], 1.0, None,
         ALU.mult, bar=True)
    src_i = 0
    for sh in (128, 256, 512):
        dst_i = 1 - src_i
        kb.o("vector", "tensor_scalar", prow[dst_i][0:1, 0:sh],
             prow[src_i][0:1, 0:sh], 1.0, None, ALU.mult)
        kb.o("vector", "tensor_add", prow[dst_i][0:1, sh:1024],
             prow[src_i][0:1, sh:1024], prow[src_i][0:1, 0:1024 - sh],
             bar=True)
        src_i = dst_i
    kb.o("vector", "tensor_sub", prow[1 - src_i][0:1, :], prow[src_i][0:1, :],
         clr[0:1, :], bar=True)
    cm.pub("vector", cm.sV)
    kb.wait("scalar", cm.sV)
    kb.o("scalar", "activation", pc_row[0:1, :], prow[1 - src_i][0:1, :],
         AF.Exp)
    if upto < 1.45:
        return
    kb.o("scalar", "activation", sd_tok[:], cum_tok[:], AF.Exp, sem=cm.sA)
    for c in range(8):
        kb.wait("vector", cm.sA)
        kb.o("vector", "tensor_scalar", scr2[:], cum_fm[:, c * 128:(c + 1) * 128],
             cum_fm[:, c * 128 + 127:c * 128 + 128], None, ALU.subtract,
             sem=cm.sV)
        kb.wait("scalar", cm.sV)
        kb.o("scalar", "activation", scr2[:], scr2[:], AF.Exp, scale=-1.0,
             sem=cm.sA)
        kb.wait("tensor", cm.sA)
        if c > 0:
            kb.wait("tensor", cm.sV)
        kb.o("tensor", "transpose", cm.pb[4][:, 128:256], scr2[:], idf_sb[:],
             sem=cm.sT)
        kb.wait("vector", cm.sT)
        kb.o("vector", "tensor_copy", dtds2[:, c * 128:(c + 1) * 128],
             cm.pb[4][:, 128:256], sem=cm.sV)
    kb.wait("vector", cm.sA)
    kb.o("vector", "tensor_mul", dtds2[:], dtds2[:], dt_tok[:], sem=cm.sV)
    if upto < 1.47:
        return
    kb.o("vector", "memset", aczero[:], 0.0, bar=True)
    for c in range(8):
        kb.o("vector", "tensor_tensor", aczero[:, 0:1], aczero[:, 0:1],
             cum_fm[:, c * 128 + 127:c * 128 + 128], ALU.add, bar=True)
    cm.pub("vector", cm.sV)
    kb.wait("gpsimd", cm.sV)
    kb.dma("gpsimd", aga_in.ap(), aczero[:], sem=cm.sG)
    kb.wait("vector", cm.sA)
    kb.free(cum_tok, scr2, zer, cum_fm, a_fm, dt_fm, *prow, clr, dtraw)
    if upto < 1.5:
        return

    kb.o("vector", "memset", S_f[:], 0.0)
    kb.o("vector", "memset", S_bf[:], 0.0, sem=cm.sV)
    sgf = [kb.sbt([128, TH], BF16, "m_sgf") for _ in range(8)]
    convin = kb.sbt([128, TH], F32, "m_cvin")
    acc = kb.sbt([128, T], F32, "m_acc")
    xc = [kb.sbt([128, T], BF16, "m_xc") for _ in range(8)]
    B_fm = kb.sbt([128, T], BF16, "m_bfm")
    xtok = kb.sbt([128, 1024], BF16, "m_xtok")
    btok = kb.sbt([128, 128], BF16, "m_btok")
    gtm = kb.sbt([128, 128], BF16, "m_gtm")
    rhsl = kb.sbt([128, 128], BF16, "m_rhsl")
    expd = kb.sbt([128, 128], BF16, "m_expd")
    gmlt = kb.sbt([128, 128], BF16, "m_gmlt")
    y1 = kb.sbt([128, 1024], F32, "m_y1")
    xw = kb.sbt([128, 1024], BF16, "m_xw")
    sgt_sb = kb.sbt([128, 1024], BF16, "m_sgts")
    sgm = kb.sbt([128, T], BF16, "m_sgm")

    sEvm = kb.sem("m_sEvm")
    state = {"cvk": None}

    for g in range(8 if upto >= 1.55 else 1):
        o_order = [1 + g * 18 + i for i in range(18)]

        def evac_m(j, o, s0, sl, ps, g=g):
            i = (o - 1) % 18
            si = j % 3
            kb.wait("scalar", sEvm)
            if upto < 1.65:
                kb.wait("scalar", cm.sT)
                kb.o("scalar", "copy", convin[:, s0:s0 + sl], ps, sem=sEvm)
                return
            if i < 8:
                kb.wait("scalar", cm.sT)
                kb.o("scalar", "activation", sgm[:, 0:sl], ps, AF.Sigmoid,
                     sem=cm.sA)
                kb.wait("vector", cm.sA)
                kb.o("vector", "tensor_tensor", sgf[i][:, s0:s0 + sl], ps,
                     sgm[:, 0:sl], ALU.mult, sem=sEvm)
            else:
                kb.wait("scalar", cm.sT)
                if si == 0 and i > 8:
                    kb.wait("scalar", cm.sV)
                kb.o("scalar", "copy", convin[:, s0:s0 + sl], ps,
                     sem=(sEvm if si < 2 else None))
                if si < 2:
                    pass
                else:
                    cidx = g * 10 + (i - 8)
                    kb.o("scalar", "copy", cm.cst[0:1, 3:4], cm.cst[0:1, 3:4],
                         sem=cm.sA)
                    kb.wait("vector", cm.sA)
                    kb.o("vector", "tensor_scalar", acc[:],
                         convin[:, 1:T + 1], convw_sb[:, cidx * 4:cidx * 4 + 1],
                         None, ALU.mult)
                    for kk in range(1, 4):
                        kb.o("vector", "scalar_tensor_tensor", acc[:],
                             convin[:, 1 + kk:T + 1 + kk],
                             convw_sb[:, cidx * 4 + kk:cidx * 4 + kk + 1],
                             acc[:], ALU.mult, ALU.add,
                             bar=(kk == 3))
                    cm.pub("vector", cm.sV)
                    kb.wait("scalar", cm.sV)
                    kb.o("scalar", "activation", sgm[:, 0:T], acc[:],
                         AF.Sigmoid, bias=convb_sb[:, cidx:cidx + 1],
                         bar=True)
                    cm.pub("scalar", cm.sA)
                    kb.wait("vector", cm.sA)
                    dst = (xc[i - 8] if i < 16 else
                           (B_fm if i == 16 else C_all))
                    dap = dst[:, 0:T] if i < 17 else dst[:, g * 1024:(g + 1) * 1024]
                    kb.o("vector", "scalar_tensor_tensor", dap, acc[:],
                         convb_sb[:, cidx:cidx + 1], sgm[:, 0:T], ALU.add,
                         ALU.mult, sem=sEvm)

        kb.wait("tensor", cm.sV)
        kb.wait("gpsimd", cm.sA)
        emit_dense(kb, cm, sEvm, io["w1t"], KH, 145,
                   lambda k, s0, sl: xb[k][:, s0:s0 + sl],
                   (1024 if upto < 1.63 else TH), evac_m,
                   o_order=o_order, dbanks=(4, 5), wb=wdense)
        for eng in ("tensor", "vector", "scalar", "gpsimd"):
            kb.wait(eng, sEvm)
            kb.wait(eng, cm.sV)
        if upto < 1.8:
            continue
        for c in range(8):
            kb.wait("scalar", cm.sG, cm.sG.n)
            for k in range(8):
                kb.wait("tensor", cm.sA)
                kb.o("tensor", "transpose",
                     cm.pb[4][:, 64 * (k % 2):64 * (k % 2) + 64].bitcast(BF16),
                     sgf[k][:, HALO + c * 128:HALO + (c + 1) * 128], idb_sb[:],
                     sem=cm.sT)
                kb.wait("scalar", cm.sT)
                kb.o("scalar", "copy", sgt_sb[:, k * 128:(k + 1) * 128],
                     cm.pb[4][:, 64 * (k % 2):64 * (k % 2) + 64].bitcast(BF16),
                     sem=cm.sA)
            kb.wait("gpsimd", cm.sA)
            kb.dma("gpsimd", sgt.ap()[g * 8 + c], sgt_sb[:], sem=cm.sG)

        if upto < 1.9:
            continue
        for c in range(8):
            kb.wait("tensor", cm.sV)
            for k in range(8):
                kb.wait("tensor", cm.sA)
                kb.o("tensor", "transpose",
                     cm.pb[4][:, 64 * (k % 2):64 * (k % 2) + 64].bitcast(BF16),
                     xc[k][:, c * 128:(c + 1) * 128], idb_sb[:], sem=cm.sT)
                kb.wait("scalar", cm.sT)
                kb.o("scalar", "copy", xtok[:, k * 128:(k + 1) * 128],
                     cm.pb[4][:, 64 * (k % 2):64 * (k % 2) + 64].bitcast(BF16),
                     sem=cm.sA)
            kb.wait("tensor", cm.sA)
            kb.o("tensor", "transpose", cm.pb[4][:, 0:64].bitcast(BF16),
                 B_fm[:, c * 128:(c + 1) * 128], idb_sb[:], sem=cm.sT)
            kb.wait("scalar", cm.sT)
            kb.o("scalar", "copy", btok[:], cm.pb[4][:, 0:64].bitcast(BF16),
                 sem=cm.sA)
            if upto < 1.92:
                continue
            kb.wait("tensor", cm.sA)
            kb.o("tensor", "matmul", cm.pb[5][:, 0:128],
                 B_fm[:, c * 128:(c + 1) * 128],
                 C_all[:, g * 1024 + c * 128:g * 1024 + (c + 1) * 128],
                 start=True, stop=True, sem=cm.sT)
            kb.wait("vector", cm.sT)
            kb.o("vector", "tensor_tensor", gtm[:], cm.pb[5][:, 0:128],
                 maskui_bf[:], ALU.mult, bar=True)
            cm.pub("vector", cm.sV)
            for h in range(16):
                hg = g * 16 + h
                kb.o("vector", "tensor_scalar", rhsl[:], maskui_bf[:],
                     a_tok[:, c * 128 + hg:c * 128 + hg + 1], None, ALU.mult,
                     bar=True)
                cm.pub("vector", cm.sV)
                kb.wait("tensor", cm.sV)
                if h > 0:
                    kb.wait("tensor", cm.sA)
                kb.o("tensor", "matmul", cm.pb[5][:, 128:256], upper_sb[:],
                     rhsl[:], start=True, stop=True, sem=cm.sT)
                kb.wait("scalar", cm.sT)
                kb.o("scalar", "activation", expd[:], cm.pb[5][:, 128:256],
                     AF.Exp, bar=True)
                cm.pub("scalar", cm.sA)
                kb.wait("vector", cm.sA)
                kb.o("vector", "scalar_tensor_tensor", gmlt[:], expd[:],
                     dt_tok[:, c * 128 + hg:c * 128 + hg + 1], gtm[:],
                     ALU.mult, ALU.mult, bar=True)
                cm.pub("vector", cm.sV)
                if upto < 1.94:
                    continue
                kb.wait("tensor", cm.sV)
                kb.o("tensor", "matmul",
                     cm.tg[0][:, h * 64:(h + 1) * 64],
                     gmlt[:], xtok[:, h * 64:(h + 1) * 64],
                     start=(h == 0 or h == 8), stop=(h == 7 or h == 15),
                     sem=(cm.sT if h in (7, 15) else None))
                kb.o("tensor", "matmul",
                     cm.tg[1][:, h * 64:(h + 1) * 64],
                     C_all[:, g * 1024 + c * 128:g * 1024 + (c + 1) * 128],
                     S_bf[:, h * 64:(h + 1) * 64],
                     start=(h == 0 or h == 8), stop=(h == 7 or h == 15),
                     sem=(cm.sT if h in (7, 15) else None))
            if upto < 1.96:
                continue
            kb.wait("vector", cm.sT)
            kb.wait("vector", cm.sG, cm.sG.n)
            for h in range(16):
                hg = g * 16 + h
                kb.o("vector", "tensor_scalar", y1[:, h * 64:(h + 1) * 64],
                     cm.tg[1][:, h * 64:(h + 1) * 64],
                     sd_tok[:, c * 128 + hg:c * 128 + hg + 1], None, ALU.mult)
            kb.o("vector", "tensor_tensor", y1[:], y1[:], cm.tg[0][:, 0:1024],
                 ALU.add, bar=True)
            for h in range(16):
                hg = g * 16 + h
                kb.o("vector", "scalar_tensor_tensor", y1[:, h * 64:(h + 1) * 64],
                     xtok[:, h * 64:(h + 1) * 64], ddbc[:, hg:hg + 1],
                     y1[:, h * 64:(h + 1) * 64], ALU.mult, ALU.add,
                     bar=(h == 15))
            cm.pub("vector", cm.sV)
            kb.wait("gpsimd", cm.sV)
            kb.dma("gpsimd", y3loc.ap()[g * 8 + c], y1[:], sem=cm.sG)
            if upto < 1.98:
                continue
            for h in range(16):
                hg = g * 16 + h
                kb.o("vector", "tensor_scalar", xw[:, h * 64:(h + 1) * 64],
                     xtok[:, h * 64:(h + 1) * 64],
                     dtds2[:, c * 128 + hg:c * 128 + hg + 1], None, ALU.mult,
                     bar=(h == 15))
            cm.pub("vector", cm.sV)
            kb.wait("tensor", cm.sV)
            for h in range(16):
                kb.o("tensor", "matmul",
                     cm.tg[4][:, h * 64:(h + 1) * 64],
                     btok[:], xw[:, h * 64:(h + 1) * 64],
                     start=(h == 0 or h == 8), stop=(h == 7 or h == 15),
                     sem=(cm.sT if h in (7, 15) else None))
            if upto < 1.99:
                continue
            kb.wait("vector", cm.sT)
            for h in range(16):
                hg = g * 16 + h
                kb.o("vector", "tensor_scalar", S_f[:, h * 64:(h + 1) * 64],
                     S_f[:, h * 64:(h + 1) * 64],
                     dtotbc[:, c * 128 + hg:c * 128 + hg + 1], None, ALU.mult,
                     bar=(h == 15))
            kb.o("vector", "tensor_tensor", S_f[:], S_f[:],
                 cm.tg[4][:, 0:1024], ALU.add, bar=True)
            kb.o("vector", "tensor_copy", S_bf[:], S_f[:], bar=True)
            cm.pub("vector", cm.sV)
        if upto < 1.995:
            continue
        kb.wait("gpsimd", cm.sV)
        kb.dma("gpsimd", ag_in.ap()[:, g * 1024:(g + 1) * 1024], S_f[:],
               sem=cm.sG)
        if upto < 1.997:
            continue
        kb.wait("vector", cm.sG)
        kb.o("vector", "memset", S_f[:], 0.0)
        kb.o("vector", "memset", S_bf[:], 0.0, sem=cm.sV)
    kb.free(*sgf, convin, acc, *xc, B_fm, xtok, btok, gtm, rhsl, expd, gmlt,
            xw, sgt_sb, sgm, y1)
    kb.free(*xb, *wdense)
    if upto < 3:
        return

    kb.wait("gpsimd", cm.sG)
    kb.wait("gpsimd", cm.sV)
    kb.cc("AllGather", ALU.bypass, [[0, 1, 2, 3], [4, 5, 6, 7]],
          [ag_in.ap().opt()], [ag_out.ap().opt()], cm.sCC)
    kb.cc("AllGather", ALU.bypass, [[0, 1, 2, 3], [4, 5, 6, 7]],
          [aga_in.ap().opt()], [aga_out.ap().opt()], cm.sCC)

    y1 = kb.sbt([128, 1024], F32, "m_y1b")
    acr = kb.sbt([128, 4], F32, "m_acr")
    wl = kb.sbt([128, 4], F32, "m_wl")
    wsr = kb.sbt([1, 512], F32, "m_wsr")
    wsrb = kb.sbt([128, 512], F32, "m_wsrb")
    srg = kb.sbt([128, 1024], F32, "m_srg")
    S_in = kb.sbt([128, 1024], F32, "m_sin")
    ylb = kb.sbt([128, 1024], F32, "m_ylb")
    sgtb = kb.sbt([128, 1024], F32, "m_sgtb")
    ssq = kb.sbt([128, 2], F32, "m_ssq")
    sqscr = kb.sbt([128, 1024], BF16, "m_sqscr")
    ygn = kb.sbt([128, 1024], BF16, "m_ygn")
    yf_sb = kb.sbt([128, 1024], BF16, "m_yfsb")
    for eng in ("tensor", "vector", "scalar", "gpsimd", "sync"):
        kb.wait(eng, cm.sCC)
    for r in range(4):
        kb.dma("sync", acr[:, r:r + 1], aga_out.ap()[r * 128:(r + 1) * 128,
                                                      0:1], sem=cm.sX,
               allow_slow_non_contiguous=True)
    kb.wait("vector", cm.sX, cm.sX.n)
    kb.o("vector", "memset", wl[:], 0.0, bar=True)
    for r in range(4):
        for m in range(4):
            kb.o("vector", "scalar_tensor_tensor", wl[:, r:r + 1],
                 acr[:, m:m + 1], msel_sb[:, 4 * r + m:4 * r + m + 1],
                 wl[:, r:r + 1], ALU.mult, ALU.add, bar=True)
    cm.pub("vector", cm.sV)
    kb.wait("scalar", cm.sV)
    kb.o("scalar", "activation", wl[:], wl[:], AF.Exp, bar=True)
    cm.pub("scalar", cm.sA)
    kb.wait("vector", cm.sA)
    kb.o("vector", "tensor_mul", wl[:], wl[:], selbc_sb[:], bar=True)
    cm.pub("vector", cm.sV)
    kb.wait("tensor", cm.sV)
    for r in range(4):
        kb.o("tensor", "transpose", cm.tg[4][0:1, r * 128:r * 128 + 128],
             wl[:, r:r + 1], idf_sb[:], sem=(cm.sT if r == 3 else None))
    kb.wait("scalar", cm.sT)
    kb.o("scalar", "copy", wsr[0:1, :], cm.tg[4][0:1, 0:512], sem=cm.sA)
    cm.pub("scalar", cm.sA)
    kb.wait("tensor", cm.sA)
    kb.o("tensor", "matmul", cm.pb[5][:], cm.one_row[:], wsr[0:1, :],
         start=True, stop=True, sem=cm.sT)
    kb.wait("scalar", cm.sT)
    kb.o("scalar", "copy", wsrb[:], cm.pb[5][:], sem=cm.sA)
    for si in range(2):
        kb.wait("tensor", cm.sA)
        kb.wait("tensor", cm.sV)
        kb.o("tensor", "matmul", cm.pb[4 + si][:], cm.one_row[:],
             pc_row[0:1, si * 512:(si + 1) * 512], start=True, stop=True,
             sem=cm.sT)
        kb.wait("vector", cm.sT)
        kb.o("vector", "tensor_tensor", sdp_tok[:, si * 512:(si + 1) * 512],
             sd_tok[:, si * 512:(si + 1) * 512],
             cm.pb[4 + si][:], ALU.mult, sem=cm.sV, bar=True)
    cm.pub("vector", cm.sV)

    sldy = {}
    for g in range(8):
        for r in range(4):
            kb.wait("sync", cm.sV, cm.sV.n)
            kb.dma("sync", srg[:], ag_out.ap()[r * 128:(r + 1) * 128,
                                               g * 1024:(g + 1) * 1024],
                   sem=cm.sX)
            kb.wait("vector", cm.sX, cm.sX.n)
            for h in range(16):
                hg = g * 16 + h
                col = wsrb[:, r * 128 + hg:r * 128 + hg + 1]
                if r == 0:
                    kb.o("vector", "tensor_scalar",
                         S_in[:, h * 64:(h + 1) * 64],
                         srg[:, h * 64:(h + 1) * 64], col, None, ALU.mult,
                         bar=(h == 15))
                else:
                    kb.o("vector", "scalar_tensor_tensor",
                         S_in[:, h * 64:(h + 1) * 64],
                         srg[:, h * 64:(h + 1) * 64], col,
                         S_in[:, h * 64:(h + 1) * 64], ALU.mult, ALU.add,
                         bar=(h == 15))
            cm.pub("vector", cm.sV)
        kb.o("vector", "tensor_copy", S_bf[:], S_in[:], bar=True)
        cm.pub("vector", cm.sV)
        for c in range(8):
            gc = g * 8 + c
            kb.wait("gpsimd", cm.sV, cm.sV.n)
            kb.dma("gpsimd", ylb[:], y3loc.ap()[gc], sem=cm.sW)
            kb.dma("gpsimd", sgtb[:], sgt.ap()[gc], sem=cm.sW)
            wv3 = cm.sW.n
            kb.wait("tensor", cm.sV)
            for h in range(16):
                kb.o("tensor", "matmul", cm.tg[0][:, h * 64:(h + 1) * 64],
                     C_all[:, g * 1024 + c * 128:g * 1024 + (c + 1) * 128],
                     S_bf[:, h * 64:(h + 1) * 64],
                     start=(h == 0 or h == 8), stop=(h == 7 or h == 15),
                     sem=(cm.sT if h in (7, 15) else None))
            kb.wait("vector", cm.sT)
            kb.wait("vector", cm.sW, wv3)
            for h in range(16):
                hg = g * 16 + h
                kb.o("vector", "tensor_scalar", y1[:, h * 64:(h + 1) * 64],
                     cm.tg[0][:, h * 64:(h + 1) * 64],
                     sdp_tok[:, c * 128 + hg:c * 128 + hg + 1], None,
                     ALU.mult)
            kb.o("vector", "tensor_add", y1[:], y1[:], ylb[:], bar=True)
            kb.o("vector", "tensor_mul", y1[:], y1[:], sgtb[:], bar=True)
            cm.pub("vector", cm.sV)
            kb.wait("scalar", cm.sV)
            kb.o("scalar", "activation", sqscr[:], y1[:], AF.Square,
                 accum_out=ssq[:, 0:1], bar=True)
            cm.pub("scalar", cm.sA)
            kb.wait("vector", cm.sA)
            kb.o("vector", "tensor_scalar", ssq[:, 0:1], ssq[:, 0:1],
                 1.0 / 1024.0, 1e-5, ALU.mult, ALU.add, bar=True)
            cm.pub("vector", cm.sV)
            kb.wait("scalar", cm.sV)
            kb.o("scalar", "activation", ssq[:, 1:2], ssq[:, 0:1], AF.Sqrt,
                 bar=True)
            cm.pub("scalar", cm.sA)
            kb.wait("vector", cm.sA)
            kb.o("vector", "reciprocal", ssq[:, 0:1], ssq[:, 1:2], bar=True)
            kb.o("vector", "tensor_scalar", ygn[:], y1[:], ssq[:, 0:1], None,
                 ALU.mult, bar=True)
            cm.pub("vector", cm.sV)
            kb.wait("tensor", cm.sV)
            kb.wait("vector", cm.sG, cm.sG.n)
            for k in range(8):
                kb.wait("tensor", cm.sV)
                kb.o("tensor", "transpose",
                     cm.pb[4][:, 64 * (k % 2):64 * (k % 2) + 64].bitcast(BF16),
                     ygn[:, k * 128:(k + 1) * 128], idb_sb[:], sem=cm.sT)
                kb.wait("vector", cm.sT)
                kb.o("vector", "tensor_scalar", yf_sb[:, k * 128:(k + 1) * 128],
                     cm.pb[4][:, 64 * (k % 2):64 * (k % 2) + 64].bitcast(BF16),
                     normw_sb[:, g * 8 + k:g * 8 + k + 1], None, ALU.mult,
                     sem=cm.sV)
            kb.wait("gpsimd", cm.sV)
            kb.dma("gpsimd", yfd.ap().rearrange("o p t -> p o t")
                   [:, g * 8:(g + 1) * 8, c * 128:(c + 1) * 128],
                   yf_sb[:].rearrange("p (k t) -> p k t", t=128), sem=cm.sG)
    kb.free(acr, wl, wsr, wsrb, srg, S_in, ylb, sgtb, ssq, sqscr, ygn,
            yf_sb, y1)
    kb.free(convw_sb, convb_sb, dtb_sb, alog_sb, dd_row, normw_sb, upper_sb,
            maskui_bf, idf_sb, idb_sb, msel_sb, selbc_sb, ddbc,
            dt_tok, a_tok, sd_tok, dtds2, sdp_tok, dtot_row, dtotbc, pc_row,
            aczero, C_all, S_f, S_bf)
    if upto < 4:
        return

    ySB = [kb.sbt([128, 1024], BF16, "m_ysb") for _ in range(64)]
    xres = [kb.sbt([128, 512], F32, "m_xres") for _ in range(4)]
    xo = [kb.sbt([128, 512], F32, "m_xo") for _ in range(2)]
    kb.wait("gpsimd", cm.sG)
    for kk in range(64):
        kb.dma("gpsimd", ySB[kk][:], yfd.ap()[kk], sem=cm.sW)
    ldw3 = cm.sW.n
    kb.wait("tensor", cm.sW, ldw3)
    sEvo = kb.sem("m_sEvo")
    xb2 = cm.sX.n
    for jj in range(3):
        of, sf = divmod(jj, 2)
        kb.dma("sync", xres[jj % 4][:],
               io["xT"].ap()[of][:, HALO + sf * 512:HALO + (sf + 1) * 512],
               sem=cm.sX)

    def evac_o(j, o, s0, sl, ps):
        jf = j + 3
        if jf < 64:
            of, sf = divmod(jf, 2)
            if j > 0:
                kb.wait("sync", sEvo, j)
            kb.dma("sync", xres[jf % 4][:],
                   io["xT"].ap()[of][:, HALO + sf * 512:HALO + (sf + 1) * 512],
                   sem=cm.sX)
        kb.wait("vector", cm.sT)
        kb.wait("vector", cm.sX, xb2 + 16 * (j + 1))
        if j >= 2:
            kb.wait("vector", cm.sS, 16 * (j - 1))
        kb.o("vector", "tensor_tensor", xo[j % 2][:, 0:sl], ps,
             xres[j % 4][:, 0:sl], ALU.add, sem=sEvo)
        kb.wait("sync", sEvo, sEvo.n)
        kb.dma("sync", io["xout"].ap()[o][:, s0:s0 + sl], xo[j % 2][:, 0:sl],
               sem=cm.sS)

    emit_dense(kb, cm, sEvo, io["opw"], 64, KH,
               lambda k, s0, sl: ySB[k][:, s0:s0 + sl], T, evac_o,
               dbanks=(4, 5))


def host_prep_l0attn(inputs, x):
    W1 = np.ascontiguousarray(inputs["L0_attn_in_proj_w"].T)
    w1tiles = tile_w_omajor(W1)
    order = [144]
    for g in range(8):
        order += [g * 8 + j for j in range(8)]
        order += [64 + g * 8 + j for j in range(8)]
        order += [128 + g, 136 + g]
    w1t = np.ascontiguousarray(w1tiles[order])
    cw = inputs["L0_attn_conv_w"][:, 0, :]
    cb = inputs["L0_attn_conv_b"]
    chan = []
    for g in range(8):
        chan += list(range(g * 1024, (g + 1) * 1024))
        chan += list(range(8192 + g * 128, 8192 + (g + 1) * 128))
        chan += list(range(9216 + g * 128, 9216 + (g + 1) * 128))
    chan = np.array(chan)
    cwr = cw[chan].reshape(80, 128, 4)
    convw_t = np.ascontiguousarray(cwr.transpose(1, 0, 2).reshape(128, 320))
    convb_t = np.ascontiguousarray(cb[chan].reshape(80, 128).T)
    dtb_t = inputs["L0_attn_dt_bias"].reshape(128, 1).astype(np.float32)
    alog_t = inputs["L0_attn_A_log"].reshape(128, 1).astype(np.float32)
    dd_t = inputs["L0_attn_D"].reshape(1, 128).astype(np.float32)
    normw_t = col_layout(inputs["L0_attn_norm_w"])
    opw_t = tile_w_omajor(np.ascontiguousarray(inputs["L0_attn_out_proj_w"].T))
    m_idx = np.arange(128)
    upper_t = (m_idx[:, None] > m_idx[None, :]).astype(np.float32)
    maskui_t = (m_idx[None, :] >= m_idx[:, None]).astype(np.float32)
    idf_t = np.eye(128, dtype=np.float32)
    maps = []
    for c in range(NCORES):
        b, krank = divmod(c, 4)
        t0 = krank * T
        sl = x[b, max(0, t0 - HALO):t0 + T, :]
        if t0 == 0:
            sl = np.concatenate([np.zeros((HALO, H), np.float32), sl], 0)
        xT_t = np.ascontiguousarray(sl.T).reshape(KH, 128, TH)
        mselm = np.zeros((4, 4), np.float32)
        selv = np.zeros(4, np.float32)
        for r in range(4):
            if r < krank:
                selv[r] = 1.0
                for m in range(r + 1, krank):
                    mselm[r, m] = 1.0
        msel_t = np.broadcast_to(mselm.reshape(1, 16), (128, 16)).astype(np.float32)
        selbc_t = np.broadcast_to(selv.reshape(1, 4), (128, 4)).astype(np.float32)
        maps.append({
            "xT": xT_t, "w1t": w1t, "convw": convw_t, "convb": convb_t,
            "dtb": dtb_t, "alog": alog_t, "DD": dd_t, "normw": normw_t,
            "opw": opw_t, "upper": upper_t, "maskui": maskui_t, "idf": idf_t,
            "msel": np.ascontiguousarray(msel_t),
            "selbc": np.ascontiguousarray(selbc_t),
        })
    return maps


# version 03 (final)
# speedup vs baseline: 1.2041x; 1.1055x over previous
import numpy as np
import ml_dtypes
NO_PSL = False
PH4 = 9
DETECT_RACES = True
from contextlib import ExitStack

import concourse.bass as bass
import concourse.mybir as mybir
from concourse.bass_utils import run_bass_kernel_spmd

F32 = mybir.dt.float32
BF16 = mybir.dt.bfloat16
AF = mybir.ActivationFunctionType
ALU = mybir.AluOpType
AX = mybir.AxisListType.X

NCORES = 8
T = 1024
HALO = 4
TH = T + HALO
H = 4096
KH = H // 128

BF = ml_dtypes.bfloat16


def tile_w_omajor(w):
    IN, OUT = w.shape
    t = w.reshape(IN // 128, 128, OUT // 128, 128).transpose(2, 0, 1, 3)
    return np.ascontiguousarray(t.reshape(OUT // 128, IN, 128))


def col_layout(v):
    return np.ascontiguousarray(np.asarray(v, np.float32).reshape(-1, 128).T)


class Sem:
    __slots__ = ("h", "n")

    def __init__(self, h):
        self.h = h
        self.n = 0


class KB:
    ENGS = ("tensor", "vector", "scalar", "gpsimd", "sync")

    def __init__(self, nc):
        self.nc = nc
        self.es = ExitStack()
        self.ops = {k: [] for k in self.ENGS}
        self._names = {}
        self._bars = {}
        self._open = {}
        self._seq = 0

    def name(self, p):
        i = self._names.get(p, 0)
        self._names[p] = i + 1
        return f"{p}_{i}"

    def sem(self, p="s"):
        return Sem(self.es.enter_context(self.nc.semaphore(self.name(p))))

    def sb(self, shape, dtype, p="t"):
        return self.es.enter_context(self.nc.sbuf_tensor(self.name(p), shape, dtype))

    def sbt(self, shape, dtype, p="t"):
        g = self.nc.sbuf_tensor(self.name(p), shape, dtype, side="right")
        h = g.__enter__()
        self._seq += 1
        self._open[id(h)] = (self._seq, g)
        return h

    def free(self, *hs):
        items = sorted((self._open.pop(id(h)) for h in hs), reverse=True,
                       key=lambda x: x[0])
        for _, g in items:
            g.__exit__(None, None, None)

    def psum(self, shape, dtype=F32, p="ps"):
        return self.es.enter_context(self.nc.psum_tensor(self.name(p), shape, dtype))

    def _push(self, eng, fn):
        self.ops[eng].append(fn)

    def wait(self, eng, sem, val=None):
        v = sem.n if val is None else val
        self._push(eng, lambda e, h=sem.h, v=v: e.wait_ge(h, v))

    def o(self, eng, meth, *a, sem=None, inc=1, bar=False, **kw):
        bsem = bval = None
        if bar and sem is None:
            if eng not in self._bars:
                self._bars[eng] = self.sem(f"bar_{eng}")
            sem, inc = self._bars[eng], 1
        if sem is not None:
            sem.n += inc
            if bar:
                bsem, bval = sem, sem.n
        def f(e, meth=meth, a=a, kw=kw, sem=sem, inc=inc, bsem=bsem, bval=bval):
            r = getattr(e, meth)(*a, **kw)
            if sem is not None:
                r.then_inc(sem.h, inc)
            if bsem is not None:
                e.wait_ge(bsem.h, bval)
        self._push(eng, f)

    def dma(self, eng, out, in_, sem=None, **kw):
        if sem is not None:
            sem.n += 16
        def f(e, out=out, in_=in_, kw=kw, sem=sem):
            r = e.dma_start(out=out, in_=in_, **kw)
            if sem is not None:
                r.then_inc(sem.h, 16)
        self._push(eng, f)

    def cc(self, kind, op, groups, ins, outs, sem):
        sem.n += 1
        def f(e, kind=kind, op=op, groups=groups, ins=ins, outs=outs, sem=sem):
            e.collective_compute(kind, op, replica_groups=groups,
                                 ins=ins, outs=outs).then_inc(sem.h, 1)
        self._push("gpsimd", f)

    def finalize(self):
        ops = self.ops
        with self.nc.Block() as blk:
            @blk.tensor
            def _(e):
                for f in ops["tensor"]:
                    f(e)

            @blk.vector
            def _(e):
                for f in ops["vector"]:
                    f(e)

            @blk.scalar
            def _(e):
                for f in ops["scalar"]:
                    f(e)

            @blk.gpsimd
            def _(e):
                for f in ops["gpsimd"]:
                    f(e)

            @blk.sync
            def _(e):
                for f in ops["sync"]:
                    f(e)
        for _, g in sorted(self._open.values(), reverse=True,
                           key=lambda x: x[0]):
            g.__exit__(None, None, None)
        self._open.clear()
        self.es.close()
        return self.nc


class Common:

    def __init__(self, kb, groups=(1, 1, 1, 1, 1, 1, 1, 1)):
        self.kb = kb
        self.sW = kb.sem("sW")
        self.sX = kb.sem("sX")
        self.sT = kb.sem("sT")
        self.sV = kb.sem("sV")
        self.sA = kb.sem("sA")
        self.sS = kb.sem("sS")
        self.sCC = kb.sem("sCC")
        self.sG = kb.sem("sG")
        self.sM = kb.sem("sM")
        assert sum(groups) == 8
        self.tg = [kb.psum([128, 512 * gsz], F32, f"ps{i}")
                   for i, gsz in enumerate(groups)]
        self.pb = []
        for t, gsz in zip(self.tg, groups):
            for i in range(gsz):
                self.pb.append(t[:, i * 512:(i + 1) * 512])
        self.ones_col_bf = kb.sb([128, 1], BF16, "ones_col_bf")
        self.one_row = kb.sb([1, 128], F32, "one_row")
        self.cst = kb.sb([128, 4], F32, "cst")
        kb.o("gpsimd", "memset", self.ones_col_bf[:], 1.0)
        kb.o("gpsimd", "memset", self.one_row[:], 1.0)
        kb.o("gpsimd", "memset", self.cst[:, 0:1], 1.0)
        kb.o("gpsimd", "memset", self.cst[:, 1:2], 1e-5)
        kb.o("gpsimd", "memset", self.cst[:, 2:3], -1.0)
        kb.o("gpsimd", "memset", self.cst[:, 3:4], 0.0, sem=self.sM)

    def pub(self, eng, sem, n=1):
        meth = "mul" if eng == "scalar" else "tensor_scalar_mul"
        for i in range(n):
            self.kb.o(eng, meth, self.cst[:, 3:4], self.cst[:, 3:4], 1.0,
                      sem=(sem if i == n - 1 else None), bar=True)

    def c1(self):
        return self.cst[:, 0:1]

    def ceps(self):
        return self.cst[:, 1:2]


def emit_ln(kb, cm, xb, n_tok, mb_sb, rb_sb, banks=(0, 1, 2, 3),
            keep_rows=False):
    nk = len(xb)
    slabs = [(s, min(512, n_tok - s)) for s in range(0, n_tok, 512)]
    mrow = kb.sb([1, n_tok], F32, "ln_mrow") if keep_rows else         kb.sbt([1, n_tok], F32, "ln_mrow")
    vrow = kb.sb([1, n_tok], F32, "ln_vrow") if keep_rows else         kb.sbt([1, n_tok], F32, "ln_vrow")
    sq = [kb.sbt([128, 512], BF16, "ln_sq") for _ in range(2)]
    m2 = kb.sbt([1, 512], F32, "ln_m2")
    rrow = kb.sbt([1, n_tok], F32, "ln_rrow")
    ps_a = cm.pb[banks[0]]
    ps_b = cm.pb[banks[1]]
    ps_bc = cm.pb[banks[2]]
    ps_bc2 = cm.pb[banks[3]]
    tvals = {}
    prev_stats_v = 0
    for si, (s0, sl) in enumerate(slabs):
        for k in range(nk):
            bq = k % 2
            pk = (si, k - 2) if k >= 2 else ((si - 1, nk - 2 + k) if si > 0 else None)
            if pk is not None:
                kb.wait("scalar", cm.sT, tvals[pk])
            kb.o("scalar", "activation", sq[bq][:, 0:sl], xb[k][:, s0:s0 + sl],
                 AF.Square, sem=cm.sA)
            kb.wait("tensor", cm.sA)
            if k == 0 and si > 0:
                kb.wait("tensor", cm.sV, prev_stats_v)
            kb.o("tensor", "matmul", ps_a[0:1, 0:sl], cm.ones_col_bf[:],
                 xb[k][:, s0:s0 + sl], start=(k == 0), stop=(k == nk - 1))
            kb.o("tensor", "matmul", ps_b[0:1, 0:sl], cm.ones_col_bf[:],
                 sq[bq][:, 0:sl], start=(k == 0), stop=(k == nk - 1), sem=cm.sT)
            tvals[(si, k)] = cm.sT.n
        kb.wait("vector", cm.sT)
        kb.o("vector", "tensor_scalar", mrow[0:1, s0:s0 + sl], ps_a[0:1, 0:sl],
             1.0 / H, None, ALU.mult, sem=cm.sV)
        kb.wait("scalar", cm.sV)
        kb.o("scalar", "activation", m2[0:1, 0:sl], mrow[0:1, s0:s0 + sl],
             AF.Square, sem=cm.sA)
        kb.wait("vector", cm.sA)
        kb.o("vector", "scalar_tensor_tensor", vrow[0:1, s0:s0 + sl],
             ps_b[0:1, 0:sl], 1.0 / H, m2[0:1, 0:sl], ALU.mult, ALU.subtract,
             sem=cm.sV)
        prev_stats_v = cm.sV.n
        kb.wait("scalar", cm.sV)
        kb.o("scalar", "activation", rrow[0:1, s0:s0 + sl],
             vrow[0:1, s0:s0 + sl], AF.Sqrt, bias=cm.ceps()[0:1, :], sem=cm.sA)
        kb.wait("vector", cm.sA)
        kb.o("vector", "reciprocal", rrow[0:1, s0:s0 + sl],
             rrow[0:1, s0:s0 + sl], sem=cm.sV)
        kb.wait("tensor", cm.sV)
        kb.o("tensor", "matmul", ps_bc[:, 0:sl], cm.one_row[:],
             mrow[0:1, s0:s0 + sl], start=True, stop=True)
        kb.o("tensor", "matmul", ps_bc2[:, 0:sl], cm.one_row[:],
             rrow[0:1, s0:s0 + sl], start=True, stop=True, sem=cm.sT)
        kb.wait("scalar", cm.sT)
        kb.o("scalar", "copy", mb_sb[:, s0:s0 + sl], ps_bc[:, 0:sl])
        kb.o("scalar", "copy", rb_sb[:, s0:s0 + sl], ps_bc2[:, 0:sl], sem=cm.sA)
        kb.wait("vector", cm.sA)
        for k in range(nk):
            kb.o("vector", "tensor_sub", xb[k][:, s0:s0 + sl],
                 xb[k][:, s0:s0 + sl], mb_sb[:, s0:s0 + sl])
            kb.o("vector", "tensor_mul", xb[k][:, s0:s0 + sl],
                 xb[k][:, s0:s0 + sl], rb_sb[:, s0:s0 + sl],
                 sem=(cm.sV if k == nk - 1 else None))
        if si == len(slabs) - 1:
            pass
    if keep_rows:
        kb.free(*sq, m2, rrow)
    else:
        kb.free(*sq, m2, rrow, mrow, vrow)
    return mrow, vrow


def emit_dense(kb, cm, sEv, wdram, n_k, n_o, rhs_fn, n_tok, evac_fn,
               lora=None, o_order=None, dbanks=(0, 1), nbuf=2, wb=None):
    slabs = [(s, min(512, n_tok - s)) for s in range(0, n_tok, 512)]
    ns = len(slabs)
    if wb is None:
        wb = [kb.sb([128, n_k * 128], BF16, "dw") for _ in range(nbuf)]
    pss = [cm.pb[dbanks[0]], cm.pb[dbanks[1]]]
    o_order = list(range(n_o)) if o_order is None else o_order
    base = sEv.n
    j = 0
    for i, o in enumerate(o_list := o_order):
        b = i % nbuf
        if i == 0:
            kb.wait("gpsimd", cm.sW)
        if i >= nbuf:
            kb.wait("gpsimd", sEv, base + (i - nbuf + 1) * ns)
        kb.dma("gpsimd", wb[b][:].rearrange("p (k c) -> p k c", c=128),
               wdram.ap()[o].rearrange("(k p) c -> p k c", p=128), sem=cm.sW)
        wv = cm.sW.n
        for si, (s0, sl) in enumerate(slabs):
            ip = j % 2
            kb.wait("tensor", cm.sW, wv)
            if j >= 2:
                kb.wait("tensor", sEv, base + j - 1)
            for k in range(n_k):
                kb.o("tensor", "matmul", pss[ip][:, 0:sl],
                     wb[b][:, k * 128:(k + 1) * 128], rhs_fn(k, s0, sl),
                     start=(k == 0), stop=(k == n_k - 1 and lora is None),
                     sem=(cm.sT if (k == n_k - 1 and lora is None) else None))
            if lora is not None:
                kb.o("tensor", "matmul", pss[ip][:, 0:sl], lora[0](o),
                     lora[1](s0, sl), start=False, stop=True, sem=cm.sT)
            evac_fn(j, o, s0, sl, pss[ip][:, 0:sl])
            j += 1
    return j


def build_l0ffn(debug=False):
    nc = bass.Bass("TRN2", target_bir_lowering=False)
    xin = nc.dram_tensor("xin", [KH, 128, T], F32, kind="ExternalInput")
    wbase = nc.dram_tensor("wbase", [KH, KH * 128, 128], F32, kind="ExternalInput")
    bias = nc.dram_tensor("bias", [128, KH], F32, kind="ExternalInput")
    wr = nc.dram_tensor("wr", [KH * 128, 8], F32, kind="ExternalInput")
    wa = nc.dram_tensor("wa", [KH * 128, 128], F32, kind="ExternalInput")
    wbm = nc.dram_tensor("wbm", [32, 128, 128], F32, kind="ExternalInput")
    e16 = nc.dram_tensor("e16", [8, 128], F32, kind="ExternalInput")
    idf = nc.dram_tensor("idf", [128, 128], F32, kind="ExternalInput")
    xout = nc.dram_tensor("xout", [KH, 128, T], F32, kind="ExternalOutput")
    dbg = {}
    if debug:
        dbg["dxln"] = nc.dram_tensor("dxln", [KH, 128, T], F32, kind="ExternalOutput")
        dbg["dlg"] = nc.dram_tensor("dlg", [8, T], F32, kind="ExternalOutput")
        dbg["dwfm"] = nc.dram_tensor("dwfm", [8, T], F32, kind="ExternalOutput")
        dbg["dhw"] = nc.dram_tensor("dhw", [128, T], F32, kind="ExternalOutput")
        dbg["dwt"] = nc.dram_tensor("dwt", [128, 64], F32, kind="ExternalOutput")
        dbg["dlgt"] = nc.dram_tensor("dlgt", [128, 64], F32, kind="ExternalOutput")
        dbg["dcol"] = nc.dram_tensor("dcol", [128, 6], F32, kind="ExternalOutput")
        dbg["dstep"] = nc.dram_tensor("dstep", [128, 32], F32, kind="ExternalOutput")

    kb = KB(nc)
    cm = Common(kb)
    emit_l0ffn(kb, cm, dict(xin=xin, wbase=wbase, bias=bias, wr=wr, wa=wa,
                            wbm=wbm, e16=e16, idf=idf, xout=xout), dbg=dbg)
    kb.wait("sync", cm.sS)
    kb.finalize()
    return nc


def emit_l0ffn(kb, cm, io, dbg=None):
    xin, wbase, bias = io["xin"], io["wbase"], io["bias"]
    wr, wa, wbm, e16, idf, xout = (io["wr"], io["wa"], io["wbm"], io["e16"],
                                   io["idf"], io["xout"])

    xb = [kb.sb([128, T], BF16, "f_xb") for _ in range(KH)]
    mb = kb.sb([128, T], BF16, "f_mb")
    rb = kb.sb([128, T], BF16, "f_rb")
    wr_sb = kb.sb([128, KH * 8], BF16, "f_wr")
    wa_sb = kb.sb([128, KH * 128], BF16, "f_wa")
    wbm_sb = kb.sb([128, 32 * 128], BF16, "f_wbm")
    e16_sb = kb.sb([8, 128], BF16, "f_e16")
    idf_sb = kb.sb([128, 128], F32, "f_idf")
    bias_sb = kb.sb([128, KH], F32, "f_bias")
    lg = kb.sb([8, T], F32, "f_lg")
    hA = kb.sb([128, T], F32, "f_hA")
    hAw = kb.sb([128, T], BF16, "f_hAw")
    lgt = kb.sb([128, 8], F32, "f_lgt")
    wtok = kb.sb([128, 8], F32, "f_wtok")
    wfm = kb.sb([8, T], BF16, "f_wfm")
    scr = kb.sb([128, 8], F32, "f_scr")
    col = kb.sb([128, 6], F32, "f_col")
    xres = [kb.sb([128, 512], F32, "f_xres") for _ in range(4)]
    dbg_wt = kb.sb([128, 64], F32, "f_dbgwt")
    dbg_st = kb.sb([128, 32], F32, "f_dbgst")
    dbg_lgt = kb.sb([128, 64], F32, "f_dbglgt")
    xo = [kb.sb([128, 512], F32, "f_xo") for _ in range(2)]

    ps8 = [cm.pb[4], cm.pb[5]]
    psA = [cm.pb[6], cm.pb[7]]
    ps_t = cm.pb[4]
    ps_t2 = cm.pb[5]
    psW = cm.pb[6]

    for k in range(KH):
        kb.dma("gpsimd", xb[k][:], xin.ap()[k], sem=cm.sW)
    kb.dma("gpsimd", wr_sb[:].rearrange("p (k e) -> p k e", e=8),
           wr.ap().rearrange("(k p) e -> p k e", p=128), sem=cm.sW)
    kb.dma("gpsimd", wa_sb[:].rearrange("p (k c) -> p k c", c=128),
           wa.ap().rearrange("(k p) c -> p k c", p=128), sem=cm.sW)
    kb.dma("gpsimd", wbm_sb[:].rearrange("p (o c) -> p o c", c=128),
           wbm.ap().rearrange("o p c -> p o c"), sem=cm.sW)
    kb.dma("gpsimd", e16_sb[:], e16.ap(), sem=cm.sW)
    kb.dma("sync", idf_sb[:], idf.ap(), sem=cm.sX)
    kb.dma("sync", bias_sb[:], bias.ap(), sem=cm.sX)
    ldw, ldx = cm.sW.n, cm.sX.n
    for eng in ("tensor", "vector", "scalar"):
        kb.wait(eng, cm.sW, ldw)
        kb.wait(eng, cm.sX, ldx)
        kb.wait(eng, cm.sM)

    emit_ln(kb, cm, xb, T, mb, rb)

    if dbg:
        kb.wait("gpsimd", cm.sV)
        for k in range(KH):
            kb.dma("gpsimd", dbg["dxln"].ap()[k], xb[k][:], sem=cm.sS)

    kb.wait("tensor", cm.sV)
    for si in range(2):
        s0 = si * 512
        for k in range(KH):
            kb.o("tensor", "matmul", ps8[si][0:8, :],
                 wr_sb[:, k * 8:(k + 1) * 8], xb[k][:, s0:s0 + 512],
                 start=(k == 0), stop=(k == KH - 1),
                 sem=(cm.sT if k == KH - 1 else None))
        for k in range(KH):
            kb.o("tensor", "matmul", psA[si][:],
                 wa_sb[:, k * 128:(k + 1) * 128], xb[k][:, s0:s0 + 512],
                 start=(k == 0), stop=(k == KH - 1),
                 sem=(cm.sT if k == KH - 1 else None))
    kb.wait("scalar", cm.sT)
    for si in range(2):
        s0 = si * 512
        kb.o("scalar", "copy", lg[0:8, s0:s0 + 512], ps8[si][0:8, :])
        kb.o("scalar", "copy", hA[:, s0:s0 + 512], psA[si][:],
             sem=(cm.sA if si == 1 else None))

    for c in range(8):
        cs = slice(c * 128, (c + 1) * 128)
        kb.wait("tensor", cm.sA)
        kb.o("tensor", "transpose", ps_t[:, 0:8], lg[0:8, cs], idf_sb[0:8, 0:8],
             sem=cm.sT)
        kb.wait("scalar", cm.sT)
        kb.o("scalar", "copy", lgt[:], ps_t[:, 0:8], sem=cm.sA)
        kb.wait("vector", cm.sA)
        kb.o("vector", "reduce_max", col[:, 0:1], lgt[:], axis=AX, bar=True)
        kb.o("vector", "tensor_scalar", scr[:], lgt[:], col[:, 0:1], None,
             ALU.is_equal, bar=True)
        kb.o("vector", "scalar_tensor_tensor", scr[:], scr[:], -1e30, lgt[:],
             ALU.mult, ALU.add, bar=True)
        kb.o("vector", "reduce_max", col[:, 1:2], scr[:], axis=AX, bar=True)
        kb.o("vector", "tensor_sub", col[:, 2:3], col[:, 1:2], col[:, 0:1],
             bar=True)
        kb.o("vector", "tensor_scalar", col[:, 3:4], col[:, 0:1], -1.0, None,
             ALU.mult, sem=cm.sV, bar=True)
        kb.wait("scalar", cm.sV)
        kb.o("scalar", "activation", col[:, 5:6], col[:, 2:3], AF.Sigmoid,
             scale=-1.0, bar=True)
        kb.o("scalar", "activation", scr[:], lgt[:], AF.Exp,
             bias=col[:, 3:4], bar=True)
        cm.pub("scalar", cm.sA, n=2)
        kb.wait("vector", cm.sA)
        kb.o("vector", "tensor_scalar", wtok[:], lgt[:], col[:, 1:2], None,
             ALU.is_ge, bar=True)
        kb.o("vector", "tensor_mul", wtok[:], wtok[:], scr[:], bar=True)
        kb.o("vector", "tensor_scalar", wtok[:], wtok[:], col[:, 5:6], None,
             ALU.mult, bar=True)
        cm.pub("vector", cm.sV)
        if dbg:
            kb.o("vector", "tensor_copy", dbg_wt[:, c * 8:(c + 1) * 8], wtok[:])
            kb.o("vector", "tensor_copy", dbg_lgt[:, c * 8:(c + 1) * 8], lgt[:],
                 sem=cm.sV)
        kb.wait("tensor", cm.sV)
        kb.o("tensor", "transpose", ps_t2[0:8, 0:128], wtok[:], idf_sb[:], sem=cm.sT)
        kb.wait("scalar", cm.sT)
        kb.o("scalar", "copy", wfm[0:8, cs], ps_t2[0:8, 0:128], sem=cm.sA)

    if dbg:
        kb.wait("gpsimd", cm.sA)
        kb.dma("gpsimd", dbg["dlg"].ap(), lg[0:8, :], sem=cm.sS)

    for si in range(2):
        s0 = si * 512
        kb.wait("tensor", cm.sA)
        if si == 1:
            kb.wait("tensor", cm.sV)
        kb.o("tensor", "matmul", psW[:], e16_sb[:], wfm[0:8, s0:s0 + 512],
             start=True, stop=True, sem=cm.sT)
        kb.wait("vector", cm.sT)
        kb.o("vector", "scalar_tensor_tensor", hAw[:, s0:s0 + 512],
             hA[:, s0:s0 + 512], 2.0, psW[:], ALU.mult, ALU.mult, sem=cm.sV)

    if dbg:
        kb.wait("gpsimd", cm.sV)
        kb.dma("gpsimd", dbg["dwt"].ap(), dbg_wt[:], sem=cm.sS)
        kb.dma("gpsimd", dbg["dcol"].ap(), col[:], sem=cm.sS)
        kb.dma("gpsimd", dbg["dstep"].ap(), dbg_st[:], sem=cm.sS)
        kb.dma("gpsimd", dbg["dlgt"].ap(), dbg_lgt[:], sem=cm.sS)
        kb.dma("gpsimd", dbg["dwfm"].ap(), wfm[0:8, :], sem=cm.sS)
        kb.dma("gpsimd", dbg["dhw"].ap(), hAw[:], sem=cm.sS)

    sEv = kb.sem("f_sEv")
    for jj in range(3):
        o0, s0 = divmod(jj, 2)
        kb.dma("sync", xres[jj % 4][:], xin.ap()[o0][:, s0 * 512:(s0 + 1) * 512],
               sem=cm.sX)

    def evac(j, o, s0, sl, ps):
        jf = j + 3
        if jf < 64:
            of, sf = divmod(jf, 2)
            if j > 0:
                kb.wait("sync", sEv, j)
            kb.dma("sync", xres[jf % 4][:],
                   xin.ap()[of][:, sf * 512:(sf + 1) * 512], sem=cm.sX)
        kb.wait("vector", cm.sT)
        kb.wait("vector", cm.sX, ldx + 16 * (j + 1))
        if j >= 2:
            kb.wait("vector", cm.sS, 16 * (j - 1))
        kb.o("vector", "scalar_tensor_tensor", xo[j % 2][:, 0:sl], ps,
             bias_sb[:, o:o + 1], xres[j % 4][:, 0:sl], ALU.add, ALU.add,
             sem=sEv)
        kb.wait("sync", sEv, sEv.n)
        kb.dma("sync", xout.ap()[o][:, s0:s0 + sl], xo[j % 2][:, 0:sl],
               sem=cm.sS)

    kb.wait("tensor", cm.sV)
    emit_dense(kb, cm, sEv, wbase, KH, KH,
               lambda k, s0, sl: xb[k][:, s0:s0 + sl], T, evac,
               lora=(lambda o: wbm_sb[:, o * 128:(o + 1) * 128],
                     lambda s0, sl: hAw[:, s0:s0 + sl]), dbanks=(0, 1))


def host_prep_l0ffn(inputs, x_fm):
    wbase_t = tile_w_omajor(np.ascontiguousarray(inputs["L0_ffn_Wb"].T))
    wr_t = np.ascontiguousarray(inputs["L0_ffn_Wr"].T)
    wa_t = np.ascontiguousarray(inputs["L0_ffn_A"].reshape(128, H).T)
    wbm_t = np.ascontiguousarray(
        inputs["L0_ffn_Bm"].transpose(0, 2, 1).reshape(128, H)
        .reshape(128, KH, 128).transpose(1, 0, 2))
    bias_t = col_layout(inputs["L0_ffn_bias"])
    e16_t = np.zeros((8, 128), np.float32)
    for e in range(8):
        e16_t[e, e * 16:(e + 1) * 16] = 1.0
    idf_t = np.eye(128, dtype=np.float32)
    maps = []
    for c in range(NCORES):
        maps.append({
            "xin": np.ascontiguousarray(x_fm[c]),
            "wbase": wbase_t, "bias": bias_t, "wr": wr_t, "wa": wa_t,
            "wbm": wbm_t, "e16": e16_t, "idf": idf_t,
        })
    return maps


def x_to_fm(x):
    out = np.empty((NCORES, KH, 128, T), np.float32)
    for c in range(NCORES):
        b, t0 = divmod(c, 4)
        sl = x[b, t0 * T:(t0 + 1) * T, :]
        out[c] = np.ascontiguousarray(sl.T).reshape(KH, 128, T)
    return out


def fm_to_x(fm):
    x = np.empty((2, 4 * T, H), np.float32)
    for c in range(NCORES):
        b, t0 = divmod(c, 4)
        x[b, t0 * T:(t0 + 1) * T, :] = fm[c].reshape(H, T).T
    return x


def kernel(**inputs):
    cores = list(range(8))
    maps0 = host_prep_l0attn(inputs, inputs["L0_attn_hidden_states"])
    res0 = run_bass_kernel_spmd(build_l0attn(), maps0, core_ids=cores)
    x1_fm = np.stack([res0.results[c]["xout"] for c in range(8)])
    maps1 = host_prep_l0ffn(inputs, x1_fm)
    res1 = run_bass_kernel_spmd(build_l0ffn(), maps1, core_ids=cores)
    x2_fm = np.stack([res1.results[c]["xout"] for c in range(8)])
    maps2 = host_prep_l1attn(inputs, x2_fm)
    mapsA = [{k: m[k] for k in ("xin", "wqkv", "bqkv", "cosd", "sind")}
             for m in maps2]
    resA = run_bass_kernel_spmd(build_l1attn_a(), mapsA, core_ids=cores)
    qkv = [resA.results[c]["qkvout"] for c in range(8)]
    kvin = [np.concatenate([qkv[r][c * 768:(c + 1) * 768] for r in range(8)],
                           0) for c in range(8)]
    mapsB = [{"kvin": kvin[c], "masks": maps2[c]["masks"],
              "idf": maps2[c]["idf"]} for c in range(8)]
    resB = run_bass_kernel_spmd(build_l1attn_b(), mapsB, core_ids=cores)
    ctx = [resB.results[c]["ctxout"] for c in range(8)]
    ctxin = [np.concatenate([ctx[r][c * 512:(c + 1) * 512] for r in range(8)],
                            0) for c in range(8)]
    mapsC = [{"xin": maps2[c]["xin"], "ctxin": ctxin[c],
              "wdo": maps2[c]["wdo"], "bd": maps2[c]["bd"]}
             for c in range(8)]
    resC = run_bass_kernel_spmd(build_l1attn_c(), mapsC, core_ids=cores)
    x3_fm = np.stack([resC.results[c]["xout"] for c in range(8)])
    maps3 = host_prep_l1ffn(inputs, x3_fm)
    res3 = run_bass_kernel_spmd(build_l1ffn(), maps3, core_ids=cores)
    x4_fm = np.stack([res3.results[c]["xout"] for c in range(8)])
    return fm_to_x(x4_fm)


def build_l1ffn(debug=False, upto=99):
    nc = bass.Bass("TRN2", target_bir_lowering=False)
    xin = nc.dram_tensor("xin", [KH, 128, T], F32, kind="ExternalInput")
    wq = nc.dram_tensor("wq", [KH * 128, 64], F32, kind="ExternalInput")
    wrp = nc.dram_tensor("wrp", [64, 48], F32, kind="ExternalInput")
    akT = nc.dram_tensor("akT", [64, 1024], F32, kind="ExternalInput")
    av = nc.dram_tensor("av", [8, 128, H], F32, kind="ExternalInput")
    wvd = nc.dram_tensor("wvd", [KH * 128, 256], F32, kind="ExternalInput")
    wvu = nc.dram_tensor("wvu", [2 * 128, H], F32, kind="ExternalInput")
    wu1o = nc.dram_tensor("wu1o", [8, KH * 128, 128], F32, kind="ExternalInput")
    wu2 = nc.dram_tensor("wu2", [128, 8], F32, kind="ExternalInput")
    bu1c = nc.dram_tensor("bu1c", [128, 8], F32, kind="ExternalInput")
    sc3 = nc.dram_tensor("sc3", [1, 4], F32, kind="ExternalInput")
    lr = nc.dram_tensor("lr", [1, 1024], F32, kind="ExternalInput")
    idf = nc.dram_tensor("idf", [128, 128], F32, kind="ExternalInput")
    xout = nc.dram_tensor("xout", [KH, 128, T], F32, kind="ExternalOutput")
    dbg = {}
    if debug:
        dbg["dgate"] = nc.dram_tensor("dgate", [1, T], F32, kind="ExternalOutput")
        dbg["dm8"] = nc.dram_tensor("dm8", [128, 8], F32, kind="ExternalOutput")
        dbg["daux"] = nc.dram_tensor("daux", [128, KH * 128], F32, kind="ExternalOutput")

    kb = KB(nc)
    cm = Common(kb)
    emit_l1ffn(kb, cm, dict(xin=xin, wq=wq, wrp=wrp, akT=akT, av=av, wvd=wvd,
                            wvu=wvu, wu1o=wu1o, wu2=wu2, bu1c=bu1c, sc3=sc3,
                            lr=lr, idf=idf, xout=xout), dbg=dbg, upto=upto)
    kb.wait("sync", cm.sS)
    kb.finalize()
    return nc


def emit_l1ffn(kb, cm, io, dbg=None, upto=99):
    xin, xout = io["xin"], io["xout"]
    RD = np.sqrt(48.0)

    xb = [kb.sb([128, T], BF16, "g_xb") for _ in range(KH)]
    mb = kb.sb([128, T], BF16, "g_mb")
    rb = kb.sb([128, T], BF16, "g_rb")
    idb_sb = kb.sb([128, 128], BF16, "g_idb")
    wq_sb = kb.sb([128, KH * 64], BF16, "g_wq")
    wrp_sb = kb.sb([64, 48], BF16, "g_wrp")
    akT_sb = kb.sb([64, 1024], BF16, "g_akT")
    avbuf = [kb.sb([128, H], BF16, "g_avb") for _ in range(2)]
    wvd_sb = kb.sb([128, KH * 256], BF16, "g_wvd")
    wvu_sb = kb.sb([128, 2 * H], BF16, "g_wvu")
    wu2_sb = kb.sb([128, 8], BF16, "g_wu2")
    bu1_sb = kb.sb([128, 8], F32, "g_bu1")
    sc3_sb = kb.sb([1, 4], F32, "g_sc3")
    lr_sb = kb.sb([1, 1024], F32, "g_lr")
    q_sb = kb.sb([64, T], BF16, "g_q")
    rq_sb = kb.sb([48, T], BF16, "g_rq")
    rk_sb = kb.sb([48, 1024], BF16, "g_rk")
    u1tmp = kb.sb([128, 512], BF16, "g_u1t")
    g0 = kb.sb([128, 512], F32, "g_g0")
    g1 = kb.sb([128, 512], F32, "g_g1")
    g2 = kb.sb([128, 512], F32, "g_g2")
    rowz = kb.sb([1, 16], F32, "g_rowz")
    urow = kb.sb([1, T], F32, "g_urow")
    u2row = kb.sb([1, T], F32, "g_u2row")
    gb_sb = kb.sb([128, T], F32, "g_gb")
    s_orig = kb.sb([128, 1024], F32, "g_sorig")
    eqt = kb.sb([128, 1024], F32, "g_eqt")
    s_work = kb.sb([128, 1024], F32, "g_swork")
    sa_sb = kb.sb([128, 1024], F32, "g_sa")
    p_sb = kb.sb([128, 1024], BF16, "g_p")
    pT_sb = kb.sb([128, 1024], BF16, "g_pT")
    aux_tc = kb.sb([128, KH * 128], BF16, "g_aux")
    vd_sb = kb.sb([128, 256], BF16, "g_vd")
    col = kb.sb([128, 4], F32, "g_col")
    xres = [kb.sb([128, 128], F32, "g_xres") for _ in range(4)]
    xo = [kb.sb([128, 128], F32, "g_xo") for _ in range(2)]

    ar_in = kb.nc.dram_tensor(kb.name("g_arin"), [1, 16], F32, kind="Internal")
    ar_out = kb.nc.dram_tensor(kb.name("g_arout"), [1, 16], F32, kind="Internal")

    for k in range(KH):
        kb.dma("gpsimd", xb[k][:], xin.ap()[k], sem=cm.sW)
    kb.dma("gpsimd", idb_sb[:], io["idf"].ap(), sem=cm.sW)
    kb.dma("gpsimd", wq_sb[:].rearrange("p (k c) -> p k c", c=64),
           io["wq"].ap().rearrange("(k p) c -> p k c", p=128), sem=cm.sW)
    kb.dma("gpsimd", wrp_sb[:], io["wrp"].ap(), sem=cm.sW)
    kb.dma("gpsimd", akT_sb[:], io["akT"].ap(), sem=cm.sW)
    kb.dma("gpsimd", wvd_sb[:].rearrange("p (k c) -> p k c", c=256),
           io["wvd"].ap().rearrange("(k p) c -> p k c", p=128), sem=cm.sW)
    kb.dma("gpsimd", wvu_sb[:].rearrange("p (k c) -> p k c", c=H),
           io["wvu"].ap().rearrange("(k p) c -> p k c", p=128), sem=cm.sW)
    kb.dma("gpsimd", wu2_sb[:], io["wu2"].ap(), sem=cm.sW)
    kb.dma("sync", bu1_sb[:], io["bu1c"].ap(), sem=cm.sX)
    kb.dma("sync", sc3_sb[:], io["sc3"].ap(), sem=cm.sX)
    kb.dma("sync", lr_sb[:], io["lr"].ap(), sem=cm.sX)
    ldw, ldx = cm.sW.n, cm.sX.n
    for eng in ("tensor", "vector", "scalar", "gpsimd"):
        kb.wait(eng, cm.sW, ldw)
        kb.wait(eng, cm.sX, ldx)
        kb.wait(eng, cm.sM)

    mrow, vrow = emit_ln(kb, cm, xb, T, mb, rb)

    if upto < 2:
        return
    kb.o("vector", "memset", rowz[:], 0.0)
    kb.o("vector", "tensor_scalar", urow[0:1, :], vrow[0:1, :], 1e-5, None,
         ALU.add, bar=True)
    kb.o("vector", "reciprocal", urow[0:1, :], urow[0:1, :], bar=True)
    kb.o("vector", "tensor_mul", urow[0:1, :], urow[0:1, :], vrow[0:1, :],
         bar=True)
    kb.wait("scalar", cm.sV, cm.sV.n)
    cm.pub("vector", cm.sV)
    kb.wait("scalar", cm.sV)
    kb.o("scalar", "activation", u2row[0:1, :], urow[0:1, :], AF.Ln,
         bias=cm.c1()[0:1, :], bar=True)
    cm.pub("scalar", cm.sA)
    kb.wait("vector", cm.sA)
    kb.o("vector", "reduce_sum", rowz[0:1, 0:1], u2row[0:1, :], axis=AX,
         bar=True)
    cm.pub("vector", cm.sV)
    kb.wait("gpsimd", cm.sV)
    kb.dma("gpsimd", ar_in.ap(), rowz[:], sem=cm.sG)
    kb.wait("gpsimd", cm.sG)
    kb.cc("AllReduce", ALU.add, [list(range(8))], [ar_in.ap().opt()],
          [ar_out.ap().opt()], cm.sCC)
    kb.wait("gpsimd", cm.sCC)
    kb.dma("gpsimd", rowz[:], ar_out.ap(), sem=cm.sG)

    if upto < 3:
        return
    kb.wait("tensor", cm.sV)
    psQ = [cm.pb[0], cm.pb[1]]
    for si in range(2):
        s0 = si * 512
        for k in range(KH):
            kb.o("tensor", "matmul", psQ[si][0:64, :],
                 wq_sb[:, k * 64:(k + 1) * 64], xb[k][:, s0:s0 + 512],
                 start=(k == 0), stop=(k == KH - 1),
                 sem=(cm.sT if k == KH - 1 else None))
    kb.wait("vector", cm.sT)
    for si in range(2):
        kb.o("vector", "tensor_scalar", q_sb[0:64, si * 512:(si + 1) * 512],
             psQ[si][0:64, :], 0.125, None, ALU.mult,
             sem=(cm.sA if si == 1 else None))
    kb.wait("tensor", cm.sA)
    psR = [cm.pb[2], cm.pb[3]]
    for si in range(2):
        s0 = si * 512
        kb.o("tensor", "matmul", psR[si][0:48, :], wrp_sb[:],
             q_sb[0:64, s0:s0 + 512], start=True, stop=True,
             sem=(cm.sT if si == 1 else None))
    kb.wait("vector", cm.sT)
    for si in range(2):
        s0 = si * 512
        kb.o("vector", "tensor_scalar", rq_sb[0:48, s0:s0 + 512],
             psR[si][0:48, :], 8.0 / RD, None, ALU.mult,
             sem=(cm.sV if si == 1 else None))
    kb.wait("tensor", cm.sV)
    psK = [cm.pb[4], cm.pb[5]]
    for si in range(2):
        s0 = si * 512
        kb.o("tensor", "matmul", psK[si][0:48, :], wrp_sb[:],
             akT_sb[:, s0:s0 + 512], start=True, stop=True,
             sem=(cm.sT if si == 1 else None))
    kb.wait("scalar", cm.sT)
    for si in range(2):
        s0 = si * 512
        kb.o("scalar", "copy", rk_sb[0:48, s0:s0 + 512], psK[si][0:48, :],
             sem=(cm.sA if si == 1 else None))

    if upto < 4:
        return
    sEv1 = kb.sem("g_sEv1")

    def evac_u1(j, o, s0, sl, ps):
        if PH4 < 2:
            kb.wait("vector", cm.sT)
            kb.o("vector", "tensor_scalar", g0[:, 0:sl], ps, 1.0, None,
                 ALU.mult, sem=sEv1)
            return
        kb.wait("vector", cm.sT)
        kb.o("vector", "tensor_scalar", g0[:, 0:sl], ps, bu1_sb[:, o:o + 1],
             None, ALU.add, sem=cm.sV)
        kb.wait("scalar", cm.sV)
        kb.o("scalar", "activation", g1[:, 0:sl], g0[:, 0:sl], AF.Square,
             sem=cm.sA)
        kb.wait("vector", cm.sA)
        kb.o("vector", "tensor_mul", g1[:, 0:sl], g1[:, 0:sl], g0[:, 0:sl])
        kb.o("vector", "scalar_tensor_tensor", g1[:, 0:sl], g1[:, 0:sl],
             0.044715, g0[:, 0:sl], ALU.mult, ALU.add, sem=cm.sV)
        kb.wait("scalar", cm.sV)
        kb.o("scalar", "activation", g2[:, 0:sl], g1[:, 0:sl], AF.Tanh,
             scale=0.7978845608028654, sem=cm.sA)
        kb.wait("vector", cm.sA)
        kb.o("vector", "tensor_scalar", g2[:, 0:sl], g2[:, 0:sl], 1.0, 0.5,
             ALU.add, ALU.mult)
        kb.o("vector", "tensor_tensor", u1tmp[:, 0:sl], g2[:, 0:sl],
             g0[:, 0:sl], ALU.mult, bar=True)
        cm.pub("vector", cm.sV)
        kb.wait("tensor", cm.sV)
        if NO_PSL:
            kb.o("tensor", "matmul", psL[j % 2][0:1, 0:sl], wu2_sb[:, o:o + 1],
                 u1tmp[:, 0:sl], start=True, stop=True, sem=sEv1)
        else:
            kb.o("tensor", "matmul", psL[j % 2][0:1, 0:sl], wu2_sb[:, o:o + 1],
                 u1tmp[:, 0:sl], start=(o == 0), stop=(o == 7), sem=sEv1)

    psL = [cm.pb[0], cm.pb[1]]
    kb.wait("tensor", cm.sA)
    emit_dense(kb, cm, sEv1, io["wu1o"], KH, 8,
               lambda k, s0, sl: xb[k][:, s0:s0 + sl], T, evac_u1,
               dbanks=(6, 7), nbuf=1)

    if PH4 < 3:
        return
    kb.wait("vector", sEv1)
    for si in range(2):
        s0 = si * 512
        kb.o("vector", "tensor_scalar", urow[0:1, s0:s0 + 512],
             psL[si][0:1, :], sc3_sb[0:1, 0:1], None, ALU.add,
             sem=(cm.sV if si == 1 else None), bar=True)
    if PH4 < 4:
        return
    kb.wait("vector", cm.sG)
    kb.o("vector", "tensor_scalar", rowz[0:1, 0:1], rowz[0:1, 0:1],
         1.0 / 8192.0, 1e-6, ALU.mult, ALU.add, bar=True)
    kb.o("vector", "reciprocal", rowz[0:1, 1:2], rowz[0:1, 0:1], bar=True)
    cm.pub("vector", cm.sV)
    kb.wait("scalar", cm.sV)
    kb.o("scalar", "activation", urow[0:1, :], urow[0:1, :], AF.Sigmoid,
         bar=True)
    cm.pub("scalar", cm.sA)
    kb.wait("vector", cm.sA)
    kb.o("vector", "tensor_scalar", u2row[0:1, :], u2row[0:1, :],
         rowz[0:1, 1:2], 0.5, ALU.mult, ALU.mult, bar=True)
    kb.o("vector", "tensor_scalar", urow[0:1, :], urow[0:1, :], 2.5, None,
         ALU.mult, bar=True)
    kb.o("vector", "tensor_add", urow[0:1, :], urow[0:1, :], u2row[0:1, :],
         bar=True)
    kb.o("vector", "tensor_scalar", urow[0:1, :], urow[0:1, :], 0.0, 5.0,
         ALU.max, ALU.min, bar=True)
    kb.o("vector", "tensor_scalar", urow[0:1, :], urow[0:1, :], -0.5,
         1.0 / 1.5, ALU.add, ALU.mult, bar=True)
    kb.o("vector", "tensor_scalar", urow[0:1, :], urow[0:1, :], 0.0, 1.0,
         ALU.max, ALU.min, bar=True)
    cm.pub("vector", cm.sV)
    kb.wait("scalar", cm.sV)
    kb.o("scalar", "activation", urow[0:1, :], urow[0:1, :], AF.Sigmoid,
         bias=sc3_sb[0:1, 2:3], scale=sc3_sb[0:1, 1:2], bar=True)
    cm.pub("scalar", cm.sA)
    kb.wait("vector", cm.sA)
    kb.o("vector", "tensor_scalar", u2row[0:1, :], urow[0:1, :], 0.05, None,
         ALU.is_ge, bar=True)
    kb.o("vector", "tensor_mul", urow[0:1, :], urow[0:1, :], u2row[0:1, :],
         bar=True)
    cm.pub("vector", cm.sV)
    if dbg:
        kb.wait("gpsimd", cm.sV)
        kb.dma("gpsimd", dbg["dgate"].ap(), urow[0:1, :], sem=cm.sS)
    kb.wait("tensor", cm.sV)
    for si in range(2):
        s0 = si * 512
        if si == 1:
            kb.wait("tensor", cm.sA)
        kb.o("tensor", "matmul", cm.pb[2][:], cm.one_row[:],
             urow[0:1, s0:s0 + 512], start=True, stop=True, sem=cm.sT)
        kb.wait("scalar", cm.sT)
        kb.o("scalar", "copy", gb_sb[:, s0:s0 + 512], cm.pb[2][:], sem=cm.sA)

    if upto < 5:
        return
    sEv2 = kb.sem("g_sEv2")
    avals = {}
    avw = {}
    avmm = {}
    for jj in range(3):
        o0 = jj
        kb.dma("sync", xres[jj % 4][:], xin.ap()[o0][:, 0:128], sem=cm.sX)
    for tc in range(8):
        ts0 = tc * 128
        kb.wait("tensor", cm.sA)
        kb.wait("tensor", cm.sV)
        for si in range(2):
            s0 = si * 512
            kb.o("tensor", "matmul", cm.pb[si][:], rq_sb[0:48, ts0:ts0 + 128],
                 rk_sb[0:48, s0:s0 + 512], start=True, stop=False)
            kb.o("tensor", "matmul", cm.pb[si][:], cm.one_row[:],
                 lr_sb[0:1, s0:s0 + 512], start=False, stop=True,
                 sem=cm.sT)
            kb.o("tensor", "matmul", cm.pb[2 + si][:],
                 q_sb[0:64, ts0:ts0 + 128], akT_sb[:, s0:s0 + 512],
                 start=True, stop=False)
            kb.o("tensor", "matmul", cm.pb[2 + si][:], cm.one_row[:],
                 lr_sb[0:1, s0:s0 + 512], start=False, stop=True, sem=cm.sT)
        kb.wait("scalar", cm.sT)
        for si in range(2):
            s0 = si * 512
            kb.o("scalar", "copy", s_orig[:, s0:s0 + 512], cm.pb[si][:])
            kb.o("scalar", "copy", s_work[:, s0:s0 + 512], cm.pb[si][:],
                 sem=(cm.sA if si == 1 else None))
        kb.wait("vector", cm.sA)
        for si in range(2):
            s0 = si * 512
            kb.o("vector", "tensor_scalar", sa_sb[:, s0:s0 + 512],
                 cm.pb[2 + si][:], 1.0, None, ALU.mult,
                 sem=(cm.sV if si == 1 else None))
        for it in range(8):
            kb.o("vector", "reduce_max", col[:, 0:1], s_work[:], axis=AX,
                 bar=True)
            if it < 7:
                kb.o("vector", "tensor_scalar", eqt[:], s_work[:],
                     col[:, 0:1], None, ALU.is_equal, bar=True)
                kb.o("vector", "scalar_tensor_tensor", s_work[:], eqt[:],
                     -1e30, s_work[:], ALU.mult, ALU.add, bar=True)
        kb.o("vector", "tensor_scalar", s_work[:], s_orig[:], col[:, 0:1],
             None, ALU.is_ge, bar=True)
        kb.o("vector", "tensor_scalar", sa_sb[:], sa_sb[:], 1e4, None,
             ALU.add)
        kb.o("vector", "tensor_mul", sa_sb[:], sa_sb[:], s_work[:], bar=True)
        kb.o("vector", "tensor_scalar", sa_sb[:], sa_sb[:], -1e4, None,
             ALU.add, bar=True)
        kb.o("vector", "reduce_max", col[:, 1:2], sa_sb[:], axis=AX, bar=True)
        kb.o("vector", "tensor_scalar", col[:, 2:3], col[:, 1:2], -1.0, None,
             ALU.mult, bar=True)
        cm.pub("vector", cm.sV)
        kb.wait("scalar", cm.sV)
        kb.o("scalar", "activation", p_sb[:], sa_sb[:], AF.Exp,
             bias=col[:, 2:3], accum_out=col[:, 3:4], bar=True)
        cm.pub("scalar", cm.sA)
        kb.wait("vector", cm.sA)
        kb.o("vector", "reciprocal", col[:, 3:4], col[:, 3:4], bar=True)
        kb.o("vector", "tensor_scalar", p_sb[:], p_sb[:], col[:, 3:4], None,
             ALU.mult, bar=True)
        cm.pub("vector", cm.sV)
        kb.wait("tensor", cm.sV)
        for sblk in range(8):
            kb.o("tensor", "transpose",
                 cm.pb[4][:, 64 * (sblk % 2):64 * (sblk % 2) + 64].bitcast(BF16),
                 p_sb[:, sblk * 128:(sblk + 1) * 128], idb_sb[:], sem=cm.sT)
            kb.wait("scalar", cm.sT)
            kb.o("scalar", "copy", pT_sb[:, sblk * 128:(sblk + 1) * 128],
                 cm.pb[4][:, 64 * (sblk % 2):64 * (sblk % 2) + 64].bitcast(BF16),
                 sem=cm.sA)
            if sblk < 7:
                kb.wait("tensor", cm.sA)
        if tc == 0:
            for pre in range(2):
                kb.dma("gpsimd", avbuf[pre][:].rearrange(
                           "p (k c) -> p k c", c=512),
                       io["av"].ap()[:, :, pre * 512:(pre + 1) * 512]
                       .rearrange("k p c -> p k c"), sem=cm.sW)
        kb.wait("tensor", cm.sA)
        kb.wait("tensor", sEv2)
        for og in range(8):
            pbk = cm.pb[5 + og % 2]
            if og >= 2:
                kb.wait("tensor", cm.sA, avals[og - 2])
            kb.wait("tensor", cm.sW, avw[(tc, og)] if (tc, og) in avw
                    else cm.sW.n)
            for oo in range(4):
                o = og * 4 + oo
                for k in range(8):
                    kb.o("tensor", "matmul", pbk[:, oo * 128:oo * 128 + 128],
                         avbuf[og % 2][:, k * 512 + oo * 128:k * 512 + (oo + 1) * 128],
                         pT_sb[:, k * 128:(k + 1) * 128],
                         start=(k == 0 and oo == 0), stop=(k == 7 and oo == 3),
                         sem=(cm.sT if (k == 7 and oo == 3) else None))
            avmm[(tc, og)] = cm.sT.n
            ng, ntc = og + 2, tc
            if ng >= 8:
                ng, ntc = ng - 8, tc + 1
            if ntc < 8:
                kb.wait("gpsimd", cm.sT, avmm[(tc, og)])
                kb.dma("gpsimd", avbuf[og % 2][:].rearrange(
                           "p (k c) -> p k c", c=512),
                       io["av"].ap()[:, :, ng * 512:(ng + 1) * 512]
                       .rearrange("k p c -> p k c"), sem=cm.sW)
                avw[(ntc, ng)] = cm.sW.n
            kb.wait("scalar", cm.sT, avmm[(tc, og)])
            kb.o("scalar", "copy", aux_tc[:, og * 512:(og + 1) * 512], pbk[:],
                 sem=cm.sA)
            avals[og] = cm.sA.n
        kb.wait("tensor", cm.sA)
        for o2 in range(2):
            for k in range(KH):
                kb.o("tensor", "matmul",
                     cm.pb[7][:, o2 * 128:o2 * 128 + 128],
                     wvd_sb[:, k * 256 + o2 * 128:k * 256 + o2 * 128 + 128],
                     aux_tc[:, k * 128:(k + 1) * 128],
                     start=(k == 0 and o2 == 0), stop=(k == KH - 1 and o2 == 1),
                     sem=(cm.sT if (k == KH - 1 and o2 == 1) else None))
        kb.wait("scalar", cm.sT)
        kb.o("scalar", "copy", vd_sb[:], cm.pb[7][:, 0:256], sem=cm.sA)
        kb.wait("tensor", cm.sA)
        for o in range(KH):
            j = tc * KH + o
            pbk = cm.pb[5 + j % 2]
            if j >= 2:
                kb.wait("tensor", sEv2, j - 1)
            for k2 in range(2):
                kb.o("tensor", "matmul", pbk[:, 0:128],
                     wvu_sb[:, k2 * H + o * 128:k2 * H + (o + 1) * 128],
                     vd_sb[:, k2 * 128:(k2 + 1) * 128],
                     start=(k2 == 0), stop=(k2 == 1),
                     sem=(cm.sT if k2 == 1 else None))
            jf = j + 3
            if jf < 256:
                tf, of = divmod(jf, KH)
                if j > 0:
                    kb.wait("sync", sEv2, j)
                kb.dma("sync", xres[jf % 4][:],
                       xin.ap()[of][:, tf * 128:(tf + 1) * 128], sem=cm.sX)
            kb.wait("vector", cm.sT)
            kb.wait("vector", cm.sX, ldx + 16 * (j + 1))
            if j >= 2:
                kb.wait("vector", cm.sS, 16 * (j - 1))
            kb.o("vector", "tensor_tensor", xo[j % 2][:], pbk[:, 0:128],
                 gb_sb[:, ts0:ts0 + 128], ALU.mult)
            kb.o("vector", "tensor_tensor", xo[j % 2][:], xo[j % 2][:],
                 xb[o][:, ts0:ts0 + 128], ALU.add)
            kb.o("vector", "tensor_tensor", xo[j % 2][:], xo[j % 2][:],
                 xres[j % 4][:], ALU.add, sem=sEv2)
            kb.wait("sync", sEv2, sEv2.n)
            kb.dma("sync", xout.ap()[o][:, ts0:ts0 + 128], xo[j % 2][:],
                   sem=cm.sS)
        if dbg and tc == 0:
            kb.wait("gpsimd", cm.sA)
            kb.dma("gpsimd", dbg["daux"].ap(), aux_tc[:], sem=cm.sS)


def host_prep_l1ffn(inputs, x_fm):
    wq_t = np.ascontiguousarray(inputs["L1_ffn_Wq"].T)
    wrp_t = np.ascontiguousarray(inputs["L1_ffn_Wr"].T)
    akT_t = np.ascontiguousarray(inputs["L1_ffn_aux_keys"].T)
    av_t = np.ascontiguousarray(
        inputs["L1_ffn_aux_values"].reshape(8, 128, H))
    wvd_t = np.ascontiguousarray(inputs["L1_ffn_Wvd"].T)
    wvu_t = np.ascontiguousarray(inputs["L1_ffn_Wvu"].T)
    wu1o_t = tile_w_omajor(np.ascontiguousarray(inputs["L1_ffn_Wu1"].T))
    wu2_t = np.ascontiguousarray(inputs["L1_ffn_Wu2"].reshape(8, 128).T)
    bu1c_t = col_layout(inputs["L1_ffn_bu1"])
    sc3_t = np.array([[inputs["L1_ffn_bu2"].reshape(-1)[0],
                       float(inputs["L1_ffn_gate_w1"]),
                       float(inputs["L1_ffn_gate_bias"]), 0.0]], np.float32)
    lr_t = np.ascontiguousarray(inputs["L1_ffn_log_rel"].reshape(1, 1024))
    idf_t = np.eye(128, dtype=np.float32)
    maps = []
    for c in range(NCORES):
        maps.append({
            "xin": np.ascontiguousarray(x_fm[c]), "wq": wq_t, "wrp": wrp_t,
            "akT": akT_t, "av": av_t, "wvd": wvd_t, "wvu": wvu_t,
            "wu1o": wu1o_t, "wu2": wu2_t, "bu1c": bu1c_t, "sc3": sc3_t,
            "lr": lr_t, "idf": idf_t,
        })
    return maps


def build_l1attn_a():
    nc = bass.Bass("TRN2", target_bir_lowering=False)
    xin = nc.dram_tensor("xin", [KH, 128, T], F32, kind="ExternalInput")
    wqkv = nc.dram_tensor("wqkv", [48, KH * 128, 128], F32, kind="ExternalInput")
    bqkv = nc.dram_tensor("bqkv", [128, 48], F32, kind="ExternalInput")
    cosd = nc.dram_tensor("cosd", [128, T], F32, kind="ExternalInput")
    sind = nc.dram_tensor("sind", [128, T], F32, kind="ExternalInput")
    qkvout = nc.dram_tensor("qkvout", [6144, 1024], BF16, kind="ExternalOutput")
    kb = KB(nc)
    cm = Common(kb, groups=(4, 1, 1, 1, 1))
    emit_l1attn_qkv(kb, cm, dict(xin=xin, wqkv=wqkv, bqkv=bqkv, cosd=cosd,
                                 sind=sind), qkvout)
    kb.wait("sync", cm.sS)
    kb.finalize()
    return nc


def build_l1attn_b():
    nc = bass.Bass("TRN2", target_bir_lowering=False)
    kvin = nc.dram_tensor("kvin", [6144, 1024], BF16, kind="ExternalInput")
    masks = nc.dram_tensor("masks", [6, 128, 512], F32, kind="ExternalInput")
    idf = nc.dram_tensor("idf", [128, 128], F32, kind="ExternalInput")
    ctxout = nc.dram_tensor("ctxout", [4096, 1024], BF16, kind="ExternalOutput")
    kb = KB(nc)
    cm = Common(kb, groups=(4, 1, 1, 1, 1))
    emit_l1attn_att(kb, cm, dict(masks=masks, idf=idf), kvin, ctxout)
    kb.wait("sync", cm.sS)
    kb.finalize()
    return nc


def build_l1attn_c():
    nc = bass.Bass("TRN2", target_bir_lowering=False)
    xin = nc.dram_tensor("xin", [KH, 128, T], F32, kind="ExternalInput")
    ctxin = nc.dram_tensor("ctxin", [4096, 1024], BF16, kind="ExternalInput")
    wdo = nc.dram_tensor("wdo", [KH, KH * 128, 128], F32, kind="ExternalInput")
    bd = nc.dram_tensor("bd", [128, KH], F32, kind="ExternalInput")
    xout = nc.dram_tensor("xout", [KH, 128, T], F32, kind="ExternalOutput")
    kb = KB(nc)
    cm = Common(kb, groups=(4, 1, 1, 1, 1))
    emit_l1attn_dense(kb, cm, dict(xin=xin, wdo=wdo, bd=bd, xout=xout), ctxin)
    kb.wait("sync", cm.sS)
    kb.finalize()
    return nc


def emit_l1attn_qkv(kb, cm, io, a2a_in):
    xin = io["xin"]

    xb = [kb.sbt([128, T], BF16, "a_xb") for _ in range(KH)]
    mb = kb.sbt([128, T], BF16, "a_mb")
    rb = kb.sbt([128, T], BF16, "a_rb")
    cos_sb = kb.sbt([128, T], F32, "a_cos")
    sin_sb = kb.sbt([128, T], F32, "a_sin")
    g0 = kb.sbt([128, 512], F32, "a_g0")
    rtmp = kb.sbt([128, 512], F32, "a_rt")
    qbuf = [kb.sbt([128, 512], BF16, "a_qb") for _ in range(2)]
    bq_sb = kb.sbt([128, 48], F32, "a_bq")

    for k in range(KH):
        kb.dma("gpsimd", xb[k][:], xin.ap()[k], sem=cm.sW)
    kb.dma("sync", cos_sb[:], io["cosd"].ap(), sem=cm.sX)
    kb.dma("sync", sin_sb[:], io["sind"].ap(), sem=cm.sX)
    kb.dma("sync", bq_sb[:], io["bqkv"].ap(), sem=cm.sX)
    ldw, ldx = cm.sW.n, cm.sX.n
    for eng in ("tensor", "vector", "scalar"):
        kb.wait(eng, cm.sW, ldw)
        kb.wait(eng, cm.sX, ldx)
        kb.wait(eng, cm.sM)
    emit_ln(kb, cm, xb, T, mb, rb, banks=(0, 4, 1, 5))

    sEvq = kb.sem("a_sEvq")

    def evac_qkv(j, o, s0, sl, ps):
        rope = (o % 6) < 5
        qb = qbuf[j % 2]
        kb.wait("vector", cm.sT)
        if j >= 2:
            kb.wait("vector", cm.sS, 16 * (j - 1))
        kb.o("vector", "tensor_scalar", g0[:, 0:sl], ps, bq_sb[:, o:o + 1],
             None, ALU.add, sem=cm.sV)
        if rope:
            kb.wait("scalar", cm.sV)
            kb.o("scalar", "copy", rtmp[0:64, 0:sl], g0[64:128, 0:sl])
            kb.o("scalar", "copy", rtmp[64:128, 0:sl], g0[0:64, 0:sl],
                 sem=cm.sA)
            kb.wait("vector", cm.sA)
            kb.o("vector", "tensor_mul", rtmp[:, 0:sl], rtmp[:, 0:sl],
                 sin_sb[:, s0:s0 + sl])
            kb.o("vector", "tensor_mul", g0[:, 0:sl], g0[:, 0:sl],
                 cos_sb[:, s0:s0 + sl])
            kb.o("vector", "tensor_add", qb[:, 0:sl], g0[:, 0:sl],
                 rtmp[:, 0:sl], sem=sEvq)
        else:
            kb.o("vector", "tensor_scalar", qb[:, 0:sl], g0[:, 0:sl], 1.0,
                 None, ALU.mult, sem=sEvq)
        kb.wait("sync", sEvq, sEvq.n)
        kb.dma("sync", a2a_in.ap()[o * 128:(o + 1) * 128, s0:s0 + sl],
               qb[:, 0:sl], sem=cm.sS)

    kb.wait("tensor", cm.sV)
    emit_dense(kb, cm, sEvq, io["wqkv"], KH, 48,
               lambda k, s0, sl: xb[k][:, s0:s0 + sl], T, evac_qkv,
               dbanks=(4, 5))
    kb.free(*xb, mb, rb, cos_sb, sin_sb, g0, rtmp, *qbuf, bq_sb)


def emit_l1attn_att(kb, cm, io, a2a_out, a2b_in):
    Kb_sb = kb.sbt([128, 4096], BF16, "a_K")
    Vt_sb = kb.sbt([128, 4096], BF16, "a_V")
    qI = kb.sbt([128, 8192], BF16, "a_qI")
    ctx_sb = kb.sbt([128, 512], BF16, "a_ctx")
    mks = kb.sbt([128, 6 * 512], F32, "a_mks")
    idf_sb = kb.sbt([128, 128], F32, "a_idf")
    idb_sb = kb.sbt([128, 128], BF16, "a_idb")
    mrow_sb = kb.sbt([1, 512], F32, "a_mrow")
    mbc = kb.sbt([128, 512], F32, "a_mbc")
    rbc = kb.sbt([128, 512], F32, "a_rbc")
    srow = kb.sbt([1, 512], F32, "a_srow")
    ps_big = cm.pb

    kb.dma("sync", idf_sb[:], io["idf"].ap(), sem=cm.sX)
    kb.dma("gpsimd", idb_sb[:], io["idf"].ap(), sem=cm.sW)
    kb.dma("sync", mks[:].rearrange("p (m c) -> p m c", c=512),
           io["masks"].ap().rearrange("m p c -> p m c"), sem=cm.sX)

    kb._mks = mks

    def mk(i):
        return mks[:, i * 512:(i + 1) * 512]

    for eng in ("tensor", "vector", "scalar", "gpsimd", "sync"):
        kb.wait(eng, cm.sX, cm.sX.n)
        kb.wait(eng, cm.sW, cm.sW.n)
        kb.wait(eng, cm.sM)

    emit_l1attn_body(kb, cm, io, a2a_out, a2b_in,
                     Kb_sb, Vt_sb, qI, ctx_sb, mk, idf_sb, idb_sb,
                     mrow_sb, mbc, rbc, srow)



def emit_l1attn_body(kb, cm, io, a2a_out, a2b_in, Kb_sb, Vt_sb, qI,
                     ctx_sb, mk, idf_sb, idb_sb, mrow_sb, mbc, rbc, srow):
    SM = 1.0 / 128.0
    PS = cm.tg[0]
    t_sb = kb.sbt([128, 512], F32, "a_t")
    pT_sb = kb.sbt([128, 512], BF16, "a_pT")
    m4 = kb.sbt([128, 4], F32, "a_m4")
    for b in range(2):
        kb.wait("gpsimd", cm.sT, cm.sT.n)
        kb.dma("gpsimd", Kb_sb[:].rearrange("p (r t) -> p r t", t=1024),
               a2a_out.ap().rearrange("(r q p) t -> q p r t", q=6, p=128)
               [4][:, 4 * b:4 * b + 4], sem=cm.sW)
        kb.dma("gpsimd", Vt_sb[:].rearrange("p (r t) -> p r t", t=1024),
               a2a_out.ap().rearrange("(r q p) t -> q p r t", q=6, p=128)
               [5][:, 4 * b:4 * b + 4], sem=cm.sW)
        wv = cm.sW.n
        for eng in ("tensor", "vector", "scalar"):
            kb.wait(eng, cm.sW, wv)
        for vt in range(32):
            if vt >= 1:
                kb.wait("tensor", cm.sA)
            kb.o("tensor", "transpose",
                 cm.pb[7][:, 64 * (vt % 2):64 * (vt % 2) + 64].bitcast(BF16),
                 Vt_sb[:, vt * 128:(vt + 1) * 128], idb_sb[:], sem=cm.sT)
            kb.wait("scalar", cm.sT)
            kb.o("scalar", "copy", Vt_sb[:, vt * 128:(vt + 1) * 128],
                 cm.pb[7][:, 64 * (vt % 2):64 * (vt % 2) + 64].bitcast(BF16),
                 sem=cm.sA)
        for qh in range(2):
            kb.wait("gpsimd", cm.sT, cm.sT.n)
            for h in range(4):
                for rr in range(2):
                    kb.dma("gpsimd",
                           qI[:].rearrange("p (r blk hh q) -> p hh r blk q",
                                           r=2, blk=8, hh=4)[:, h, rr],
                           a2a_out.ap()
                           .rearrange("(r q p) t -> q p r t", q=6, p=128)
                           [h][:, 4 * b + 2 * qh + rr]
                           .rearrange("p (blk q) -> p blk q", q=128),
                           sem=cm.sW)
            wv2 = cm.sW.n
            kb.wait("tensor", cm.sW, wv2)
            for P in range(16):
                Q = qh * 16 + P
                stri = list(range(3, Q - 8, 4))
                wnd = list(range(max(0, Q - 8), Q + 1))
                tiles = stri + wnd
                nt = len(tiles)

                def tmask(kp):
                    if kp in stri:
                        return 5
                    if Q >= 8 and kp == Q - 8:
                        return 5 if Q % 4 == 3 else 4
                    if kp == Q:
                        return 3
                    return None

                for h in range(4):
                    kb.wait("tensor", cm.sV)
                    kb.wait("tensor", cm.sA)
                    lq = qI[:, P * 512 + h * 128:P * 512 + (h + 1) * 128]
                    for ti, kp in enumerate(tiles):
                        kb.o("tensor", "matmul", PS[:, ti * 128:(ti + 1) * 128],
                             lq, Kb_sb[:, kp * 128:(kp + 1) * 128],
                             start=True, stop=True,
                             sem=(cm.sT if ti == nt - 1 else None))
                    kb.wait("vector", cm.sT)
                    for ti, kp in enumerate(tiles):
                        mi = tmask(kp)
                        reg = PS[:, ti * 128:(ti + 1) * 128]
                        if mi is not None:
                            kb.o("vector", "scalar_tensor_tensor", reg, reg,
                                 SM, mkT(kb, mi), ALU.mult, ALU.add)
                        else:
                            kb.o("vector", "tensor_scalar", reg, reg, SM,
                                 None, ALU.mult)
                    kb.o("vector", "reduce_max", m4[:, h:h + 1],
                         PS[:, 0:nt * 128], axis=AX, bar=True)
                    cm.pub("vector", cm.sV)
                kb.wait("tensor", cm.sV)
                for h in range(4):
                    kb.o("tensor", "transpose",
                         cm.pb[7][0:1, h * 128:(h + 1) * 128],
                         m4[:, h:h + 1], idf_sb[:],
                         sem=(cm.sT if h == 3 else None))
                kb.wait("vector", cm.sT)
                kb.o("vector", "tensor_scalar", mrow_sb[0:1, :],
                     cm.pb[7][0:1, 0:512], -1.0, None, ALU.mult, bar=True)
                cm.pub("vector", cm.sV)
                kb.wait("tensor", cm.sV)
                kb.o("tensor", "matmul", cm.pb[7][:], cm.one_row[:],
                     mrow_sb[0:1, :], start=True, stop=True, sem=cm.sT)
                kb.wait("scalar", cm.sT)
                kb.o("scalar", "copy", mbc[:], cm.pb[7][:], sem=cm.sA)
                kb.wait("vector", cm.sA)
                for ti, kp in enumerate(tiles):
                    if ti > 0:
                        kb.wait("tensor", cm.sA, aexp)
                    else:
                        kb.wait("tensor", cm.sV)
                    kb.o("tensor", "matmul", cm.pb[4][:],
                         Kb_sb[:, kp * 128:(kp + 1) * 128],
                         qI[:, P * 512:(P + 1) * 512],
                         start=True, stop=True, sem=cm.sT)
                    kb.wait("vector", cm.sT)
                    mi = tmask(kp)
                    if mi is not None:
                        kb.o("vector", "scalar_tensor_tensor", t_sb[:],
                             cm.pb[4][:], SM, mk(mi), ALU.mult, ALU.add)
                        kb.o("vector", "tensor_sub", t_sb[:], t_sb[:], mbc[:],
                             sem=cm.sV)
                    else:
                        kb.o("vector", "scalar_tensor_tensor", t_sb[:],
                             cm.pb[4][:], SM, mbc[:], ALU.mult, ALU.subtract,
                             sem=cm.sV)
                    kb.wait("scalar", cm.sV)
                    kb.o("scalar", "activation", pT_sb[:], t_sb[:], AF.Exp,
                         sem=cm.sA)
                    aexp = cm.sA.n
                    kb.wait("tensor", cm.sA, aexp)
                    kb.o("tensor", "matmul", cm.pb[5][:],
                         Vt_sb[:, kp * 128:(kp + 1) * 128], pT_sb[:],
                         start=(ti == 0), stop=(ti == nt - 1))
                    kb.o("tensor", "matmul", cm.pb[6][0:1, 0:512],
                         cm.ones_col_bf[:], pT_sb[:],
                         start=(ti == 0), stop=(ti == nt - 1),
                         sem=(cm.sT if ti == nt - 1 else None))
                kb.wait("vector", cm.sT)
                kb.o("vector", "reciprocal", srow[0:1, :],
                     cm.pb[6][0:1, 0:512], bar=True)
                cm.pub("vector", cm.sV)
                kb.wait("tensor", cm.sV)
                kb.o("tensor", "matmul", cm.pb[7][:], cm.one_row[:],
                     srow[0:1, :], start=True, stop=True, sem=cm.sT)
                kb.wait("scalar", cm.sT)
                kb.o("scalar", "copy", rbc[:], cm.pb[7][:], sem=cm.sA)
                kb.wait("vector", cm.sA)
                if P > 0 or qh > 0 or b > 0:
                    kb.wait("vector", cm.sS, cm.sS.n)
                kb.o("vector", "tensor_tensor", ctx_sb[:], cm.pb[5][:],
                     rbc[:], ALU.mult, sem=cm.sV)
                kb.wait("sync", cm.sV)
                gt0 = qh * 2048 + P * 128
                r = 4 * b + gt0 // 1024
                tl = gt0 % 1024
                kb.dma("sync",
                       a2b_in.ap()[r * 512:(r + 1) * 512, tl:tl + 128]
                       .rearrange("(hh d) q -> d hh q", d=128),
                       ctx_sb[:].rearrange("d (hh q) -> d hh q", q=128),
                       sem=cm.sS)
    kb.free(Kb_sb, Vt_sb, qI, ctx_sb, t_sb, pT_sb, m4, kb._mks,
            idf_sb, idb_sb, mrow_sb, mbc, rbc, srow)


def emit_l1attn_dense(kb, cm, io, a2b_out):
    ctx_all = [kb.sbt([128, T], BF16, "a_ctxall") for _ in range(KH)]
    bd_sb = kb.sbt([128, KH], F32, "a_bd")
    xres = [kb.sbt([128, 512], F32, "a_xres") for _ in range(4)]
    xo = [kb.sbt([128, 512], F32, "a_xo") for _ in range(2)]
    for eng in ("tensor", "vector", "scalar", "gpsimd"):
        kb.wait(eng, cm.sM)
    for k in range(KH):
        kb.dma("gpsimd", ctx_all[k][:], a2b_out.ap()[k * 128:(k + 1) * 128, :],
               sem=cm.sW)
    kb.dma("sync", bd_sb[:], io["bd"].ap(), sem=cm.sX)
    ldw2, ldx2 = cm.sW.n, cm.sX.n
    for eng in ("tensor", "vector"):
        kb.wait(eng, cm.sW, ldw2)
        kb.wait(eng, cm.sX, ldx2)
    sEvd = kb.sem("a_sEvd")
    sbase = cm.sS.n
    xbase = cm.sX.n
    for jj in range(3):
        of, sf = divmod(jj, 2)
        kb.dma("sync", xres[jj % 4][:],
               io["xin"].ap()[of][:, sf * 512:(sf + 1) * 512], sem=cm.sX)

    def evac_d(j, o, s0, sl, ps):
        jf = j + 3
        if jf < 64:
            of, sf = divmod(jf, 2)
            if j > 0:
                kb.wait("sync", sEvd, j)
            kb.dma("sync", xres[jf % 4][:],
                   io["xin"].ap()[of][:, sf * 512:(sf + 1) * 512], sem=cm.sX)
        kb.wait("vector", cm.sT)
        kb.wait("vector", cm.sX, xbase + 16 * (j + 1))
        if j >= 2:
            kb.wait("vector", cm.sS, sbase + 16 * (j - 1))
        kb.o("vector", "scalar_tensor_tensor", xo[j % 2][:, 0:sl], ps,
             bd_sb[:, o:o + 1], xres[j % 4][:, 0:sl], ALU.add, ALU.add,
             sem=sEvd)
        kb.wait("sync", sEvd, sEvd.n)
        kb.dma("sync", io["xout"].ap()[o][:, s0:s0 + sl], xo[j % 2][:, 0:sl],
               sem=cm.sS)

    emit_dense(kb, cm, sEvd, io["wdo"], KH, KH,
               lambda k, s0, sl: ctx_all[k][:, s0:s0 + sl], T, evac_d,
               dbanks=(4, 5))


def mkT(kb, mi):
    return kb._mks[:, (mi - 3) * 512:(mi - 3) * 512 + 128]


def _bs_masks():
    NEG = -1e30
    k = np.arange(128)[:, None]
    q = np.arange(128)[None, :]
    vis_diag = ((k < 64) & (q < 64) & (k <= q)) | ((k < 64) & (q >= 64)) | \
               ((k >= 64) & (q >= 64) & (k - 64 <= q - 64))
    vis_edge = (k >= 64) & (q < 64)
    vis_stride = np.broadcast_to(k >= 64, (128, 128))
    out = np.zeros((6, 128, 512), np.float32)
    for i, vis in enumerate([vis_diag, vis_edge, vis_stride]):
        m = np.where(vis, 0.0, NEG).astype(np.float32)
        out[i, :, 0:128] = m.T
        out[3 + i] = np.tile(m, (1, 4))
    return out


def host_prep_l1attn(inputs, x_fm):
    wqkv_t = tile_w_omajor(np.ascontiguousarray(inputs["L1_attn_w_qkv"].T))
    bqkv_t = col_layout(inputs["L1_attn_b_qkv"])
    wdo_t = tile_w_omajor(np.ascontiguousarray(inputs["L1_attn_w_dense"].T))
    bd_t = col_layout(inputs["L1_attn_b_dense"])
    inv_freq = 1.0 / (1e6 ** (np.arange(0, 128, 2, np.float32) / 128.0))
    masks_t = _bs_masks()
    idf_t = np.eye(128, dtype=np.float32)
    maps = []
    for c in range(NCORES):
        t0 = (c % 4) * T
        pos = np.arange(t0, t0 + T, dtype=np.float32)
        fr = np.outer(inv_freq, pos)
        cos_t = np.concatenate([np.cos(fr), np.cos(fr)], 0).astype(np.float32)
        sin_t = np.concatenate([-np.sin(fr), np.sin(fr)], 0).astype(np.float32)
        maps.append({
            "xin": np.ascontiguousarray(x_fm[c]), "wqkv": wqkv_t,
            "bqkv": bqkv_t, "wdo": wdo_t, "bd": bd_t, "cosd": cos_t,
            "sind": sin_t, "masks": masks_t, "idf": idf_t,
        })
    return maps


def build_l0attn(debug=False, upto=99):
    nc = bass.Bass("TRN2", target_bir_lowering=False,
                   detect_race_conditions=DETECT_RACES)
    xT = nc.dram_tensor("xT", [KH, 128, TH], F32, kind="ExternalInput")
    w1t = nc.dram_tensor("w1t", [145, KH * 128, 128], F32, kind="ExternalInput")
    convw = nc.dram_tensor("convw", [128, 80 * 4], F32, kind="ExternalInput")
    convb = nc.dram_tensor("convb", [128, 80], F32, kind="ExternalInput")
    dtb = nc.dram_tensor("dtb", [128, 1], F32, kind="ExternalInput")
    alog = nc.dram_tensor("alog", [128, 1], F32, kind="ExternalInput")
    DD = nc.dram_tensor("DD", [1, 128], F32, kind="ExternalInput")
    normw = nc.dram_tensor("normw", [128, 64], F32, kind="ExternalInput")
    opw = nc.dram_tensor("opw", [KH, 64 * 128, 128], F32, kind="ExternalInput")
    upper = nc.dram_tensor("upper", [128, 128], F32, kind="ExternalInput")
    maskui = nc.dram_tensor("maskui", [128, 128], F32, kind="ExternalInput")
    idf = nc.dram_tensor("idf", [128, 128], F32, kind="ExternalInput")
    msel = nc.dram_tensor("msel", [128, 16], F32, kind="ExternalInput")
    selbc = nc.dram_tensor("selbc", [128, 4], F32, kind="ExternalInput")
    xout = nc.dram_tensor("xout", [KH, 128, T], F32, kind="ExternalOutput")
    kb = KB(nc)
    cm = Common(kb, groups=(2, 2, 1, 1, 2))
    emit_l0attn(kb, cm, dict(xT=xT, w1t=w1t, convw=convw, convb=convb, dtb=dtb,
                             alog=alog, DD=DD, normw=normw, opw=opw,
                             upper=upper, maskui=maskui, idf=idf, msel=msel,
                             selbc=selbc, xout=xout), upto=upto)
    kb.wait("sync", cm.sS)
    kb.finalize()
    return nc


def emit_l0attn(kb, cm, io, upto=99):
    nc = kb.nc
    ag_in = nc.dram_tensor(kb.name("m_agin"), [128, 8192], F32, kind="Internal")
    ag_out = nc.dram_tensor(kb.name("m_agout"), [512, 8192], F32, kind="Internal")
    aga_in = nc.dram_tensor(kb.name("m_againa"), [128, 64], F32, kind="Internal")
    aga_out = nc.dram_tensor(kb.name("m_agouta"), [512, 64], F32, kind="Internal")
    y3loc = nc.dram_tensor(kb.name("m_y3"), [64, 128, 1024], BF16, kind="Internal")
    sgt = nc.dram_tensor(kb.name("m_sgt"), [64, 128, 1024], BF16, kind="Internal")
    yfd = nc.dram_tensor(kb.name("m_yf"), [64, 128, 1024], BF16, kind="Internal")

    convw_sb = kb.sbt([128, 320], F32, "m_cw")
    convb_sb = kb.sbt([128, 80], F32, "m_cb")
    dtb_sb = kb.sbt([128, 1], F32, "m_dtb")
    alog_sb = kb.sbt([128, 1], F32, "m_alog")
    dd_row = kb.sbt([1, 128], F32, "m_ddrow")
    normw_sb = kb.sbt([128, 64], F32, "m_nw")
    upper_sb = kb.sbt([128, 128], BF16, "m_upb")
    maskui_bf = kb.sbt([128, 128], BF16, "m_muib")
    idf_sb = kb.sbt([128, 128], F32, "m_idf")
    idb_sb = kb.sbt([128, 128], BF16, "m_idb")
    msel_sb = kb.sbt([128, 16], F32, "m_msel")
    selbc_sb = kb.sbt([128, 4], F32, "m_selbc")
    ddbc = kb.sbt([128, 128], F32, "m_ddbc")
    dt_tok = kb.sbt([128, T], F32, "m_dttok")
    a_tok = kb.sbt([128, T], F32, "m_atok")
    sd_tok = kb.sbt([128, T], F32, "m_sdtok")
    dtds2 = kb.sbt([128, T], F32, "m_dtds2")
    sdp_tok = kb.sbt([128, T], F32, "m_sdp")
    dtot_row = kb.sbt([1, 1024], F32, "m_dtot")
    dtotbc = kb.sbt([128, 1024], F32, "m_dtotbc")
    pc_row = kb.sbt([1, 1024], F32, "m_pc")
    aczero = kb.sbt([128, 64], F32, "m_acz")
    C_all = kb.sbt([128, 8 * 1024], BF16, "m_call")
    S_f = kb.sbt([128, 1024], F32, "m_sf")
    S_bf = kb.sbt([128, 1024], BF16, "m_sbf")
    wdense = [kb.sbt([128, KH * 128], BF16, "m_dw") for _ in range(2)]
    xb = [kb.sbt([128, TH], BF16, "m_xb") for _ in range(KH)]
    mb = kb.sbt([128, TH], BF16, "m_mb")
    rb = kb.sbt([128, TH], BF16, "m_rb")

    for k in range(KH):
        kb.dma("gpsimd", xb[k][:], io["xT"].ap()[k], sem=cm.sW)
    kb.dma("gpsimd", upper_sb[:], io["upper"].ap(), sem=cm.sW)
    kb.dma("gpsimd", idb_sb[:], io["idf"].ap(), sem=cm.sW)
    kb.dma("gpsimd", maskui_bf[:], io["maskui"].ap(), sem=cm.sW)
    for nm, t in (("convw", convw_sb), ("convb", convb_sb), ("dtb", dtb_sb),
                  ("alog", alog_sb), ("DD", dd_row), ("normw", normw_sb),
                  ("idf", idf_sb), ("msel", msel_sb),
                  ("selbc", selbc_sb)):
        kb.dma("sync", t[:], io[nm].ap(), sem=cm.sX)
    ldw, ldx = cm.sW.n, cm.sX.n
    for eng in ("tensor", "vector", "scalar"):
        kb.wait(eng, cm.sW, ldw)
        kb.wait(eng, cm.sX, ldx)
        kb.wait(eng, cm.sM)

    emit_ln(kb, cm, xb, TH, mb, rb, banks=(0, 2, 4, 5))
    kb.wait("vector", cm.sV)
    kb.free(rb, mb)

    if upto < 1.2:
        return
    kb.wait("tensor", cm.sV)
    kb.o("tensor", "matmul", cm.pb[4][:, 0:128], cm.one_row[:], dd_row[0:1, :],
         start=True, stop=True, sem=cm.sT)
    kb.wait("scalar", cm.sT)
    kb.o("scalar", "copy", ddbc[:], cm.pb[4][:, 0:128], sem=cm.sA)

    dtraw = kb.sbt([128, TH], F32, "m_dtraw")
    clr = kb.sbt([1, 1024], F32, "m_clr")
    prow = [kb.sbt([1, 1024], F32, "m_pr") for _ in range(2)]
    dt_fm = kb.sbt([128, T], F32, "m_dtfm")
    a_fm = kb.sbt([128, T], F32, "m_afm")
    cum_fm = kb.sbt([128, T], F32, "m_cumfm")
    zer = kb.sbt([128, 128], F32, "m_zer")
    scr2 = kb.sbt([128, 128], F32, "m_scr2")
    kb.o("gpsimd", "memset", zer[:], 0.0, sem=cm.sM)
    slabs = [(s, min(512, TH - s)) for s in range(0, TH, 512)]
    if upto < 1.3:
        return
    kb.wait("gpsimd", cm.sW)
    kb.dma("gpsimd", wdense[0][:].rearrange("p (k c) -> p k c", c=128),
           io["w1t"].ap()[0].rearrange("(k p) c -> p k c", p=128), sem=cm.sW)
    wv = cm.sW.n
    kb.wait("tensor", cm.sW, wv)
    kb.wait("tensor", cm.sA)
    for si, (s0, sl) in enumerate(slabs):
        ip = 4 + si % 2
        if si >= 2:
            kb.wait("tensor", cm.sA)
        for k in range(KH):
            kb.o("tensor", "matmul", cm.pb[ip][:, 0:sl],
                 wdense[0][:, k * 128:(k + 1) * 128], xb[k][:, s0:s0 + sl],
                 start=(k == 0), stop=(k == KH - 1),
                 sem=(cm.sT if k == KH - 1 else None))
        kb.wait("scalar", cm.sT)
        kb.o("scalar", "copy", dtraw[:, s0:s0 + sl], cm.pb[ip][:, 0:sl],
             sem=cm.sA)
    if upto < 1.35:
        return
    kb.wait("vector", cm.sA)
    kb.o("vector", "tensor_scalar", dtraw[:, 0:T], dtraw[:, HALO:TH],
         dtb_sb[:, 0:1], None, ALU.add, bar=True)
    kb.o("vector", "tensor_scalar", dt_fm[:], dtraw[:, 0:T], 0.0, None,
         ALU.max, bar=True)
    cm.pub("vector", cm.sV)
    kb.wait("scalar", cm.sV)
    kb.o("scalar", "activation", a_fm[:], dtraw[:, 0:T], AF.Abs, bar=True)
    kb.o("scalar", "activation", a_fm[:], a_fm[:], AF.Exp, scale=-1.0,
         bar=True)
    kb.o("scalar", "activation", a_fm[:], a_fm[:], AF.Ln,
         bias=cm.c1(), bar=True)
    kb.o("scalar", "activation", scr2[:, 0:1], alog_sb[:], AF.Exp, sem=cm.sA)
    kb.wait("vector", cm.sA)
    kb.o("vector", "tensor_add", dt_fm[:], dt_fm[:], a_fm[:], bar=True)
    kb.o("vector", "tensor_scalar", scr2[:, 1:2], scr2[:, 0:1], -1.0, None,
         ALU.mult, bar=True)
    kb.o("vector", "tensor_scalar", a_fm[:], dt_fm[:], scr2[:, 1:2], None,
         ALU.mult, bar=True)
    cm.pub("vector", cm.sV)
    for c in range(8):
        kb.o("vector", "tensor_tensor_scan", cum_fm[:, c * 128:(c + 1) * 128],
             a_fm[:, c * 128:(c + 1) * 128], zer[:], 0.0, ALU.add, ALU.add,
             sem=(cm.sV if c == 7 else None))
    kb.wait("tensor", cm.sV)
    tts = [(a_fm, a_tok), (dt_fm, dt_tok), (cum_fm, None)]
    cum_tok = kb.sbt([128, T], F32, "m_cumtok")
    tts[2] = (cum_fm, cum_tok)
    for src, dst in tts:
        for c in range(8):
            kb.wait("tensor", cm.sA)
            kb.o("tensor", "transpose", cm.pb[4][:, 0:128],
                 src[:, c * 128:(c + 1) * 128], idf_sb[:], sem=cm.sT)
            kb.wait("scalar", cm.sT)
            kb.o("scalar", "copy", dst[:, c * 128:(c + 1) * 128],
                 cm.pb[4][:, 0:128], sem=cm.sA)
    if upto < 1.4:
        return
    kb.wait("gpsimd", cm.sA)
    kb.dma("gpsimd", clr[:], cum_tok[127:128, :], sem=cm.sG)
    kb.wait("scalar", cm.sG)
    kb.o("scalar", "activation", dtot_row[0:1, :], clr[0:1, :], AF.Exp,
         sem=cm.sA)
    for si in range(2):
        kb.wait("tensor", cm.sA)
        kb.o("tensor", "matmul", cm.pb[6 + si][:], cm.one_row[:],
             dtot_row[0:1, si * 512:(si + 1) * 512], start=True, stop=True,
             sem=cm.sT)
        kb.wait("scalar", cm.sT)
        kb.o("scalar", "copy", dtotbc[:, si * 512:(si + 1) * 512],
             cm.pb[6 + si][:], sem=cm.sA)
    kb.wait("vector", cm.sG)
    kb.o("vector", "tensor_scalar", prow[0][0:1, :], clr[0:1, :], 1.0, None,
         ALU.mult, bar=True)
    src_i = 0
    for sh in (128, 256, 512):
        dst_i = 1 - src_i
        kb.o("vector", "tensor_scalar", prow[dst_i][0:1, 0:sh],
             prow[src_i][0:1, 0:sh], 1.0, None, ALU.mult)
        kb.o("vector", "tensor_add", prow[dst_i][0:1, sh:1024],
             prow[src_i][0:1, sh:1024], prow[src_i][0:1, 0:1024 - sh],
             bar=True)
        src_i = dst_i
    kb.o("vector", "tensor_sub", prow[1 - src_i][0:1, :], prow[src_i][0:1, :],
         clr[0:1, :], bar=True)
    cm.pub("vector", cm.sV)
    kb.wait("scalar", cm.sV)
    kb.o("scalar", "activation", pc_row[0:1, :], prow[1 - src_i][0:1, :],
         AF.Exp)
    if upto < 1.45:
        return
    kb.o("scalar", "activation", sd_tok[:], cum_tok[:], AF.Exp, sem=cm.sA)
    for c in range(8):
        kb.wait("vector", cm.sA)
        kb.o("vector", "tensor_scalar", scr2[:], cum_fm[:, c * 128:(c + 1) * 128],
             cum_fm[:, c * 128 + 127:c * 128 + 128], None, ALU.subtract,
             sem=cm.sV)
        kb.wait("scalar", cm.sV)
        kb.o("scalar", "activation", scr2[:], scr2[:], AF.Exp, scale=-1.0,
             sem=cm.sA)
        kb.wait("tensor", cm.sA)
        if c > 0:
            kb.wait("tensor", cm.sV)
        kb.o("tensor", "transpose", cm.pb[4][:, 128:256], scr2[:], idf_sb[:],
             sem=cm.sT)
        kb.wait("vector", cm.sT)
        kb.o("vector", "tensor_copy", dtds2[:, c * 128:(c + 1) * 128],
             cm.pb[4][:, 128:256], sem=cm.sV)
    kb.wait("vector", cm.sA)
    kb.o("vector", "tensor_mul", dtds2[:], dtds2[:], dt_tok[:], sem=cm.sV)
    if upto < 1.47:
        return
    kb.o("vector", "memset", aczero[:], 0.0, bar=True)
    for c in range(8):
        kb.o("vector", "tensor_tensor", aczero[:, 0:1], aczero[:, 0:1],
             cum_fm[:, c * 128 + 127:c * 128 + 128], ALU.add, bar=True)
    cm.pub("vector", cm.sV)
    kb.wait("gpsimd", cm.sV)
    kb.dma("gpsimd", aga_in.ap(), aczero[:], sem=cm.sG)
    kb.wait("vector", cm.sA)
    kb.free(cum_tok, scr2, zer, cum_fm, a_fm, dt_fm, *prow, clr, dtraw)
    if upto < 1.5:
        return

    kb.o("vector", "memset", S_f[:], 0.0)
    kb.o("vector", "memset", S_bf[:], 0.0, sem=cm.sV)
    sgf = [kb.sbt([128, TH], BF16, "m_sgf") for _ in range(8)]
    convin = kb.sbt([128, TH], F32, "m_cvin")
    acc = kb.sbt([128, T], F32, "m_acc")
    xc = [kb.sbt([128, T], BF16, "m_xc") for _ in range(8)]
    B_fm = kb.sbt([128, T], BF16, "m_bfm")
    xtok = kb.sbt([128, 1024], BF16, "m_xtok")
    btok = kb.sbt([128, 128], BF16, "m_btok")
    gtm = kb.sbt([128, 128], BF16, "m_gtm")
    rhsl = kb.sbt([128, 512], BF16, "m_rhsl")
    expd = kb.sbt([128, 512], BF16, "m_expd")
    gmlt = kb.sbt([128, 512], BF16, "m_gmlt")
    y1 = kb.sbt([128, 1024], F32, "m_y1")
    xw = kb.sbt([128, 1024], BF16, "m_xw")
    sgt_sb = kb.sbt([128, 1024], BF16, "m_sgts")
    sgm = kb.sbt([128, T], BF16, "m_sgm")

    sEvm = kb.sem("m_sEvm")
    state = {"cvk": None}

    for g in range(8 if upto >= 1.55 else 1):
        o_order = [1 + g * 18 + i for i in range(18)]

        def evac_m(j, o, s0, sl, ps, g=g):
            i = (o - 1) % 18
            si = j % 3
            kb.wait("scalar", sEvm)
            if upto < 1.65:
                kb.wait("scalar", cm.sT)
                kb.o("scalar", "copy", convin[:, s0:s0 + sl], ps, sem=sEvm)
                return
            if i < 8:
                kb.wait("scalar", cm.sT)
                kb.o("scalar", "activation", sgm[:, 0:sl], ps, AF.Sigmoid,
                     sem=cm.sA)
                kb.wait("vector", cm.sA)
                kb.o("vector", "tensor_tensor", sgf[i][:, s0:s0 + sl], ps,
                     sgm[:, 0:sl], ALU.mult, sem=sEvm)
            else:
                kb.wait("scalar", cm.sT)
                if si == 0 and i > 8:
                    kb.wait("scalar", cm.sV)
                kb.o("scalar", "copy", convin[:, s0:s0 + sl], ps,
                     sem=(sEvm if si < 2 else None))
                if si < 2:
                    pass
                else:
                    cidx = g * 10 + (i - 8)
                    kb.o("scalar", "copy", cm.cst[0:1, 3:4], cm.cst[0:1, 3:4],
                         sem=cm.sA)
                    kb.wait("vector", cm.sA)
                    kb.o("vector", "tensor_scalar", acc[:],
                         convin[:, 1:T + 1], convw_sb[:, cidx * 4:cidx * 4 + 1],
                         None, ALU.mult)
                    for kk in range(1, 4):
                        kb.o("vector", "scalar_tensor_tensor", acc[:],
                             convin[:, 1 + kk:T + 1 + kk],
                             convw_sb[:, cidx * 4 + kk:cidx * 4 + kk + 1],
                             acc[:], ALU.mult, ALU.add,
                             bar=(kk == 3))
                    cm.pub("vector", cm.sV)
                    kb.wait("scalar", cm.sV)
                    kb.o("scalar", "activation", sgm[:, 0:T], acc[:],
                         AF.Sigmoid, bias=convb_sb[:, cidx:cidx + 1],
                         bar=True)
                    cm.pub("scalar", cm.sA)
                    kb.wait("vector", cm.sA)
                    dst = (xc[i - 8] if i < 16 else
                           (B_fm if i == 16 else C_all))
                    dap = dst[:, 0:T] if i < 17 else dst[:, g * 1024:(g + 1) * 1024]
                    kb.o("vector", "scalar_tensor_tensor", dap, acc[:],
                         convb_sb[:, cidx:cidx + 1], sgm[:, 0:T], ALU.add,
                         ALU.mult, sem=sEvm)

        kb.wait("tensor", cm.sV)
        kb.wait("gpsimd", cm.sA)
        emit_dense(kb, cm, sEvm, io["w1t"], KH, 145,
                   lambda k, s0, sl: xb[k][:, s0:s0 + sl],
                   (1024 if upto < 1.63 else TH), evac_m,
                   o_order=o_order, dbanks=(4, 5), wb=wdense)
        for eng in ("tensor", "vector", "scalar", "gpsimd"):
            kb.wait(eng, sEvm)
            kb.wait(eng, cm.sV)
        if upto < 1.8:
            continue
        for c in range(8):
            kb.wait("scalar", cm.sG, cm.sG.n)
            for k in range(8):
                kb.wait("tensor", cm.sA)
                kb.o("tensor", "transpose",
                     cm.pb[4][:, 64 * (k % 2):64 * (k % 2) + 64].bitcast(BF16),
                     sgf[k][:, HALO + c * 128:HALO + (c + 1) * 128], idb_sb[:],
                     sem=cm.sT)
                kb.wait("scalar", cm.sT)
                kb.o("scalar", "copy", sgt_sb[:, k * 128:(k + 1) * 128],
                     cm.pb[4][:, 64 * (k % 2):64 * (k % 2) + 64].bitcast(BF16),
                     sem=cm.sA)
            kb.wait("gpsimd", cm.sA)
            kb.dma("gpsimd", sgt.ap()[g * 8 + c], sgt_sb[:], sem=cm.sG)

        if upto < 1.9:
            continue
        for c in range(8):
            kb.wait("tensor", cm.sV)
            for k in range(8):
                kb.wait("tensor", cm.sA)
                kb.o("tensor", "transpose",
                     cm.pb[4][:, 64 * (k % 2):64 * (k % 2) + 64].bitcast(BF16),
                     xc[k][:, c * 128:(c + 1) * 128], idb_sb[:], sem=cm.sT)
                kb.wait("scalar", cm.sT)
                kb.o("scalar", "copy", xtok[:, k * 128:(k + 1) * 128],
                     cm.pb[4][:, 64 * (k % 2):64 * (k % 2) + 64].bitcast(BF16),
                     sem=cm.sA)
            kb.wait("tensor", cm.sA)
            kb.o("tensor", "transpose", cm.pb[4][:, 0:64].bitcast(BF16),
                 B_fm[:, c * 128:(c + 1) * 128], idb_sb[:], sem=cm.sT)
            kb.wait("scalar", cm.sT)
            kb.o("scalar", "copy", btok[:], cm.pb[4][:, 0:64].bitcast(BF16),
                 sem=cm.sA)
            if upto < 1.92:
                continue
            kb.wait("tensor", cm.sA)
            kb.o("tensor", "matmul", cm.pb[5][:, 0:128],
                 B_fm[:, c * 128:(c + 1) * 128],
                 C_all[:, g * 1024 + c * 128:g * 1024 + (c + 1) * 128],
                 start=True, stop=True, sem=cm.sT)
            kb.wait("vector", cm.sT)
            kb.o("vector", "tensor_tensor", gtm[:], cm.pb[5][:, 0:128],
                 maskui_bf[:], ALU.mult, bar=True)
            cm.pub("vector", cm.sV)
            for h4 in range(4):
                for hh in range(4):
                    hg = g * 16 + h4 * 4 + hh
                    kb.o("vector", "tensor_scalar",
                         rhsl[:, hh * 128:(hh + 1) * 128], maskui_bf[:],
                         a_tok[:, c * 128 + hg:c * 128 + hg + 1], None,
                         ALU.mult, sem=(cm.sV if hh == 3 else None))
                kb.wait("tensor", cm.sV)
                kb.wait("tensor", cm.sA)
                for hh in range(4):
                    kb.o("tensor", "matmul",
                         cm.pb[5][:, hh * 128:(hh + 1) * 128], upper_sb[:],
                         rhsl[:, hh * 128:(hh + 1) * 128],
                         start=(hh == 0), stop=(hh == 3),
                         sem=(cm.sT if hh == 3 else None))
                kb.wait("scalar", cm.sT)
                kb.o("scalar", "activation", expd[:], cm.pb[5][:, 0:512],
                     AF.Exp, sem=cm.sA)
                kb.wait("vector", cm.sA)
                for hh in range(4):
                    hg = g * 16 + h4 * 4 + hh
                    kb.o("vector", "scalar_tensor_tensor",
                         gmlt[:, hh * 128:(hh + 1) * 128],
                         expd[:, hh * 128:(hh + 1) * 128],
                         dt_tok[:, c * 128 + hg:c * 128 + hg + 1], gtm[:],
                         ALU.mult, ALU.mult,
                         sem=(cm.sV if hh == 3 else None))
                if upto < 1.94:
                    continue
                kb.wait("tensor", cm.sV)
                for hh in range(4):
                    h = h4 * 4 + hh
                    kb.o("tensor", "matmul",
                         cm.tg[0][:, h * 64:(h + 1) * 64],
                         gmlt[:, hh * 128:(hh + 1) * 128],
                         xtok[:, h * 64:(h + 1) * 64],
                         start=(h == 0 or h == 8), stop=(h == 7 or h == 15),
                         sem=(cm.sT if h in (7, 15) else None))
                    kb.o("tensor", "matmul",
                         cm.tg[1][:, h * 64:(h + 1) * 64],
                         C_all[:, g * 1024 + c * 128:g * 1024 + (c + 1) * 128],
                         S_bf[:, h * 64:(h + 1) * 64],
                         start=(h == 0 or h == 8), stop=(h == 7 or h == 15),
                         sem=(cm.sT if h in (7, 15) else None))
            if upto < 1.96:
                continue
            kb.wait("vector", cm.sT)
            kb.wait("vector", cm.sG, cm.sG.n)
            for h in range(16):
                hg = g * 16 + h
                kb.o("vector", "tensor_scalar", y1[:, h * 64:(h + 1) * 64],
                     cm.tg[1][:, h * 64:(h + 1) * 64],
                     sd_tok[:, c * 128 + hg:c * 128 + hg + 1], None, ALU.mult)
            kb.o("vector", "tensor_tensor", y1[:], y1[:], cm.tg[0][:, 0:1024],
                 ALU.add, bar=True)
            for h in range(16):
                hg = g * 16 + h
                kb.o("vector", "scalar_tensor_tensor", y1[:, h * 64:(h + 1) * 64],
                     xtok[:, h * 64:(h + 1) * 64], ddbc[:, hg:hg + 1],
                     y1[:, h * 64:(h + 1) * 64], ALU.mult, ALU.add,
                     bar=(h == 15))
            cm.pub("vector", cm.sV)
            kb.wait("gpsimd", cm.sV)
            kb.dma("gpsimd", y3loc.ap()[g * 8 + c], y1[:], sem=cm.sG)
            if upto < 1.98:
                continue
            for h in range(16):
                hg = g * 16 + h
                kb.o("vector", "tensor_scalar", xw[:, h * 64:(h + 1) * 64],
                     xtok[:, h * 64:(h + 1) * 64],
                     dtds2[:, c * 128 + hg:c * 128 + hg + 1], None, ALU.mult,
                     bar=(h == 15))
            cm.pub("vector", cm.sV)
            kb.wait("tensor", cm.sV)
            for h in range(16):
                kb.o("tensor", "matmul",
                     cm.tg[4][:, h * 64:(h + 1) * 64],
                     btok[:], xw[:, h * 64:(h + 1) * 64],
                     start=(h == 0 or h == 8), stop=(h == 7 or h == 15),
                     sem=(cm.sT if h in (7, 15) else None))
            if upto < 1.99:
                continue
            kb.wait("vector", cm.sT)
            for h in range(16):
                hg = g * 16 + h
                kb.o("vector", "tensor_scalar", S_f[:, h * 64:(h + 1) * 64],
                     S_f[:, h * 64:(h + 1) * 64],
                     dtotbc[:, c * 128 + hg:c * 128 + hg + 1], None, ALU.mult,
                     bar=(h == 15))
            kb.o("vector", "tensor_tensor", S_f[:], S_f[:],
                 cm.tg[4][:, 0:1024], ALU.add, bar=True)
            kb.o("vector", "tensor_copy", S_bf[:], S_f[:], bar=True)
            cm.pub("vector", cm.sV)
        if upto < 1.995:
            continue
        kb.wait("gpsimd", cm.sV)
        kb.dma("gpsimd", ag_in.ap()[:, g * 1024:(g + 1) * 1024], S_f[:],
               sem=cm.sG)
        if upto < 1.997:
            continue
        kb.wait("vector", cm.sG)
        kb.o("vector", "memset", S_f[:], 0.0)
        kb.o("vector", "memset", S_bf[:], 0.0, sem=cm.sV)
    kb.free(*sgf, convin, acc, *xc, B_fm, xtok, btok, gtm, rhsl, expd, gmlt,
            xw, sgt_sb, sgm, y1)
    kb.free(*xb, *wdense)
    if upto < 3:
        return

    kb.wait("gpsimd", cm.sG)
    kb.wait("gpsimd", cm.sV)
    kb.cc("AllGather", ALU.bypass, [[0, 1, 2, 3], [4, 5, 6, 7]],
          [ag_in.ap().opt()], [ag_out.ap().opt()], cm.sCC)
    kb.cc("AllGather", ALU.bypass, [[0, 1, 2, 3], [4, 5, 6, 7]],
          [aga_in.ap().opt()], [aga_out.ap().opt()], cm.sCC)

    y1 = kb.sbt([128, 1024], F32, "m_y1b")
    acr = kb.sbt([128, 4], F32, "m_acr")
    wl = kb.sbt([128, 4], F32, "m_wl")
    wsr = kb.sbt([1, 512], F32, "m_wsr")
    wsrb = kb.sbt([128, 512], F32, "m_wsrb")
    srg = kb.sbt([128, 1024], F32, "m_srg")
    S_in = kb.sbt([128, 1024], F32, "m_sin")
    ylb = kb.sbt([128, 1024], F32, "m_ylb")
    sgtb = kb.sbt([128, 1024], F32, "m_sgtb")
    ssq = kb.sbt([128, 2], F32, "m_ssq")
    sqscr = kb.sbt([128, 1024], BF16, "m_sqscr")
    ygn = kb.sbt([128, 1024], BF16, "m_ygn")
    yf_sb = kb.sbt([128, 1024], BF16, "m_yfsb")
    for eng in ("tensor", "vector", "scalar", "gpsimd", "sync"):
        kb.wait(eng, cm.sCC)
    for r in range(4):
        kb.dma("sync", acr[:, r:r + 1], aga_out.ap()[r * 128:(r + 1) * 128,
                                                      0:1], sem=cm.sX,
               allow_slow_non_contiguous=True)
    kb.wait("vector", cm.sX, cm.sX.n)
    kb.o("vector", "memset", wl[:], 0.0, bar=True)
    for r in range(4):
        for m in range(4):
            kb.o("vector", "scalar_tensor_tensor", wl[:, r:r + 1],
                 acr[:, m:m + 1], msel_sb[:, 4 * r + m:4 * r + m + 1],
                 wl[:, r:r + 1], ALU.mult, ALU.add, bar=True)
    cm.pub("vector", cm.sV)
    kb.wait("scalar", cm.sV)
    kb.o("scalar", "activation", wl[:], wl[:], AF.Exp, bar=True)
    cm.pub("scalar", cm.sA)
    kb.wait("vector", cm.sA)
    kb.o("vector", "tensor_mul", wl[:], wl[:], selbc_sb[:], bar=True)
    cm.pub("vector", cm.sV)
    kb.wait("tensor", cm.sV)
    for r in range(4):
        kb.o("tensor", "transpose", cm.tg[4][0:1, r * 128:r * 128 + 128],
             wl[:, r:r + 1], idf_sb[:], sem=(cm.sT if r == 3 else None))
    kb.wait("scalar", cm.sT)
    kb.o("scalar", "copy", wsr[0:1, :], cm.tg[4][0:1, 0:512], sem=cm.sA)
    cm.pub("scalar", cm.sA)
    kb.wait("tensor", cm.sA)
    kb.o("tensor", "matmul", cm.pb[5][:], cm.one_row[:], wsr[0:1, :],
         start=True, stop=True, sem=cm.sT)
    kb.wait("scalar", cm.sT)
    kb.o("scalar", "copy", wsrb[:], cm.pb[5][:], sem=cm.sA)
    for si in range(2):
        kb.wait("tensor", cm.sA)
        kb.wait("tensor", cm.sV)
        kb.o("tensor", "matmul", cm.pb[4 + si][:], cm.one_row[:],
             pc_row[0:1, si * 512:(si + 1) * 512], start=True, stop=True,
             sem=cm.sT)
        kb.wait("vector", cm.sT)
        kb.o("vector", "tensor_tensor", sdp_tok[:, si * 512:(si + 1) * 512],
             sd_tok[:, si * 512:(si + 1) * 512],
             cm.pb[4 + si][:], ALU.mult, sem=cm.sV, bar=True)
    cm.pub("vector", cm.sV)

    sldy = {}
    for g in range(8):
        for r in range(4):
            kb.wait("sync", cm.sV, cm.sV.n)
            kb.dma("sync", srg[:], ag_out.ap()[r * 128:(r + 1) * 128,
                                               g * 1024:(g + 1) * 1024],
                   sem=cm.sX)
            kb.wait("vector", cm.sX, cm.sX.n)
            for h in range(16):
                hg = g * 16 + h
                col = wsrb[:, r * 128 + hg:r * 128 + hg + 1]
                if r == 0:
                    kb.o("vector", "tensor_scalar",
                         S_in[:, h * 64:(h + 1) * 64],
                         srg[:, h * 64:(h + 1) * 64], col, None, ALU.mult,
                         bar=(h == 15))
                else:
                    kb.o("vector", "scalar_tensor_tensor",
                         S_in[:, h * 64:(h + 1) * 64],
                         srg[:, h * 64:(h + 1) * 64], col,
                         S_in[:, h * 64:(h + 1) * 64], ALU.mult, ALU.add,
                         bar=(h == 15))
            cm.pub("vector", cm.sV)
        kb.o("vector", "tensor_copy", S_bf[:], S_in[:], bar=True)
        cm.pub("vector", cm.sV)
        for c in range(8):
            gc = g * 8 + c
            kb.wait("gpsimd", cm.sV, cm.sV.n)
            kb.dma("gpsimd", ylb[:], y3loc.ap()[gc], sem=cm.sW)
            kb.dma("gpsimd", sgtb[:], sgt.ap()[gc], sem=cm.sW)
            wv3 = cm.sW.n
            kb.wait("tensor", cm.sV)
            for h in range(16):
                kb.o("tensor", "matmul", cm.tg[0][:, h * 64:(h + 1) * 64],
                     C_all[:, g * 1024 + c * 128:g * 1024 + (c + 1) * 128],
                     S_bf[:, h * 64:(h + 1) * 64],
                     start=(h == 0 or h == 8), stop=(h == 7 or h == 15),
                     sem=(cm.sT if h in (7, 15) else None))
            kb.wait("vector", cm.sT)
            kb.wait("vector", cm.sW, wv3)
            for h in range(16):
                hg = g * 16 + h
                kb.o("vector", "tensor_scalar", y1[:, h * 64:(h + 1) * 64],
                     cm.tg[0][:, h * 64:(h + 1) * 64],
                     sdp_tok[:, c * 128 + hg:c * 128 + hg + 1], None,
                     ALU.mult)
            kb.o("vector", "tensor_add", y1[:], y1[:], ylb[:], bar=True)
            kb.o("vector", "tensor_mul", y1[:], y1[:], sgtb[:], bar=True)
            cm.pub("vector", cm.sV)
            kb.wait("scalar", cm.sV)
            kb.o("scalar", "activation", sqscr[:], y1[:], AF.Square,
                 accum_out=ssq[:, 0:1], bar=True)
            cm.pub("scalar", cm.sA)
            kb.wait("vector", cm.sA)
            kb.o("vector", "tensor_scalar", ssq[:, 0:1], ssq[:, 0:1],
                 1.0 / 1024.0, 1e-5, ALU.mult, ALU.add, bar=True)
            cm.pub("vector", cm.sV)
            kb.wait("scalar", cm.sV)
            kb.o("scalar", "activation", ssq[:, 1:2], ssq[:, 0:1], AF.Sqrt,
                 bar=True)
            cm.pub("scalar", cm.sA)
            kb.wait("vector", cm.sA)
            kb.o("vector", "reciprocal", ssq[:, 0:1], ssq[:, 1:2], bar=True)
            kb.o("vector", "tensor_scalar", ygn[:], y1[:], ssq[:, 0:1], None,
                 ALU.mult, bar=True)
            cm.pub("vector", cm.sV)
            kb.wait("tensor", cm.sV)
            kb.wait("vector", cm.sG, cm.sG.n)
            for k in range(8):
                kb.wait("tensor", cm.sV)
                kb.o("tensor", "transpose",
                     cm.pb[4][:, 64 * (k % 2):64 * (k % 2) + 64].bitcast(BF16),
                     ygn[:, k * 128:(k + 1) * 128], idb_sb[:], sem=cm.sT)
                kb.wait("vector", cm.sT)
                kb.o("vector", "tensor_scalar", yf_sb[:, k * 128:(k + 1) * 128],
                     cm.pb[4][:, 64 * (k % 2):64 * (k % 2) + 64].bitcast(BF16),
                     normw_sb[:, g * 8 + k:g * 8 + k + 1], None, ALU.mult,
                     sem=cm.sV)
            kb.wait("gpsimd", cm.sV)
            kb.dma("gpsimd", yfd.ap().rearrange("o p t -> p o t")
                   [:, g * 8:(g + 1) * 8, c * 128:(c + 1) * 128],
                   yf_sb[:].rearrange("p (k t) -> p k t", t=128), sem=cm.sG)
    kb.free(acr, wl, wsr, wsrb, srg, S_in, ylb, sgtb, ssq, sqscr, ygn,
            yf_sb, y1)
    kb.free(convw_sb, convb_sb, dtb_sb, alog_sb, dd_row, normw_sb, upper_sb,
            maskui_bf, idf_sb, idb_sb, msel_sb, selbc_sb, ddbc,
            dt_tok, a_tok, sd_tok, dtds2, sdp_tok, dtot_row, dtotbc, pc_row,
            aczero, C_all, S_f, S_bf)
    if upto < 4:
        return

    ySB = [kb.sbt([128, 1024], BF16, "m_ysb") for _ in range(64)]
    xres = [kb.sbt([128, 512], F32, "m_xres") for _ in range(4)]
    xo = [kb.sbt([128, 512], F32, "m_xo") for _ in range(2)]
    kb.wait("gpsimd", cm.sG)
    for kk in range(64):
        kb.dma("gpsimd", ySB[kk][:], yfd.ap()[kk], sem=cm.sW)
    ldw3 = cm.sW.n
    kb.wait("tensor", cm.sW, ldw3)
    sEvo = kb.sem("m_sEvo")
    xb2 = cm.sX.n
    for jj in range(3):
        of, sf = divmod(jj, 2)
        kb.dma("sync", xres[jj % 4][:],
               io["xT"].ap()[of][:, HALO + sf * 512:HALO + (sf + 1) * 512],
               sem=cm.sX)

    def evac_o(j, o, s0, sl, ps):
        jf = j + 3
        if jf < 64:
            of, sf = divmod(jf, 2)
            if j > 0:
                kb.wait("sync", sEvo, j)
            kb.dma("sync", xres[jf % 4][:],
                   io["xT"].ap()[of][:, HALO + sf * 512:HALO + (sf + 1) * 512],
                   sem=cm.sX)
        kb.wait("vector", cm.sT)
        kb.wait("vector", cm.sX, xb2 + 16 * (j + 1))
        if j >= 2:
            kb.wait("vector", cm.sS, 16 * (j - 1))
        kb.o("vector", "tensor_tensor", xo[j % 2][:, 0:sl], ps,
             xres[j % 4][:, 0:sl], ALU.add, sem=sEvo)
        kb.wait("sync", sEvo, sEvo.n)
        kb.dma("sync", io["xout"].ap()[o][:, s0:s0 + sl], xo[j % 2][:, 0:sl],
               sem=cm.sS)

    emit_dense(kb, cm, sEvo, io["opw"], 64, KH,
               lambda k, s0, sl: ySB[k][:, s0:s0 + sl], T, evac_o,
               dbanks=(4, 5))


def host_prep_l0attn(inputs, x):
    W1 = np.ascontiguousarray(inputs["L0_attn_in_proj_w"].T)
    w1tiles = tile_w_omajor(W1)
    order = [144]
    for g in range(8):
        order += [g * 8 + j for j in range(8)]
        order += [64 + g * 8 + j for j in range(8)]
        order += [128 + g, 136 + g]
    w1t = np.ascontiguousarray(w1tiles[order])
    cw = inputs["L0_attn_conv_w"][:, 0, :]
    cb = inputs["L0_attn_conv_b"]
    chan = []
    for g in range(8):
        chan += list(range(g * 1024, (g + 1) * 1024))
        chan += list(range(8192 + g * 128, 8192 + (g + 1) * 128))
        chan += list(range(9216 + g * 128, 9216 + (g + 1) * 128))
    chan = np.array(chan)
    cwr = cw[chan].reshape(80, 128, 4)
    convw_t = np.ascontiguousarray(cwr.transpose(1, 0, 2).reshape(128, 320))
    convb_t = np.ascontiguousarray(cb[chan].reshape(80, 128).T)
    dtb_t = inputs["L0_attn_dt_bias"].reshape(128, 1).astype(np.float32)
    alog_t = inputs["L0_attn_A_log"].reshape(128, 1).astype(np.float32)
    dd_t = inputs["L0_attn_D"].reshape(1, 128).astype(np.float32)
    normw_t = col_layout(inputs["L0_attn_norm_w"])
    opw_t = tile_w_omajor(np.ascontiguousarray(inputs["L0_attn_out_proj_w"].T))
    m_idx = np.arange(128)
    upper_t = (m_idx[:, None] > m_idx[None, :]).astype(np.float32)
    maskui_t = (m_idx[None, :] >= m_idx[:, None]).astype(np.float32)
    idf_t = np.eye(128, dtype=np.float32)
    maps = []
    for c in range(NCORES):
        b, krank = divmod(c, 4)
        t0 = krank * T
        sl = x[b, max(0, t0 - HALO):t0 + T, :]
        if t0 == 0:
            sl = np.concatenate([np.zeros((HALO, H), np.float32), sl], 0)
        xT_t = np.ascontiguousarray(sl.T).reshape(KH, 128, TH)
        mselm = np.zeros((4, 4), np.float32)
        selv = np.zeros(4, np.float32)
        for r in range(4):
            if r < krank:
                selv[r] = 1.0
                for m in range(r + 1, krank):
                    mselm[r, m] = 1.0
        msel_t = np.broadcast_to(mselm.reshape(1, 16), (128, 16)).astype(np.float32)
        selbc_t = np.broadcast_to(selv.reshape(1, 4), (128, 4)).astype(np.float32)
        maps.append({
            "xT": xT_t, "w1t": w1t, "convw": convw_t, "convb": convb_t,
            "dtb": dtb_t, "alog": alog_t, "DD": dd_t, "normw": normw_t,
            "opw": opw_t, "upper": upper_t, "maskui": maskui_t, "idf": idf_t,
            "msel": np.ascontiguousarray(msel_t),
            "selbc": np.ascontiguousarray(selbc_t),
        })
    return maps
